# version 14
# baseline (speedup 1.0000x reference)
import numpy as np
from contextlib import ExitStack

import concourse.bass as bass
import concourse.mybir as mybir
from concourse.bass_utils import run_bass_kernel_spmd

F32 = mybir.dt.float32
BF16 = mybir.dt.bfloat16
AF = mybir.ActivationFunctionType
ALU = mybir.AluOpType
AX = mybir.AxisListType

ENGINES = ("tensor", "vector", "scalar", "gpsimd", "sync")

C = 1024
H = 16
DH = 64
L = 128
FH = 8
FDH = 64
WIN = 512
INNER = 512
HID = 2730
HIDP = 2816
GN_EPS = 1e-5 * 64.0
LN_EPS = 1e-5


class Sem:
    def __init__(self, handle, name):
        self.h = handle
        self.name = name
        self.n = 0


class Prog:

    def __init__(self):
        self.ops = {e: [] for e in ENGINES}

    def emit(self, engine, fn, waits=(), incs=()):
        waits_f = [(s.h, (s.n if v is None else v)) for (s, v) in waits]
        incs_f = [(s.h, k) for (s, k) in incs]
        for s, k in incs:
            s.n += k
        self.ops[engine].append((waits_f, fn, incs_f))

    def replay(self, engine_name, eng):
        waited = {}
        for waits_f, fn, incs_f in self.ops[engine_name]:
            for h, v in waits_f:
                if v > 0 and waited.get(id(h), -1) < v:
                    eng.wait_ge(h, v)
                    waited[id(h)] = v
            inst = fn(eng)
            for h, k in incs_f:
                assert inst is not None, "op with incs must return an instruction"
                inst.then_inc(h, k)


class Arena:

    def __init__(self, t, cols, dtype):
        self.t = t
        self.cols = cols
        self.dtype = dtype
        self.off = 0
        self.align = 8 if dtype == BF16 else 4
        self.peak = 0

    def alloc(self, w):
        o = self.off
        w_al = -(-w // self.align) * self.align
        assert o + w_al <= self.cols, f"arena overflow {o}+{w} > {self.cols}"
        self.off = o + w_al
        self.peak = max(self.peak, self.off)
        return self.t.ap()[:, o:o + w]

    def mark(self):
        return self.off

    def reset(self, m):
        self.off = m


def ceil_div(a, b):
    return -(-a // b)


class Builder:
    def __init__(self, nc, T, dbg=()):
        self.nc = nc
        self.T = T
        self.NTT = T // 128
        self.NCHK = T // L
        self.NW = T // WIN
        self.dbg = set(dbg)
        self.p = Prog()
        self.es = ExitStack()
        self.sems = {}
        self.dram = {}
        self.dbg_outs = {}

    def set_guard(self, key, *sems):
        if not hasattr(self, "guards"):
            self.guards = {}
        self.guards[key] = [(s, s.n) for s in sems]

    def guard(self, key):
        if not hasattr(self, "guards"):
            self.guards = {}
        return list(self.guards.get(key, []))

    def sem(self, name):
        if name not in self.sems:
            self.sems[name] = Sem(self.es.enter_context(self.nc.semaphore(name)), name)
        return self.sems[name]

    def setup(self):
        nc, T = self.nc, self.T
        def din(name, shape, dt=F32):
            self.dram[name] = nc.dram_tensor(name, list(shape), dt, kind="ExternalInput").ap()
        din("x", (T, C))
        for nm in ("tmx", "tmw", "tmk", "tmv", "tmr", "tmg", "tdecay", "u_", "lnw_", "lnb_"):
            din(nm, (128, 8))
        din("maa_w1", (C, 160))
        din("maa_w2", (32, 5 * C))
        din("td_w1", (C, 64))
        din("td_w2", (64, C))
        din("w_r", (C, C)); din("w_k", (C, C)); din("w_v", (C, C)); din("w_g", (C, C)); din("w_o", (C, C))
        din("lnw_rep", (128, C)); din("lnb_rep", (128, C))
        din("wqkv", (2, C, 3 * INNER))
        din("qscale_rep", (2, 128, FDH)); din("kscale_rep", (2, 128, FDH))
        din("wo", (2, INNER, C))
        din("lng", (2, 128, 8)); din("lnb2", (2, 128, 8)); din("b2", (2, 128, 8))
        din("w1p", (2, C, 2 * HIDP))
        din("b1p", (2, 128, 44))
        din("w2p", (2, HIDP, C))
        din("ident", (128, 128))
        din("maskA", (128, 512))
        din("mask_low", (128, 128))
        din("mask_up", (128, 128))
        din("ct_q", (T, 64)); din("st_q", (T, 64)); din("ct_k", (T, 64)); din("st_k", (T, 64))
        def dscr(name, shape, dt):
            self.dram[name] = nc.dram_tensor(name, list(shape), dt, kind="Internal").ap()
        dscr("rq_d", (2 * C, T), BF16)
        dscr("kk_d", (C, T), BF16)
        dscr("v_d", (T, C), BF16)
        dscr("g2x_d", (T, C), BF16)
        dscr("s_d", (T, C), F32)
        dscr("h4_d", (HIDP, T), BF16)
        self.dram["out"] = nc.dram_tensor("out", [T, C], F32, kind="ExternalOutput").ap()

        ncc = self.nc
        self.af = Arena(self.es.enter_context(ncc.sbuf_tensor("arena_f", [128, 23000], F32)), 23000, F32)
        self.ab = Arena(self.es.enter_context(ncc.sbuf_tensor("arena_b", [128, 56000], BF16)), 56000, BF16)
        self.ps = [self.es.enter_context(ncc.psum_tensor(f"ps{i}", [128, 512], F32)) for i in range(4)]
        self.pw = [self.es.enter_context(ncc.psum_tensor(f"pw{i}", [128, 1024], F32)) for i in range(2)]
        self.wq_lo = self.ab.t.ap()[:, 46900:46900 + 3 * 1536]
        self.wo2 = self.ab.t.ap()[:, 51520:51520 + 4 * C]

        self.consts_small()

    def consts_small(self):
        pass

    def xn_view(self, ct, t0, n, shift=0):
        base = ct * (self.T + 1) + 1 + t0 - shift
        return self.xn[:, base:base + n]

    def dbg_out(self, name, shape, dt=F32):
        if name in self.dbg:
            ap = self.nc.dram_tensor("dbg_" + name, list(shape), dt, kind="ExternalOutput").ap()
            self.dbg_outs[name] = ap
            return ap
        return None

    def phase0(self):
        p, T = self.p, self.T
        d = self.dram
        s_xs = [self.sem(f"p0_x{i}") for i in range(4)]
        s_sta = self.sem("p0_sta")
        s_stv = self.sem("p0_stv")
        s_vch = self.sem("p0_vch")
        s_nv = self.sem("p0_norm")
        s_tr = self.sem("p0_tr")
        s_ev = self.sem("p0_ev")
        m = self.af.mark()
        mb = self.ab.mark()
        xtm = [self.af.alloc(C) for _ in range(4)]
        sq = [self.af.alloc(C) for _ in range(4)]
        stats = self.af.alloc(32)
        xnt = [self.ab.alloc(C) for _ in range(2)]
        p.emit("vector", lambda v: v.memset(
            self.xn.rearrange("p (c t) -> p c t", c=8)[:, :, 0:1], 0.0))
        for tt in range(self.NTT):
            xb = xtm[tt % 4]
            s_x = s_xs[tt % 4]
            waits = [(s_nv, tt - 4 + 1)] if tt >= 4 else []
            p.emit("sync", lambda sy, xb=xb, tt=tt: sy.dma_start(
                out=xb, in_=d["x"][tt * 128:(tt + 1) * 128, :]), waits=waits, incs=[(s_x, 16)])
            sqb = sq[tt % 4]
            col = lambda j, tt=tt: stats[:, (tt % 4) * 8 + j:(tt % 4) * 8 + j + 1]
            ssq, sxs, muv, tmp, var, lnv, rst = (col(j) for j in range(7))
            guards = []
            if tt >= 4:
                guards = [(s_nv, tt - 4 + 1), (s_stv, tt - 4 + 1)]
            p.emit("scalar", lambda sc, xb=xb, sqb=sqb, ssq=ssq: sc.activation(
                sqb, xb, AF.Square, accum_out=ssq),
                waits=[(s_x, None)] + guards + ([(s_sta, 2 * (tt - 4) + 1)] if tt >= 4 else []),
                incs=[(s_sta, 1)])
            p.emit("vector", lambda v, xb=xb, sxs=sxs: v.tensor_reduce(sxs, xb, AX.X, ALU.add),
                   waits=[(s_x, None)] + guards, incs=[(s_vch, 1)])
            p.emit("vector", lambda v, sxs=sxs, muv=muv: v.tensor_scalar_mul(muv, sxs, 1.0 / C),
                   waits=[(s_vch, 3 * tt + 1)], incs=[(s_vch, 1)])
            p.emit("vector", lambda v, muv=muv, tmp=tmp: v.tensor_tensor(tmp, muv, muv, ALU.mult),
                   waits=[(s_vch, 3 * tt + 2)], incs=[(s_vch, 1)])
            p.emit("vector", lambda v, ssq=ssq, tmp=tmp, var=var: v.scalar_tensor_tensor(
                var, ssq, 1.0 / C, tmp, ALU.mult, ALU.subtract),
                waits=[(s_vch, 3 * tt + 3), (s_sta, 2 * tt + 1)], incs=[(s_stv, 1)])
            p.emit("scalar", lambda sc, var=var, lnv=lnv: sc.activation(
                lnv, var, AF.Ln, bias=self.cc[LN_EPS]),
                waits=[(s_stv, tt + 1)], incs=[(self.sem("p0_lnch"), 1)])
            p.emit("scalar", lambda sc, lnv=lnv, rst=rst: sc.activation(rst, lnv, AF.Exp, scale=-0.5),
                   waits=[(self.sem("p0_lnch"), tt + 1)], incs=[(s_sta, 1)])
            xnb = xnt[tt % 2]
            waits = [(s_sta, 2 * tt + 2), (s_vch, 3 * tt + 2)]
            if tt >= 2:
                waits.append((s_ev, (tt - 2 + 1) * 2))
            p.emit("vector", lambda v, xb=xb, xnb=xnb, muv=muv, rst=rst: v.tensor_scalar(
                xnb, xb, muv, rst, ALU.subtract, ALU.mult),
                waits=waits, incs=[(s_nv, 1)])
            for g in range(2):
                bank = self.ps[(tt * 2 + g) % 4].ap().bitcast(BF16)
                def tr_f(t, xnb=xnb, g=g, bank=bank):
                    last = None
                    for j in range(4):
                        ct = g * 4 + j
                        last = t.transpose(bank[:, j * 128:(j + 1) * 128],
                                           xnb[:, ct * 128:(ct + 1) * 128], self.identb)
                    return last
                waits = [(s_nv, tt + 1)]
                if tt * 2 + g >= 4:
                    waits.append((s_ev, (tt * 2 + g) - 4 + 1))
                if tt == 0 and g == 0:
                    waits.append((self.sem("consts_g"), self.cg_sem_target))
                p.emit("tensor", tr_f, waits=waits, incs=[(s_tr, 1)])
                xn4 = self.xn.rearrange("p (c t) -> p c t", c=8)[:, g * 4:(g + 1) * 4,
                                                                1 + tt * 128:1 + (tt + 1) * 128]
                p.emit("vector", lambda v, bank=bank, xn4=xn4: v.tensor_copy(
                    xn4, bank[:, 0:512].rearrange("p (c t) -> p c t", c=4)),
                    waits=[(s_tr, tt * 2 + g + 1)], incs=[(s_ev, 1)])
        self.p0_done = [(s_ev, s_ev.n)]
        dbg = self.dbg_out("xn", (128, 8 * (T + 1)), BF16)
        if dbg is not None:
            p.emit("sync", lambda sy: sy.dma_start(out=dbg, in_=self.xn),
                   waits=[(s_ev, None)], incs=[(self.sem("dbg"), 16)])
        self.af.reset(m)
        self.ab.reset(mb)

    def load_consts(self):
        p, d = self.p, self.dram
        s_c = self.sem("consts")
        self.identb = self.ab.alloc(128)
        self.identf = self.af.alloc(128)
        s_cg = self.sem("consts_g")
        p.emit("gpsimd", lambda gp: gp.dma_start(out=self.identb, in_=d["ident"]), incs=[(s_cg, 16)])
        p.emit("sync", lambda sy: sy.dma_start(out=self.identf, in_=d["ident"]), incs=[(s_c, 16)])
        self.maskA = self.af.alloc(512)
        p.emit("sync", lambda sy: sy.dma_start(out=self.maskA, in_=d["maskA"]), incs=[(s_c, 16)])
        self.mask_low = self.af.alloc(128)
        self.mask_up = self.af.alloc(128)
        p.emit("sync", lambda sy: sy.dma_start(out=self.mask_low, in_=d["mask_low"]), incs=[(s_c, 16)])
        p.emit("sync", lambda sy: sy.dma_start(out=self.mask_up, in_=d["mask_up"]), incs=[(s_c, 16)])
        self.vecs = {}
        for nm in ("tmx", "tmw", "tmk", "tmv", "tmr", "tmg", "tdecay", "u_", "lnw_", "lnb_"):
            t = self.af.alloc(8)
            self.vecs[nm] = t
            p.emit("sync", lambda sy, t=t, nm=nm: sy.dma_start(out=t, in_=d[nm]), incs=[(s_c, 16)])
        self.blockones = self.ab.alloc(2)
        def bo_f(v):
            v.memset(self.blockones[0:64, 0:1], 1.0)
            v.memset(self.blockones[64:128, 0:1], 0.0)
            v.memset(self.blockones[0:64, 1:2], 0.0)
            return v.memset(self.blockones[64:128, 1:2], 1.0)
        s_cv = self.sem("consts_v")
        p.emit("vector", bo_f, incs=[(s_cv, 1)])
        self.onescol = self.ab.alloc(8)
        p.emit("vector", lambda v: v.memset(self.onescol, 1.0), incs=[(s_cv, 1)])
        self.cc = {}
        for val in (LN_EPS, GN_EPS, 1e-30, 0.0):
            t = self.af.alloc(1)
            self.cc[val] = t
            p.emit("vector", lambda v, t=t, val=val: v.memset(t, val), incs=[(s_cv, 1)])
        self.c_sem_target = s_c.n
        self.cv_sem_target = s_cv.n
        self.cg_sem_target = s_cg.n
        self.xn_mark = self.ab.mark()
        self.xn = self.ab.alloc(8 * (self.T + 1))

    def load_rwkv_weights(self):
        p, d = self.p, self.dram
        self.wmark_b = self.ab.mark()
        s_w = self.sem("rwkv_w")
        self.Wslot = [self.ab.alloc(8 * C), self.ab.alloc(8 * C)]
        for slot, nm in ((0, "w_k"), (1, "w_r")):
            t = self.Wslot[slot]
            p.emit("gpsimd", lambda gp, t=t, nm=nm: gp.dma_start(
                out=t.rearrange("p (g n) -> p g n", g=8),
                in_=d[nm].rearrange("(g p) n -> p g n", p=128)), incs=[(s_w, 16)])
        self.maa_w1 = self.ab.alloc(8 * 160)
        p.emit("gpsimd", lambda gp: gp.dma_start(
            out=self.maa_w1.rearrange("p (g n) -> p g n", g=8),
            in_=d["maa_w1"].rearrange("(g p) n -> p g n", p=128)), incs=[(s_w, 16)])
        self.maa_w2 = self.ab.alloc(5 * C)
        p.emit("gpsimd", lambda gp: gp.dma_start(
            out=self.maa_w2[0:32, :], in_=d["maa_w2"]), incs=[(s_w, 16)])
        self.td_w2 = self.ab.alloc(C)[0:64, :]
        p.emit("gpsimd", lambda gp: gp.dma_start(
            out=self.td_w2, in_=d["td_w2"]), incs=[(s_w, 16)])
        self.td_w1 = self.ab.alloc(8 * 64)
        p.emit("gpsimd", lambda gp: gp.dma_start(
            out=self.td_w1.rearrange("p (g n) -> p g n", g=8),
            in_=d["td_w1"].rearrange("(g p) n -> p g n", p=128)), incs=[(s_w, 16)])
        self.rwkv_w = s_w

    def phase1(self):
        p, T, d = self.p, self.T, self.dram
        TS = 256
        NTIL = T // TS
        self.dvec = self.af.alloc(self.NTT * 16)
        self.e3 = self.af.alloc(8 * self.NCHK)
        m = self.af.mark(); mb = self.ab.mark()
        xx2 = [self.af.alloc(8 * TS) for _ in range(2)]
        kf = self.af.alloc(8 * TS)
        ld = self.af.alloc(8 * TS)
        q = self.af.alloc(8 * TS)
        teb = [self.af.alloc(TS) for _ in range(2)]
        e1b = [self.af.alloc(TS) for _ in range(2)]
        e2b = [self.af.alloc(TS) for _ in range(2)]
        zbb = [self.ab.alloc(TS) for _ in range(2)]
        qlb = self.af.alloc(2)
        xxx1 = self.ab.alloc(8 * TS)
        oper = [self.ab.alloc(8 * TS) for _ in range(3)]
        t5f1 = [self.ab.alloc(TS) for _ in range(5)]
        tw = self.ab.alloc(TS)
        rkub = self.ab.alloc(TS)
        fRQ = self.ab.alloc(8 * TS)
        fKK = self.ab.alloc(8 * TS)
        fV = fRQ
        fG = fKK
        msb2 = [self.af.alloc(TS) for _ in range(2)]
        rrb2 = [self.af.alloc(TS) for _ in range(2)]
        sv = self.sem("p1_v")
        sa = self.sem("p1_a")
        sm = self.sem("p1_m")
        sl = self.sem("p1_pool")
        sst = self.sem("p1_st")
        sw2a = self.sem("p1_w2a")
        sw2b = self.sem("p1_w2b")

        s_zro = self.sem("p1_zro")
        p.emit("vector", lambda v: v.memset(fKK, 0.0), waits=list(self.p0_done), incs=[(sv, 1)])
        vv0 = d["rq_d"].rearrange("(c p) t -> p c t", c=8)
        for cz in range(8):
            p.emit("sync", lambda sy, cz=cz: sy.dma_start(
                out=vv0[64:192, cz, :], in_=fKK[:, 0:T]),
                waits=[(sv, None)], incs=[(s_zro, 16)])
        zro_t = s_zro.n

        def tile_pass(it, pass2):
            t0 = it * TS
            c0g = (it * TS) // L
            G = self.guard; SG = self.set_guard
            par = it % 2
            xx = xx2[par]
            xxx = xxx1
            t5f = t5f1
            for ct in range(8):
                w0 = G(("xx", par, ct)) + (self.p0_done + [
                    (self.sem("consts"), self.c_sem_target),
                    (self.sem("consts_v"), self.cv_sem_target)]
                    if (it == 0 and not pass2 and ct == 0) else [])
                p.emit("vector", lambda v, ct=ct, t0=t0, xx=xx: v.tensor_tensor(
                    xx[:, ct * TS:(ct + 1) * TS], self.xn_view(ct, t0, TS, shift=1),
                    self.xn_view(ct, t0, TS), ALU.subtract),
                    waits=w0, incs=[(sv, 1)])
                p.emit("vector", lambda v, ct=ct, t0=t0: v.scalar_tensor_tensor(
                    xxx[:, ct * TS:(ct + 1) * TS], xx[:, ct * TS:(ct + 1) * TS],
                    self.vecs["tmx"][:, ct:ct + 1], self.xn_view(ct, t0, TS),
                    ALU.mult, ALU.add), waits=G(("xxx", par, ct)) + [(sv, None)], incs=[(sv, 1)])
            xxx_done = sv.n
            for f5 in range(5):
                pb = self.ps[f5 % 2]
                po = 0
                def t5mm(t, f5=f5, pb=pb, po=po):
                    last = None
                    for k in range(8):
                        last = t.matmul(pb.ap()[0:32, po:po + TS],
                                        self.maa_w1[:, k * 160 + f5 * 32:k * 160 + f5 * 32 + 32],
                                        xxx[:, k * TS:(k + 1) * TS],
                                        start=(k == 0), stop=(k == 7))
                    return last
                p.emit("tensor", t5mm, waits=[(sv, xxx_done), (self.rwkv_w, None)] +
                       G(("t5ps", f5 % 2)), incs=[(sm, 1)])
                p.emit("scalar", lambda sc, f5=f5, pb=pb, po=po: sc.activation(
                    t5f[f5][0:32, :], pb.ap()[0:32, po:po + TS], AF.Tanh),
                    waits=[(sm, None)] + G(("t5f", par, f5)), incs=[(sa, 1)])
                SG(("t5ps", f5 % 2), sa)
            fmap = {0: "tmw", 1: "tmk", 2: "tmv", 3: "tmr", 4: "tmg"}
            fs = [2, 4] if pass2 else [0, 1, 3]
            t5_a = {f5: None for f5 in range(5)}
            for fi, f in enumerate(fs):
                nmv = fmap[f]
                ob = oper[fi] if not pass2 else oper[fi % 2]
                t5a_cnt = sa.n
                for ct in range(8):
                    pb = self.ps[2 + ct % 2]
                    p.emit("tensor", lambda t, f=f, ct=ct, pb=pb: t.matmul(
                        pb.ap()[:, 0:TS], self.maa_w2[0:32, f * C + ct * 128:f * C + ct * 128 + 128],
                        t5f[f][0:32, :], start=True, stop=True),
                        waits=[(sa, t5a_cnt)] + G(("pk", ct % 2)), incs=[(sm, 1)])
                    msb = msb2[ct % 2]
                    p.emit("scalar", lambda sc, pb=pb, msb=msb: sc.activation(
                        msb, pb.ap()[:, 0:TS], AF.Copy),
                        waits=[(sm, None)] + G(("msb", ct % 2)), incs=[(sa, 1)])
                    SG(("pk", ct % 2), sa)
                    zb = zbb[ct % 2]
                    p.emit("vector", lambda v, ct=ct, zb=zb, msb=msb, nmv=nmv, xx=xx: v.scalar_tensor_tensor(
                        zb, msb, self.vecs[nmv][:, ct:ct + 1],
                        xx[:, ct * TS:(ct + 1) * TS], ALU.add, ALU.mult),
                        waits=[(sa, None)] + G(("zbb", ct % 2)), incs=[(sv, 1)])
                    SG(("msb", ct % 2), sv)
                    p.emit("vector", lambda v, ct=ct, zb=zb, ob=ob, t0=t0: v.tensor_tensor(
                        ob[:, ct * TS:(ct + 1) * TS], zb, self.xn_view(ct, t0, TS), ALU.add),
                        waits=[(sv, None)] + G(("oper", ob.offset, ct)), incs=[(sl, 1)])
                    SG(("zbb", ct % 2), sl)
            oper_done = sl.n
            if not pass2:
                ob = oper[0]
                def twmm(t):
                    last = None
                    for k in range(8):
                        last = t.matmul(self.pw[0].ap()[0:64, 0:TS],
                                        self.td_w1[:, k * 64:(k + 1) * 64],
                                        ob[:, k * TS:(k + 1) * TS], start=(k == 0), stop=(k == 7))
                    return last
                p.emit("tensor", twmm, waits=[(sl, oper_done - 16)] + G(("pw0",)), incs=[(sm, 1)])
                p.emit("scalar", lambda sc: sc.activation(tw[0:64, :], self.pw[0].ap()[0:64, 0:TS],
                                                          AF.Tanh), waits=[(sm, None)], incs=[(sa, 1)])
                SG(("pw0",), sa)
                tw_done = sa.n
                for ct in range(8):
                    pb = self.ps[2 + ct % 2]
                    p.emit("tensor", lambda t, ct=ct, pb=pb: t.matmul(
                        pb.ap()[:, 0:TS], self.td_w2[:, ct * 128:(ct + 1) * 128], tw[0:64, :],
                        start=True, stop=True),
                        waits=[(sa, tw_done)] + G(("pk", ct % 2)), incs=[(sm, 1)])
                    p.emit("scalar", lambda sc, ct=ct, pb=pb: sc.activation(
                        ld[:, ct * TS:(ct + 1) * TS], pb.ap()[:, 0:TS], AF.Exp,
                        bias=self.vecs["tdecay"][:, ct:ct + 1]),
                        waits=[(sm, None)] + G(("ld", ct)), incs=[(sa, 1)])
                    SG(("pk", ct % 2), sa)
                    for cc_ in range(TS // L):
                        p.emit("vector", lambda v, ct=ct, cc_=cc_: v.tensor_tensor_scan(
                            q[:, ct * TS + cc_ * L: ct * TS + (cc_ + 1) * L],
                            ld[:, ct * TS + cc_ * L: ct * TS + (cc_ + 1) * L],
                            self.cc[0.0].broadcast_to([128, L]), 0.0, ALU.add, ALU.add),
                            waits=[(sa, None)] + G(("q", ct, cc_)), incs=[(sv, 1)])
                    p.emit("vector", lambda v, ct=ct: v.tensor_copy(
                        qlb, q[:, ct * TS + L - 1: (ct + 1) * TS: L]),
                        waits=[(sv, None)] + G(("qlb",)), incs=[(sv, 1)])
                    p.emit("scalar", lambda sc, ct=ct, c0g=c0g: sc.activation(
                        self.e3[:, ct * self.NCHK + c0g: ct * self.NCHK + c0g + TS // L],
                        qlb, AF.Exp, scale=-1.0),
                        waits=[(sv, None)], incs=[(sa, 1)])
                    SG(("qlb",), sa)
                for ct in range(8):
                    pb = self.ps[2 + ct % 2]
                    ob1 = oper[1]
                    def kmm(t, ct=ct, pb=pb, ob1=ob1):
                        last = None
                        for k in range(8):
                            last = t.matmul(pb.ap()[:, 0:TS],
                                            self.Wslot[0][:, k * C + ct * 128: k * C + ct * 128 + 128],
                                            ob1[:, k * TS:(k + 1) * TS], start=(k == 0), stop=(k == 7))
                        return last
                    p.emit("tensor", kmm, waits=[(sl, oper_done - 8), (s_zro, zro_t)] +
                           G(("pk", ct % 2)), incs=[(sm, 1)])
                    p.emit("scalar", lambda sc, ct=ct, pb=pb: sc.activation(
                        kf[:, ct * TS:(ct + 1) * TS], pb.ap()[:, 0:TS], AF.Copy),
                        waits=[(sm, None)] + G(("kf", ct)), incs=[(sa, 1)])
                    SG(("pk", ct % 2), sa)
                    p.emit("scalar", lambda sc, ct=ct, e2=e2b[ct % 2]: sc.activation(
                        e2, q[:, ct * TS:(ct + 1) * TS], AF.Exp),
                        waits=[(sv, None)] + G(("e2b", ct % 2)), incs=[(sa, 1)])
                    p.emit("vector", lambda v, ct=ct, e2=e2b[ct % 2]: v.tensor_tensor(
                        fKK[:, ct * TS:(ct + 1) * TS], kf[:, ct * TS:(ct + 1) * TS], e2, ALU.mult),
                        waits=[(sa, None)] + G(("fKK", ct)), incs=[(sv, 1)])
                    SG(("e2b", ct % 2), sv)
                    SG(("q", ct, 0), sv); SG(("q", ct, 1), sv)
                for ct in range(8):
                    pb = self.ps[2 + ct % 2]
                    ob2 = oper[2]
                    def rmm(t, ct=ct, pb=pb, ob2=ob2):
                        last = None
                        for k in range(8):
                            last = t.matmul(pb.ap()[:, 0:TS],
                                            self.Wslot[1][:, k * C + ct * 128: k * C + ct * 128 + 128],
                                            ob2[:, k * TS:(k + 1) * TS], start=(k == 0), stop=(k == 7))
                        return last
                    p.emit("tensor", rmm, waits=[(sl, oper_done)] + G(("pk", ct % 2)), incs=[(sm, 1)])
                    p.emit("vector", lambda v, ct=ct, te=teb[ct % 2]: v.tensor_tensor(
                        te, ld[:, ct * TS:(ct + 1) * TS], q[:, ct * TS:(ct + 1) * TS], ALU.subtract),
                        waits=[(sv, None)] + G(("teb", ct % 2)), incs=[(sv, 1)])
                    SG(("ld", ct), sv)
                    p.emit("scalar", lambda sc, ct=ct, te=teb[ct % 2], e1=e1b[ct % 2]: sc.activation(
                        e1, te, AF.Exp), waits=[(sv, None)] + G(("e1b", ct % 2)), incs=[(sa, 1)])
                    SG(("teb", ct % 2), sa)
                    rrb = rrb2[ct % 2]
                    p.emit("scalar", lambda sc, pb=pb, rrb=rrb: sc.activation(
                        rrb, pb.ap()[:, 0:TS], AF.Copy),
                        waits=[(sm, None)] + G(("rrb", ct % 2)), incs=[(sa, 1)])
                    SG(("pk", ct % 2), sa)
                    p.emit("vector", lambda v, ct=ct, rrb=rrb, e1=e1b[ct % 2]: v.tensor_tensor(
                        fRQ[:, ct * TS:(ct + 1) * TS], rrb, e1, ALU.mult),
                        waits=[(sa, None)] + G(("fRQ", ct)), incs=[(sv, 1)])
                    p.emit("vector", lambda v, ct=ct, rrb=rrb: v.scalar_tensor_tensor(
                        rkub, rrb, self.vecs["u_"][:, ct:ct + 1],
                        kf[:, ct * TS:(ct + 1) * TS], ALU.mult, ALU.mult),
                        waits=[(sv, None)] + G(("rkub",)), incs=[(sv, 1)])
                    SG(("rrb", ct % 2), sv)
                    SG(("e1b", ct % 2), sv)
                    SG(("kf", ct), sv)
                    for ts_ in range(2):
                        tt = it * 2 + ts_
                        p.emit("tensor", lambda t, ts_=ts_, ct=ct: t.matmul(
                            self.pw[1].ap()[:, 960:962], rkub[:, ts_ * 128:(ts_ + 1) * 128],
                            self.blockones, start=True, stop=True),
                            waits=[(sv, None)] + G(("pw1d",)), incs=[(sm, 1)])
                        p.emit("vector", lambda v, tt=tt, ct=ct: v.tensor_copy(
                            self.dvec[:, tt * 16 + 2 * ct: tt * 16 + 2 * ct + 2],
                            self.pw[1].ap()[:, 960:962]),
                            waits=[(sm, None)], incs=[(sv, 1)])
                        SG(("pw1d",), sv)
                    SG(("rkub",), sm)
                vv = d["rq_d"].rearrange("(c p) t -> p c t", c=8)
                p.emit("sync", lambda sy, t0=t0, vv=vv: sy.dma_start(
                    out=vv[0:64, :, t0:t0 + TS],
                    in_=fRQ[0:64, :].rearrange("p (g t) -> p g t", g=8)),
                    waits=[(sv, None)], incs=[(sst, 16)])
                p.emit("sync", lambda sy, t0=t0, vv=vv: sy.dma_start(
                    out=vv[192:256, :, t0:t0 + TS],
                    in_=fRQ[64:128, :].rearrange("p (g t) -> p g t", g=8)),
                    waits=[(sv, None)], incs=[(sst, 16)])
                p.emit("sync", lambda sy, it=it: sy.dma_start(
                    out=d["kk_d"].rearrange("(g p) t -> p g t", p=128)[:, :, t0:t0 + TS],
                    in_=fKK.rearrange("p (g t) -> p g t", g=8)),
                    waits=[(sv, None)], incs=[(sst, 16)])
                sst_t = sst.n
                for ct in range(8):
                    SG(("fRQ", ct), sst); SG(("fKK", ct), sst)
                    SG(("xx", par, ct), sl); SG(("xxx", par, ct), sm)
                for f5 in range(5):
                    SG(("t5f", par, f5), sm)
            else:
                for fi, (f, dst, silu) in enumerate(((2, fV, False), (4, fG, True))):
                    ob = oper[fi % 2]
                    wslot = self.Wslot[0] if f == 2 else self.Wslot[1]
                    for ts_ in range(2):
                        pb = self.pw[ts_ % 2]
                        def vmm(t, ob=ob, wslot=wslot, ts_=ts_, pb=pb):
                            last = None
                            for k in range(8):
                                for nh in range(2):
                                    last = t.matmul(
                                        pb.ap()[:, nh * 512:(nh + 1) * 512],
                                        ob[:, k * TS + ts_ * 128: k * TS + ts_ * 128 + 128],
                                        wslot[:, k * C + nh * 512: k * C + (nh + 1) * 512],
                                        start=(k == 0), stop=(k == 7))
                            return last
                        p.emit("tensor", vmm,
                               waits=[(sl, oper_done - 16 + (fi + 1) * 8),
                                      ((sw2a if f == 2 else sw2b), 16)] + G(("pwv", ts_ % 2)),
                               incs=[(sm, 1)])
                        if not silu:
                            p.emit("scalar", lambda sc, ts_=ts_, pb=pb, dst=dst: sc.activation(
                                dst[:, ts_ * C:(ts_ + 1) * C], pb.ap()[:, 0:C], AF.Copy),
                                waits=[(sm, None)] + G(("fV", ts_)), incs=[(sa, 1)])
                            SG(("pwv", ts_ % 2), sa)
                        else:
                            th = kf[:, ts_ * C:(ts_ + 1) * C]
                            p.emit("scalar", lambda sc, ts_=ts_, pb=pb, th=th: sc.activation(
                                th, pb.ap()[:, 0:C], AF.Tanh, scale=0.5),
                                waits=[(sm, None)] + G(("thv", ts_)), incs=[(sa, 1)])
                            p.emit("vector", lambda v, ts_=ts_, pb=pb, th=th, dst=dst: v.scalar_tensor_tensor(
                                dst[:, ts_ * C:(ts_ + 1) * C], th, 1.0, pb.ap()[:, 0:C],
                                ALU.add, ALU.mult),
                                waits=[(sa, None)] + G(("fG", ts_)), incs=[(sv, 1)])
                            SG(("pwv", ts_ % 2), sv)
                            SG(("thv", ts_), sv)
                p.emit("sync", lambda sy, it=it: sy.dma_start(
                    out=d["v_d"][t0:t0 + TS, :].rearrange("(g p) c -> p g c", p=128),
                    in_=fV.rearrange("p (g c) -> p g c", g=2)),
                    waits=[(sa, None)], incs=[(sst, 16)])
                p.emit("sync", lambda sy, it=it: sy.dma_start(
                    out=d["g2x_d"][t0:t0 + TS, :].rearrange("(g p) c -> p g c", p=128),
                    in_=fG.rearrange("p (g c) -> p g c", g=2)),
                    waits=[(sv, None)], incs=[(sst, 16)])
                for ts_ in range(2):
                    SG(("fV", ts_), sst); SG(("fG", ts_), sst)
                for ct in range(8):
                    SG(("xx", par, ct), sl); SG(("xxx", par, ct), sm)
                for f5 in range(5):
                    SG(("t5f", par, f5), sm)

        for it in range(NTIL):
            tile_pass(it, pass2=False)
        for ts_ in range(2):
            self.set_guard(("fV", ts_), sst)
            self.set_guard(("fG", ts_), sst)
            self.set_guard(("thv", ts_), sv)
            self.set_guard(("pwv", ts_), sv)
        p.emit("gpsimd", lambda gp: gp.dma_start(
            out=self.Wslot[0].rearrange("p (g n) -> p g n", g=8),
            in_=d["w_v"].rearrange("(g p) n -> p g n", p=128)),
            waits=[(sm, None)], incs=[(sw2a, 16)])
        p.emit("gpsimd", lambda gp: gp.dma_start(
            out=self.Wslot[1].rearrange("p (g n) -> p g n", g=8),
            in_=d["w_g"].rearrange("(g p) n -> p g n", p=128)),
            waits=[(sm, None)], incs=[(sw2b, 16)])
        for it in range(NTIL):
            tile_pass(it, pass2=True)
        self.p1_done = [(sst, sst.n), (sv, sv.n), (sa, sa.n), (sm, sm.n), (sl, sl.n)]
        for nm, buf in (("dvec", self.dvec), ("e3", self.e3)):
            dbg = self.dbg_out(nm, (128, buf.shape[1]))
            if dbg is not None:
                p.emit("sync", lambda sy, dbg=dbg, buf=buf: sy.dma_start(out=dbg, in_=buf),
                       waits=self.p1_done, incs=[(self.sem("dbg"), 16)])
        self.af.reset(m); self.ab.reset(mb)

    def prefetch_ffn0(self):
        p, d = self.p, self.dram
        s = self.sem("p4_gw")
        p.emit("gpsimd", lambda gp: gp.dma_start(
            out=self.wq_lo.rearrange("p (g n) -> p g n", g=3),
            in_=d["wqkv"][0].rearrange("(g p) n -> p g n", p=128)[:, 5:8, :]),
            waits=list(self.p1_done), incs=[(s, 16)])
        p.emit("gpsimd", lambda gp: gp.dma_start(
            out=self.wo2.rearrange("p (g n) -> p g n", g=4),
            in_=d["wo"][0].rearrange("(g p) n -> p g n", p=128)),
            waits=[], incs=[(s, 16)])

    def phase2(self, parts=31):
        p, T, d = self.p, self.T, self.dram
        NC = self.NCHK
        self.ab.reset(self.wmark_b)
        m = self.af.mark(); mb = self.ab.mark()
        self.yng = self.ab.alloc(self.NTT * C)
        self.p2_streams_mark = self.ab.mark()
        rqc = [self.ab.alloc(2 * C) for _ in range(2)]
        kkc = [self.ab.alloc(C) for _ in range(2)]
        vc = [self.ab.alloc(C) for _ in range(2)]
        gc = [self.ab.alloc(C) for _ in range(2)]
        s_rq = [self.sem(f"p2_rq{i}") for i in range(2)]
        s_kk = [self.sem(f"p2_kk{i}") for i in range(2)]
        s_v = [self.sem(f"p2_v{i}") for i in range(2)]
        s_g = [self.sem(f"p2_g{i}") for i in range(2)]
        kkt = self.ab.alloc(C)
        absb = self.ab.alloc(2048)
        Sst = self.af.alloc(8 * DH)
        stp = self.af.alloc(8 * DH)
        Sbf = self.ab.alloc(8 * DH)
        ysb = self.af.alloc(C)
        sqb = self.af.alloc(C)
        self.lnw_rep = self.af.alloc(C)
        self.lnb_rep = self.af.alloc(C)
        s_lnr = self.sem("p2_lnr")
        p.emit("sync", lambda sy: sy.dma_start(out=self.lnw_rep, in_=d["lnw_rep"]),
               waits=list(self.p1_done), incs=[(s_lnr, 16)])
        p.emit("sync", lambda sy: sy.dma_start(out=self.lnb_rep, in_=d["lnb_rep"]),
               waits=[], incs=[(s_lnr, 16)])
        st16 = self.af.alloc(7 * 16)
        col16 = lambda j: st16[:, j * 16:(j + 1) * 16]
        sm = self.sem("p2_mm"); sv = self.sem("p2_dv"); sa = self.sem("p2_act"); sl = self.sem("p2_pl")
        p.emit("vector", lambda v: v.memset(Sst, 0.0), waits=self.p1_done, incs=[(sv, 1)])
        wave_consumed = [None] * (NC + 2)

        def emit_loads(c, ldw):
            sl_ = c % 2
            p.emit("sync", lambda sy, c=c, b=rqc[sl_]: sy.dma_start(
                out=b.rearrange("p (g t) -> p g t", g=16),
                in_=d["rq_d"].rearrange("(g p) t -> p g t", p=128)[:, :, c * L:(c + 1) * L]),
                waits=ldw, incs=[(s_rq[sl_], 16)])
            p.emit("sync", lambda sy, c=c, b=kkc[sl_]: sy.dma_start(
                out=b.rearrange("p (g t) -> p g t", g=8),
                in_=d["kk_d"].rearrange("(g p) t -> p g t", p=128)[:, :, c * L:(c + 1) * L]),
                waits=ldw, incs=[(s_kk[sl_], 16)])
            p.emit("sync", lambda sy, c=c, b=vc[sl_]: sy.dma_start(
                out=b, in_=d["v_d"][c * L:(c + 1) * L, :]),
                waits=ldw, incs=[(s_v[sl_], 16)])
            p.emit("sync", lambda sy, c=c, b=gc[sl_]: sy.dma_start(
                out=b, in_=d["g2x_d"][c * L:(c + 1) * L, :]),
                waits=ldw, incs=[(s_g[sl_], 16)])

        emit_loads(0, list(self.p1_done))
        if NC > 1:
            emit_loads(1, list(self.p1_done))
        for c in range(NC):
            sl_ = c % 2
            RQ, KK, V, G = rqc[sl_], kkc[sl_], vc[sl_], gc[sl_]
            pw0b = self.pw[0].ap().bitcast(BF16)
            def trf(t, KK=KK):
                last = None
                for j in range(8):
                    last = t.transpose(pw0b[:, j * 128:(j + 1) * 128],
                                       KK[:, j * 128:(j + 1) * 128], self.identb)
                return last
            if not (parts & 1):
                continue
            p.emit("tensor", trf, waits=[(s_kk[sl_], s_kk[sl_].n)] + self.guard(("p2pw0",)),
                   incs=[(sm, 1)])
            p.emit("vector", lambda v: v.tensor_copy(kkt, pw0b[:, 0:1024]), waits=[(sm, None)], incs=[(sv, 1)])
            kkt_done = sv.n
            p.emit("vector", lambda v: v.tensor_copy(Sbf, Sst), waits=[(sv, None), (sm, None)],
                   incs=[(sv, 1)])
            bank_heads = [(self.ps[0], (0, 2, 4, 6)), (self.ps[1], (8, 10, 12, 14)),
                          (self.ps[2], (1, 3, 5, 7)), (self.ps[3], (9, 11, 13, 15))]
            for bank, heads in (bank_heads if parts & 2 else []):
                def amm(t, bank=bank, heads=heads, RQ=RQ, KK=KK):
                    last = None
                    for hh, h in enumerate(heads):
                        ct = h // 2; r0 = 64 * (h % 2)
                        last = t.matmul(bank.ap()[:, hh * 128:(hh + 1) * 128],
                                        KK[r0:r0 + 64, ct * 128:(ct + 1) * 128],
                                        RQ[r0:r0 + 64, h * 128:(h + 1) * 128],
                                        start=True, stop=True)
                    return last
                p.emit("tensor", amm, waits=[(s_rq[sl_], s_rq[sl_].n), (s_kk[sl_], s_kk[sl_].n)] +
                       self.guard(("Aps", bank_heads.index((bank, heads)) if False else id(bank))),
                       incs=[(sm, 1)])
                p.emit("vector", lambda v, bank=bank, heads=heads: v.tensor_tensor(
                    absb.rearrange("p (h t) -> p h t", h=16)[:, heads[0]:heads[0] + 7:2, :],
                    bank.ap()[:].rearrange("p (h t) -> p h t", h=4),
                    self.maskA.rearrange("p (h t) -> p h t", h=4), ALU.mult),
                    waits=[(sm, None)], incs=[(sv, 1)])
                self.set_guard(("Aps", id(bank)), sv)
            aev_done = sv.n
            for h in (range(16) if parts & 4 else []):
                ct = h // 2; r0 = 64 * (h % 2)
                def ymm(t, h=h, ct=ct, r0=r0, RQ=RQ, V=V):
                    t.matmul(self.pw[0].ap()[:, h * 64:(h + 1) * 64],
                             absb[:, h * 128:(h + 1) * 128],
                             V[:, h * 64:(h + 1) * 64], start=True, stop=False)
                    return t.matmul(self.pw[0].ap()[:, h * 64:(h + 1) * 64],
                                    RQ[:, h * 128:(h + 1) * 128],
                                    Sbf[:, ct * 64:(ct + 1) * 64],
                                    start=False, stop=True)
                p.emit("tensor", ymm, waits=[(sv, aev_done), (s_v[sl_], s_v[sl_].n)] +
                       self.guard(("p2pw0",)), incs=[(sm, 1)])
            y_done = sm.n
            for h in (range(16) if parts & 4 else []):
                ct = h // 2; r0 = 64 * (h % 2)
                p.emit("tensor", lambda t, h=h, ct=ct, r0=r0, V=V: t.matmul(
                    self.pw[1].ap()[0:64, h * 64:(h + 1) * 64],
                    kkt[:, ct * 128 + r0: ct * 128 + r0 + 64],
                    V[:, h * 64:(h + 1) * 64], start=True, stop=True),
                    waits=[(sv, kkt_done)], incs=[(sm, 1)])
            u_done = sm.n
            for h in (range(16) if parts & 8 else []):
                p.emit("vector", lambda v, h=h, c=c, V=V: v.scalar_tensor_tensor(
                    ysb[:, h * 64:(h + 1) * 64], V[:, h * 64:(h + 1) * 64],
                    self.dvec[:, c * 16 + h: c * 16 + h + 1],
                    self.pw[0].ap()[:, h * 64:(h + 1) * 64], ALU.mult, ALU.add),
                    waits=[(sm, y_done), (sv, None), (sl, None)], incs=[(sv, 1)])
            yev_done = sv.n
            self.set_guard(("p2pw0",), sv)
            if parts & 8:
                uview = self.pw[1].ap()[0:64, :].rearrange("p (h x e) -> p h x e", h=8, x=2)
                p.emit("vector", lambda v, uview=uview: v.tensor_tensor(
                    stp[0:64, :].rearrange("p (h e) -> p h e", h=8),
                    Sst[0:64, :].rearrange("p (h e) -> p h e", h=8),
                    uview[:, :, 0, :], ALU.add),
                    waits=[(sm, u_done)], incs=[(sv, 1)])
                p.emit("vector", lambda v, uview=uview: v.tensor_tensor(
                    stp[64:128, :].rearrange("p (h e) -> p h e", h=8),
                    Sst[64:128, :].rearrange("p (h e) -> p h e", h=8),
                    uview[:, :, 1, :], ALU.add),
                    waits=[(sm, u_done), (sv, None)], incs=[(sv, 1)])
                e3v = self.e3.rearrange("p (ct n) -> p ct n", ct=8)[:, :, c:c + 1]
                p.emit("vector", lambda v, e3v=e3v: v.tensor_tensor(
                    Sst.rearrange("p (ct e) -> p ct e", ct=8), stp.rearrange("p (ct e) -> p ct e", ct=8),
                    e3v.broadcast_to([128, 8, DH]), ALU.mult),
                    waits=[(sv, None)], incs=[(sv, 1)])
            if not (parts & 16):
                wave_consumed[c] = (sv.n, sm.n)
                if c + 2 < NC:
                    emit_loads(c + 2, [(sv, sv.n), (sm, sm.n)])
                continue
            p.emit("gpsimd", lambda g_: g_.tensor_tensor(sqb, ysb, ysb, ALU.mult),
                   waits=[(sv, yev_done), (sv, None)], incs=[(sl, 1)])
            ysum, sqs, mu, msq, var, lnv, rstd = (col16(j) for j in range(7))
            p.emit("vector", lambda v: v.tensor_reduce(
                ysum, ysb.rearrange("p (h j) -> p h j", h=16), AX.X, ALU.add),
                waits=[(sv, yev_done)], incs=[(sv, 1)])
            p.emit("vector", lambda v: v.tensor_reduce(
                sqs, sqb.rearrange("p (h j) -> p h j", h=16), AX.X, ALU.add),
                waits=[(sl, None)], incs=[(sv, 1)])
            p.emit("vector", lambda v: v.tensor_scalar_mul(mu, ysum, 1.0 / DH),
                   waits=[(sv, None)], incs=[(sv, 1)])
            p.emit("vector", lambda v: v.tensor_tensor(msq, mu, mu, ALU.mult),
                   waits=[(sv, None)], incs=[(sv, 1)])
            p.emit("vector", lambda v: v.scalar_tensor_tensor(
                var, sqs, 1.0 / DH, msq, ALU.mult, ALU.subtract),
                waits=[(sv, None)], incs=[(sv, 1)])
            p.emit("scalar", lambda sc: sc.activation(lnv, var, AF.Ln, bias=self.cc[GN_EPS]),
                   waits=[(sv, None)], incs=[(sa, 1)])
            p.emit("scalar", lambda sc: sc.activation(rstd, lnv, AF.Exp, scale=-0.5),
                   waits=[(sa, None)], incs=[(sa, 1)])
            for h in range(16):
                p.emit("vector", lambda v, h=h: v.tensor_scalar(
                    sqb[:, h * 64:(h + 1) * 64], ysb[:, h * 64:(h + 1) * 64],
                    mu[:, h:h + 1], rstd[:, h:h + 1], ALU.subtract, ALU.mult),
                    waits=[(sa, None), (sl, None)], incs=[(sv, 1)])
            p.emit("gpsimd", lambda g_: g_.tensor_tensor(ysb, sqb, self.lnw_rep, ALU.mult),
                   waits=[(sv, None), (sl, None), (self.sem("p2_lnr"), 32)], incs=[(sl, 1)])
            p.emit("gpsimd", lambda g_: g_.tensor_tensor(sqb, ysb, self.lnb_rep, ALU.add),
                   waits=[(sl, None)], incs=[(sl, 1)])
            p.emit("vector", lambda v, c=c, G=G: v.scalar_tensor_tensor(
                self.yng[:, c * C:(c + 1) * C], sqb, 0.5, G, ALU.mult, ALU.mult),
                waits=[(sl, None), (s_g[sl_], s_g[sl_].n)], incs=[(sv, 1)])
            wave_consumed[c] = (sv.n, sm.n)
            if c + 2 < NC:
                emit_loads(c + 2, [(sv, sv.n), (sm, sm.n)])
        self.p2_done = [(sv, sv.n), (sm, sm.n), (sa, sa.n), (sl, sl.n)]
        dbg = self.dbg_out("yng", (128, self.NTT * C), BF16)
        if dbg is not None:
            p.emit("sync", lambda sy, dbg=dbg: sy.dma_start(out=dbg, in_=self.yng),
                   waits=self.p2_done, incs=[(self.sem("dbg"), 16)])
        self.af.reset(m)
        self.p2_mb = mb

    def phase3(self):
        p, T, d = self.p, self.T, self.dram
        m = self.af.mark()
        self.ab.reset(self.p2_streams_mark)
        yngf = self.ab.alloc(C)
        self.wo_sb = self.ab.alloc(8 * C)
        s_til = [self.af.alloc(C) for _ in range(2)]
        sm = self.sem("p3_mm"); sv = self.sem("p3_dv"); sa = self.sem("p3_act")
        ssts = [self.sem(f"p3_st{i}") for i in range(2)]
        ps3b = self.ps[3].ap().bitcast(BF16)
        s_wo = self.sem("p2_wo")
        p.emit("gpsimd", lambda gp: gp.dma_start(
            out=self.wo_sb.rearrange("p (g n) -> p g n", g=8),
            in_=d["w_o"].rearrange("(g p) n -> p g n", p=128)),
            waits=list(self.p2_done), incs=[(s_wo, 16)])
        for tt in range(self.NTT):
            def trf(t, tt=tt):
                last = None
                for ct in range(8):
                    last = t.transpose(ps3b[:, ct * 128:(ct + 1) * 128],
                                       self.yng[:, tt * C + ct * 128: tt * C + (ct + 1) * 128],
                                       self.identb)
                return last
            w_ = [(sv, None)] + (self.p2_done if tt == 0 else [])
            p.emit("tensor", trf, waits=w_, incs=[(sm, 1)])
            p.emit("vector", lambda v: v.tensor_copy(yngf, ps3b), waits=[(sm, None)], incs=[(sv, 1)])
            def womm(t, tt=tt):
                last = None
                for ct in range(8):
                    for nh in range(2):
                        last = t.matmul(self.pw[0].ap()[:, nh * 512:(nh + 1) * 512],
                                        yngf[:, ct * 128:(ct + 1) * 128],
                                        self.wo_sb[:, ct * C + nh * 512: ct * C + nh * 512 + 512],
                                        start=(ct == 0), stop=(ct == 7))
                return last
            p.emit("tensor", womm, waits=[(sv, None), (s_wo, 16), (sa, None)], incs=[(sm, 1)])
            stl = s_til[tt % 2]
            sst = ssts[tt % 2]
            w_ = [(sm, None)]
            if tt >= 2:
                w_.append((sst, sst.n))
            p.emit("scalar", lambda sc, stl=stl: sc.activation(stl, self.pw[0].ap()[:], AF.Copy),
                   waits=w_, incs=[(sa, 1)])
            p.emit("sync", lambda sy, tt=tt, stl=stl: sy.dma_start(
                out=d["s_d"][tt * 128:(tt + 1) * 128, :], in_=stl),
                waits=[(sa, None)], incs=[(sst, 16)])
        self.p3_done = [(ssts[0], ssts[0].n), (ssts[1], ssts[1].n), (sm, sm.n), (sv, sv.n), (sa, sa.n)]
        self.af.reset(m)
        self.ab.reset(self.p2_mb)

    def phase4(self):
        p, T, d = self.p, self.T, self.dram
        NW = self.NW
        sm = self.sem("p4_mm"); sv = self.sem("p4_dv"); sa = self.sem("p4_act"); sl = self.sem("p4_pl")
        sld = self.sem("p4_ld")
        sgw = self.sem("p4_gw")
        sg1 = [self.sem(f"p4_g1{i}") for i in range(2)]
        sg2 = [self.sem(f"p4_g2{i}") for i in range(2)]
        sh4 = [self.sem(f"p4_h4{i}") for i in range(2)]
        sst4 = self.sem("p4_st")
        mf0 = self.af.mark()
        xf = self.af.alloc(8 * (T + 1))
        self.xf = xf
        for ct in range(8):
            p.emit("vector", lambda v, ct=ct: v.tensor_copy(
                xf[:, ct * (T + 1):(ct + 1) * (T + 1)], self.xn[:, ct * (T + 1):(ct + 1) * (T + 1)]),
                waits=(list(self.p3_done) + [(sv, None)]) if ct == 0 else [], incs=[(sv, 1)])
        xf_init = sv.n
        self.ab.reset(self.xn_mark)
        qtm = self.af.alloc(512); q2b = self.af.alloc(512)
        rot1 = self.af.alloc(512); rot2 = self.af.alloc(512)
        s8 = self.af.alloc(8); l8 = self.af.alloc(8); r8 = self.af.alloc(8)
        r_mu = qtm[:, 0:512]; r_e2 = q2b[:, 0:512]
        r_ms = rot1[:, 0:512]; r_va = rot2[:, 0:512]
        r_rs = self.af.alloc(512); r_msb = self.af.alloc(512)
        onesrow = self.af.alloc(128)
        p.emit("vector", lambda v: v.memset(onesrow[0:1, :], 1.0), incs=[(sv, 1)])
        qsc = self.af.alloc(FDH); ksc = self.af.alloc(FDH)
        lngv = self.af.alloc(8); lnbv = self.af.alloc(8); b2v = self.af.alloc(8)
        b1v = self.af.alloc(44)
        scr1 = self.af.alloc(512); scr2 = self.af.alloc(512)
        rcp4 = self.af.alloc(4)
        m4a = self.ab.mark()

        def attn_stage(l):
            self.ab.reset(m4a)
            wq_hi = self.ab.alloc(5 * 3 * INNER)
            wo_sb2 = self.wo2
            p.emit("gpsimd", lambda gp, l=l: gp.dma_start(
                out=wq_hi.rearrange("p (g n) -> p g n", g=5),
                in_=d["wqkv"][l].rearrange("(g p) n -> p g n", p=128)[:, 0:5, :]),
                waits=[(sm, None), (sv, None)], incs=[(sgw, 16)])
            if l > 0:
                p.emit("gpsimd", lambda gp, l=l: gp.dma_start(
                    out=self.wq_lo.rearrange("p (g n) -> p g n", g=3),
                    in_=d["wqkv"][l].rearrange("(g p) n -> p g n", p=128)[:, 5:8, :]),
                    waits=[(sm, None), (sv, None)], incs=[(sgw, 16)])
                p.emit("gpsimd", lambda gp, l=l: gp.dma_start(
                    out=wo_sb2.rearrange("p (g n) -> p g n", g=4),
                    in_=d["wo"][l].rearrange("(g p) n -> p g n", p=128)),
                    waits=[], incs=[(sgw, 16)])
            tabs_raw = {}
            for nm in ("ct_q", "st_q", "ct_k", "st_k"):
                t = self.ab.alloc(self.NTT * FDH)
                tabs_raw[nm] = t
                p.emit("gpsimd", lambda gp, t=t, nm=nm: gp.dma_start(
                    out=t.rearrange("p (g n) -> p g n", g=self.NTT),
                    in_=d[nm].rearrange("(g p) n -> p g n", p=128)),
                    waits=([(sv, xf_init)] + list(self.p3_done)) if l == 0 else
                          [(sm, None), (sv, None)],
                    incs=[(sgw, 16)])
            gw_t = sgw.n
            for nm, t in (("lng", lngv), ("lnb2", lnbv), ("b2", b2v), ("b1p", b1v)):
                p.emit("sync", lambda sy, nm=nm, t=t, l=l: sy.dma_start(out=t, in_=d[nm][l]),
                       waits=self.guard(("smallvec",)), incs=[(sld, 16)])
            ld_t = sld.n
            tabs = tabs_raw
            qscb = self.ab.alloc(FDH)
            kscb = self.ab.alloc(FDH)
            p.emit("gpsimd", lambda gp, l=l: gp.dma_start(out=qscb, in_=d["qscale_rep"][l]),
                   waits=[(sm, None), (sv, None)], incs=[(sgw, 16)])
            p.emit("gpsimd", lambda gp, l=l: gp.dma_start(out=kscb, in_=d["kscale_rep"][l]),
                   waits=[], incs=[(sgw, 16)])
            gw_t = sgw.n
            for nm, scv in (("ct_q", qscb), ("st_q", qscb), ("ct_k", kscb), ("st_k", kscb)):
                p.emit("vector", lambda v, nm=nm, scv=scv: v.tensor_tensor(
                    tabs_raw[nm].rearrange("p (g e) -> p g e", g=self.NTT),
                    tabs_raw[nm].rearrange("p (g e) -> p g e", g=self.NTT),
                    scv.unsqueeze(1).broadcast_to([128, self.NTT, FDH]), ALU.mult),
                    waits=[(sgw, gw_t), (sv, None)], incs=[(sv, 1)])
            zwin = self.ab.alloc(8 * 512)
            kfm = [self.ab.alloc(4 * 512) for _ in range(3)]
            vtl = [self.ab.alloc(4 * 520) for _ in range(3)]
            qfm = self.ab.alloc(4 * 512)
            attn2 = [self.ab.alloc(2560) for _ in range(2)]
            otm = self.ab.alloc(4 * 512)
            b_rotB = otm
            ofm = self.ab.alloc(4 * 512)
            b_sqB = ofm
            q_tm = self.ab.alloc(2048)
            k_tm = self.ab.alloc(2048)
            s32 = self.af.alloc(32)
            xbnd = self.ab.alloc(8 * NW)
            assert self.ab.off <= 46900, f"attn allocs {self.ab.off} hit prefetch region"
            for rwi in range(3):
                p.emit("vector", lambda v, rwi=rwi: v.memset(
                    vtl[rwi].rearrange("p (s h e) -> p s h e", s=4, h=8)[:, :, :, 64:65], 1.0),
                    waits=[(sv, None), (sm, None)], incs=[(sv, 1)])
            p.emit("vector", lambda v: v.tensor_copy(
                xbnd.rearrange("p (c w2) -> p c w2", c=8),
                xf.rearrange("p (c t) -> p c t", c=8, t=T + 1)[:, :, 0:T:512]),
                waits=[(sv, None), (sm, None)], incs=[(sv, 1)])
            xbnd_done = sv.n

            def spans(w):
                out = []
                if w >= 1:
                    for kt in range(4):
                        out.append((kt, (w - 1) % 3, kt, 0, 128 * (kt + 1), "low", 128 * kt))
                for kt in range(4, 8):
                    tq0 = 128 * (kt - 4)
                    out.append((kt, w % 3, kt - 4, tq0, 512 - tq0, "up", 0))
                return out

            ps3b = self.ps[3].ap().bitcast(BF16)

            qk_evac = {}
            qk_done = {}

            def front(w):
                sp = spans(w)
                offs = {}
                o = 0
                for e in sp:
                    offs[e[0]] = o
                    o += e[4]
                for ct in range(8):
                    sh = 1 if ct >= 4 else 0
                    p.emit("vector", lambda v, ct=ct, w=w, sh=sh: v.tensor_copy(
                        zwin[:, ct * 512:(ct + 1) * 512],
                        xf[:, ct * (T + 1) + 1 + w * 512 - sh: ct * (T + 1) + 1 + (w + 1) * 512 - sh]),
                        waits=[(sv, None)] + self.guard(("zwin", ct)), incs=[(sv, 1)])
                for ct in range(4, 8):
                    p.emit("vector", lambda v, ct=ct, w=w: v.tensor_copy(
                        zwin[:, ct * 512:ct * 512 + 1], xbnd[:, ct * NW + w: ct * NW + w + 1]),
                        waits=[(sv, None)], incs=[(sv, 1)])
                zdone = sv.n
                for ts_ in range(4):
                    for wh, pb in ((0, self.ps[0]), (1, self.ps[1]), (2, self.ps[2])):
                        def qkvmm(t, ts_=ts_, wh=wh, pb=pb):
                            last = None
                            for k in range(8):
                                wsrc = (wq_hi[:, k * 1536 + wh * 512: k * 1536 + (wh + 1) * 512]
                                        if k < 5 else
                                        self.wq_lo[:, (k - 5) * 1536 + wh * 512: (k - 5) * 1536 + (wh + 1) * 512])
                                last = t.matmul(pb.ap()[:],
                                                zwin[:, k * 512 + ts_ * 128: k * 512 + ts_ * 128 + 128],
                                                wsrc,
                                                start=(k == 0), stop=(k == 7))
                            return last
                        p.emit("tensor", qkvmm,
                               waits=[(sv, zdone), (sgw, gw_t)] + self.guard(("ps", wh)),
                               incs=[(sm, 1)])
                    p.emit("scalar", lambda sc, ts_=ts_, w=w: sc.activation(
                        vtl[w % 3].rearrange("p (s h e) -> p s h e", s=4, h=8)[:, ts_, :, 0:64],
                        self.ps[2].ap()[:].rearrange("p (h e) -> p h e", h=8), AF.Copy),
                        waits=[(sm, None)] + self.guard(("vtlrd", w % 3)), incs=[(sa, 1)])
                    self.set_guard(("ps", 2), sa)
                    p.emit("scalar", lambda sc, ts_=ts_: sc.activation(
                        q_tm[:, ts_ * 512:(ts_ + 1) * 512], self.ps[0].ap()[:], AF.Copy),
                        waits=[(sm, None)] + self.guard(("q_tm",)), incs=[(sa, 1)])
                    self.set_guard(("ps", 0), sa)
                    p.emit("scalar", lambda sc, ts_=ts_: sc.activation(
                        k_tm[:, ts_ * 512:(ts_ + 1) * 512], self.ps[1].ap()[:], AF.Copy),
                        waits=[(sm, None)] + self.guard(("k_tm",)), incs=[(sa, 1)])
                    self.set_guard(("ps", 1), sa)
                qk_evac[w] = sa.n
                for ct in range(8):
                    self.set_guard(('zwin', ct), sm)
                return sp, offs

            def chains(w):
                    for which, src_tm, ct_t, st_t, dstfm in (
                            (0, q_tm, "ct_q", "st_q", qfm),
                            (1, k_tm, "ct_k", "st_k", kfm[w % 3])):
                        p.emit("scalar", lambda sc, src_tm=src_tm: sc.activation(
                            b_sqB, src_tm, AF.Square),
                            waits=[(sa, qk_evac[w])] + self.guard(("b_sq",)), incs=[(sa, 1)])
                        p.emit("vector", lambda v: v.tensor_reduce(
                            s32, b_sqB.rearrange("p (g e) -> p g e", g=32), AX.X, ALU.add),
                            waits=[(sa, None)] + self.guard(("s32",)), incs=[(sv, 1)])
                        p.emit("scalar", lambda sc: sc.activation(s32, s32, AF.Ln, bias=self.cc[1e-30]),
                               waits=[(sv, None)], incs=[(sa, 1)])
                        p.emit("scalar", lambda sc: sc.activation(s32, s32, AF.Exp, scale=-0.5),
                               waits=[(sa, None)], incs=[(sa, 1)])
                        p.emit("vector", lambda v: v.tensor_scalar_min(s32, s32, 1e12),
                               waits=[(sa, None)], incs=[(sv, 1)])
                        p.emit("vector", lambda v, src_tm=src_tm: v.tensor_tensor(
                            b_sqB.rearrange("p (g e) -> p g e", g=32),
                            src_tm.rearrange("p (g e) -> p g e", g=32),
                            s32.unsqueeze(2).broadcast_to([128, 32, FDH]), ALU.mult),
                            waits=[(sv, None)], incs=[(sv, 1)])
                        ctab = tabs[ct_t][:, w * 256:(w + 1) * 256]
                        stab = tabs[st_t][:, w * 256:(w + 1) * 256]
                        p.emit("vector", lambda v, src_tm=src_tm, ctab=ctab: v.tensor_tensor(
                            src_tm.rearrange("p (s h e) -> p s h e", s=4, h=8),
                            b_sqB.rearrange("p (s h e) -> p s h e", s=4, h=8),
                            ctab.rearrange("p (s e) -> p s e", s=4).unsqueeze(2).broadcast_to(
                                [128, 4, 8, FDH]), ALU.mult),
                            waits=[(sv, None)], incs=[(sv, 1)])
                        for half in range(2):
                            p.emit("vector", lambda v, half=half, stab=stab: v.tensor_tensor(
                                b_rotB.rearrange("p (s h x e) -> p s h x e", s=4, h=8, x=2)[:, :, :, half, :],
                                b_sqB.rearrange("p (s h x e) -> p s h x e", s=4, h=8, x=2)[:, :, :, 1 - half, :],
                                stab.rearrange("p (s x e) -> p s x e", s=4, x=2)[:, :, half, :]
                                .unsqueeze(2).broadcast_to([128, 4, 8, 32]), ALU.mult),
                                waits=[(sv, None)], incs=[(sv, 1)])
                        p.emit("vector", lambda v, src_tm=src_tm: v.tensor_tensor(
                            b_sqB, src_tm, b_rotB, ALU.add),
                            waits=[(sv, None)], incs=[(sv, 1)])
                        fin_v = sv.n
                        for ts_ in range(4):
                            po = which * 512
                            def trq(t, ts_=ts_, po=po):
                                last = None
                                for ic in range(4):
                                    last = t.transpose(ps3b[:, po + ic * 128: po + (ic + 1) * 128],
                                                       b_sqB[:, ts_ * 512 + ic * 128: ts_ * 512 + (ic + 1) * 128],
                                                       self.identb)
                                return last
                            p.emit("tensor", trq, waits=[(sv, fin_v)] + self.guard(("ps3b", which)),
                                   incs=[(sm, 1)])
                            p.emit("vector", lambda v, ts_=ts_, dstfm=dstfm, po=po: v.tensor_copy(
                                dstfm.rearrange("p (ic t2) -> p ic t2", ic=4)[:, :, ts_ * 128:(ts_ + 1) * 128],
                                ps3b[:, po:po + 512].rearrange("p (ic t2) -> p ic t2", ic=4)),
                                waits=[(sm, None)], incs=[(sv, 1)])
                            self.set_guard(("ps3b", which), sv)
                        if which == 0:
                            self.set_guard(("q_tm",), sm)
                        else:
                            self.set_guard(("k_tm",), sm)
                        self.set_guard(("b_sq",), sm)
                        self.set_guard(("s32",), sv)
                    qk_done[w] = sv.n

            def attn(w, sp, offs):
                    sslot = {0: self.pw[1].ap()[:, 0:512], 1: self.pw[1].ap()[:, 512:1024]}
                    mask_done = {}

                    def emit_sblock(h):
                        ab_ = attn2[h % 2]
                        r0 = 64 * (h % 2); ic = h // 2
                        for ei, (kt, rw, ts_, tq0, span, mk, mko) in enumerate(sp):
                            slot = sslot[h % 2][:, 0:span]
                            kf_src = kfm[rw]
                            p.emit("tensor", lambda t, kf_src=kf_src, r0=r0, ic=ic, ts_=ts_, tq0=tq0, span=span, slot=slot: t.matmul(
                                slot, kf_src[r0:r0 + 64, ic * 512 + ts_ * 128: ic * 512 + (ts_ + 1) * 128],
                                qfm[r0:r0 + 64, ic * 512 + tq0: ic * 512 + tq0 + span],
                                start=True, stop=True),
                                waits=[(sv, qk_done[w])] + self.guard(("sslot", h % 2)),
                                incs=[(sm, 1)])
                            oo = offs[kt]
                            p.emit("scalar", lambda sc, slot=slot, ab_=ab_, oo=oo, span=span: sc.activation(
                                ab_[:, oo:oo + span], slot, AF.Exp, scale=8.0),
                                waits=[(sm, None)] + self.guard(("attn2", h % 2)), incs=[(sa, 1)])
                            self.set_guard(("sslot", h % 2), sa)
                            msk = self.mask_low if mk == "low" else self.mask_up
                            p.emit("vector", lambda v, ab_=ab_, oo=oo, mko=mko, msk=msk: v.tensor_tensor(
                                ab_[:, oo + mko: oo + mko + 128],
                                ab_[:, oo + mko: oo + mko + 128], msk, ALU.mult),
                                waits=[(sa, None)], incs=[(sv, 1)])
                        mask_done[h] = sv.n

                    def emit_oblock(h):
                        ab_ = attn2[h % 2]
                        for mi in range(4):
                            oslot = self.pw[0].ap()[:, (mi % 2) * 512:(mi % 2) * 512 + 65]
                            kts = [e for e in sp if e[3] <= mi * 128 < e[3] + e[4]]
                            def omm(t, kts=kts, mi=mi, oslot=oslot, ab_=ab_, h=h, offs=dict(offs)):
                                last = None
                                for j, (kt, rw, ts_, tq0, span, mk, mko) in enumerate(kts):
                                    last = t.matmul(oslot,
                                                    ab_[:, offs[kt] + mi * 128 - tq0: offs[kt] + mi * 128 - tq0 + 128],
                                                    vtl[rw].rearrange("p (s h2 e) -> p s h2 e", s=4, h2=8)[:, ts_, h, :],
                                                    start=(j == 0), stop=(j == len(kts) - 1))
                                return last
                            p.emit("tensor", omm, waits=[(sv, mask_done[h])] + self.guard(("oslot", mi % 2)),
                                   incs=[(sm, 1)])
                            p.emit("vector", lambda v, oslot=oslot, mi=mi: v.reciprocal(
                                rcp4[:, mi:mi + 1], oslot[:, 64:65]),
                                waits=[(sm, None)], incs=[(sv, 1)])
                            p.emit("vector", lambda v, oslot=oslot, mi=mi, h=h: v.tensor_single_scalar(
                                otm[:, mi * 512 + h * 64: mi * 512 + (h + 1) * 64],
                                oslot[:, 0:64], rcp4[:, mi:mi + 1], ALU.mult),
                                waits=[(sv, None)], incs=[(sv, 1)])
                            self.set_guard(("oslot", mi % 2), sv)
                        self.set_guard(("attn2", h % 2), sm)

                    for h in range(8):
                        emit_sblock(h)
                        if h > 0:
                            emit_oblock(h - 1)
                    emit_oblock(7)
                    o_done = sv.n
                    for mi in range(4):
                        po = 0
                        def tro(t, mi=mi, po=po):
                            last = None
                            for ic in range(4):
                                last = t.transpose(ps3b[:, po + ic * 128: po + (ic + 1) * 128],
                                                   otm[:, mi * 512 + ic * 128: mi * 512 + (ic + 1) * 128],
                                                   self.identb)
                            return last
                        p.emit("tensor", tro, waits=[(sv, o_done)] + self.guard(("ps3b", 0)),
                               incs=[(sm, 1)])
                        p.emit("vector", lambda v, mi=mi, po=po: v.tensor_copy(
                            ofm.rearrange("p (ic t2) -> p ic t2", ic=4)[:, :, mi * 128:(mi + 1) * 128],
                            ps3b[:, po:po + 512].rearrange("p (ic t2) -> p ic t2", ic=4)),
                            waits=[(sm, None)], incs=[(sv, 1)])
                        self.set_guard(("ps3b", 0), sv)
                    ofm_done = sv.n
                    for ctO in range(8):
                        wslice = self.pw[1].ap()[:, (ctO % 2) * 512:(ctO % 2) * 512 + 512]
                        def womm(t, ctO=ctO, wslice=wslice):
                            last = None
                            for icg in range(4):
                                last = t.matmul(wslice,
                                                wo_sb2[:, icg * C + ctO * 128: icg * C + ctO * 128 + 128],
                                                ofm[:, icg * 512:(icg + 1) * 512],
                                                start=(icg == 0), stop=(icg == 3))
                            return last
                        p.emit("tensor", womm,
                               waits=[(sv, ofm_done), (sgw, gw_t)] + self.guard(("sslot", ctO % 2)),
                               incs=[(sm, 1)])
                        p.emit("vector", lambda v, ctO=ctO, w=w, wslice=wslice: v.tensor_tensor(
                            xf[:, ctO * (T + 1) + 1 + w * 512: ctO * (T + 1) + 1 + (w + 1) * 512],
                            xf[:, ctO * (T + 1) + 1 + w * 512: ctO * (T + 1) + 1 + (w + 1) * 512],
                            wslice, ALU.add),
                            waits=[(sm, None)], incs=[(sv, 1)])
                        self.set_guard(("sslot", ctO % 2), sv)
                    self.set_guard(("vtlrd", (w + 2) % 3), sm)
                    self.set_guard(("b_sq",), sm)

            state = {}
            state[0] = front(0)
            chains(0)
            for w in range(NW):
                if w + 1 < NW:
                    state[w + 1] = front(w + 1)
                attn(w, *state[w])
                if w + 1 < NW:
                    chains(w + 1)

        def geglu_stage(l):
            self.ab.reset(m4a)
            ge_m = sm.n; ge_v = sv.n
            z1 = self.ab.alloc(8 * 512 * (T // 512))
            zbuf = self.ab.alloc(8 * 512)
            zqb = self.ab.alloc(8 * 512)
            ggb = [self.ab.alloc(512) for _ in range(2)]
            h4b = [self.ab.alloc(512) for _ in range(2)]
            w1q = [self.ab.alloc(8 * 1536) for _ in range(2)]
            sh4st = [self.sem(f"p4_h4st{l}{i}") for i in range(2)]
            NT4 = T // 512
            z1_t = [None] * NT4
            pw0row = self.pw[0].ap()[0:1, 512:1024]
            pw1row = self.pw[1].ap()[0:1, 512:1024]
            pw0bc = self.pw[0].ap()[:, 0:512]
            pw1bc = self.pw[1].ap()[:, 0:512]
            def emit_ln(tt4):
                for ct in range(8):
                    sh = 1 if ct >= 4 else 0
                    w_ = [(sv, None)] + self.guard(("zbuf",))
                    if tt4 == 0 and ct == 0:
                        w_.append((sm, ge_m))
                    p.emit("vector", lambda v, ct=ct, tt4=tt4, sh=sh: v.tensor_copy(
                        zbuf[:, ct * 512:(ct + 1) * 512],
                        xf[:, ct * (T + 1) + 1 + tt4 * 512 - sh: ct * (T + 1) + 1 + (tt4 + 1) * 512 - sh]),
                        waits=w_, incs=[(sv, 1)])
                    p.emit("scalar", lambda sc, ct=ct: sc.activation(
                        zqb[:, ct * 512:(ct + 1) * 512], zbuf[:, ct * 512:(ct + 1) * 512], AF.Square),
                        waits=[(sv, None)] + self.guard(("zqb",)), incs=[(sa, 1)])
                zd = sv.n; zqd = sa.n
                def statmm(t, out, buf):
                    last = None
                    for k in range(8):
                        last = t.matmul(out, self.onescol[:, 0:1],
                                        buf[:, k * 512:(k + 1) * 512], start=(k == 0), stop=(k == 7))
                    return last
                p.emit("tensor", lambda t: statmm(t, pw0row, zbuf),
                       waits=[(sv, zd)] + self.guard(("geps", 0)), incs=[(sm, 1)])
                p.emit("tensor", lambda t: statmm(t, pw1row, zqb),
                       waits=[(sa, zqd)] + self.guard(("geps", 1)), incs=[(sm, 1)])
                mm_t = sm.n
                self.set_guard(("zbuf",), sm)
                self.set_guard(("zqb",), sm)
                p.emit("vector", lambda v: v.tensor_scalar_mul(r_mu[0:1, :], pw0row, 1.0 / C),
                       waits=[(sm, mm_t), (sv, None)], incs=[(sv, 1)])
                p.emit("vector", lambda v: v.tensor_scalar_mul(r_e2[0:1, :], pw1row, 1.0 / C),
                       waits=[(sv, None)], incs=[(sv, 1)])
                self.set_guard(("geps", 0), sv)
                self.set_guard(("geps", 1), sv)
                p.emit("vector", lambda v: v.tensor_tensor(r_ms[0:1, :], r_mu[0:1, :], r_mu[0:1, :], ALU.mult),
                       waits=[(sv, None)], incs=[(sv, 1)])
                p.emit("vector", lambda v: v.tensor_tensor(r_va[0:1, :], r_e2[0:1, :], r_ms[0:1, :], ALU.subtract),
                       waits=[(sv, None)], incs=[(sv, 1)])
                p.emit("scalar", lambda sc: sc.activation(r_va[0:1, :], r_va[0:1, :], AF.Ln,
                                                          bias=self.cc[LN_EPS][0:1, :]),
                       waits=[(sv, None)], incs=[(sa, 1)])
                p.emit("scalar", lambda sc: sc.activation(r_rs[0:1, :], r_va[0:1, :], AF.Exp, scale=-0.5),
                       waits=[(sa, None)], incs=[(sa, 1)])
                p.emit("vector", lambda v: v.tensor_tensor(r_msb[0:1, :], r_mu[0:1, :], r_rs[0:1, :], ALU.mult),
                       waits=[(sa, None), (sv, None)], incs=[(sv, 1)])
                rows_done_v = sv.n; rows_done_a = sa.n
                p.emit("tensor", lambda t: t.matmul(pw0bc, onesrow[0:1, :], r_rs[0:1, :],
                                                    start=True, stop=True),
                       waits=[(sa, rows_done_a), (sv, rows_done_v)] + self.guard(("geps", 2)),
                       incs=[(sm, 1)])
                p.emit("tensor", lambda t: t.matmul(pw1bc, onesrow[0:1, :], r_msb[0:1, :],
                                                    start=True, stop=True),
                       waits=[(sv, rows_done_v)] + self.guard(("geps", 3)), incs=[(sm, 1)])
                bc_t = sm.n
                for ct in range(8):
                    sh = 1 if ct >= 4 else 0
                    p.emit("vector", lambda v, ct=ct, tt4=tt4, sh=sh: v.tensor_tensor(
                        scr1, xf[:, ct * (T + 1) + 1 + tt4 * 512 - sh: ct * (T + 1) + 1 + (tt4 + 1) * 512 - sh],
                        pw0bc, ALU.mult),
                        waits=[(sm, bc_t), (sv, None)], incs=[(sv, 1)])
                    p.emit("vector", lambda v: v.tensor_tensor(scr2, scr1, pw1bc, ALU.subtract),
                           waits=[(sv, None)], incs=[(sv, 1)])
                    p.emit("vector", lambda v, ct=ct, tt4=tt4: v.tensor_scalar(
                        z1[:, ct * T + tt4 * 512: ct * T + (tt4 + 1) * 512], scr2,
                        lngv[:, ct:ct + 1], lnbv[:, ct:ct + 1], ALU.mult, ALU.add),
                        waits=[(sv, None), (sld, None)], incs=[(sv, 1)])
                self.set_guard(("geps", 2), sv)
                self.set_guard(("geps", 3), sv)
                z1_t[tt4] = sv.n
            quarters = [(0, 6), (6, 5), (11, 6), (17, 5)]
            q_cons = [None] * 6
            stc = [0]

            def emit_q_load(qi):
                p0, np_ = quarters[qi]
                wb = w1q[qi % 2]
                qw = np_ * 256
                ldw = [(sm, ge_m), (sv, ge_v)] if qi < 2 else [(sm, q_cons[qi - 2])]
                p.emit("gpsimd", lambda gp, l=l, p0=p0, qw=qw, wb=wb: gp.dma_start(
                    out=wb.rearrange("p (g n) -> p g n", g=8)[:, :, 0:qw],
                    in_=d["w1p"][l][:, p0 * 256: p0 * 256 + qw].rearrange(
                        "(g p) n -> p g n", p=128)),
                    waits=ldw, incs=[(sg1[qi % 2], 16)])

            def emit_gemm_tile(qi, tt4):
                p0, np_ = quarters[qi]
                wb = w1q[qi % 2]
                for pi in range(np_):
                    step = stc[0]
                    i_glob = p0 + pi
                    pa = self.ps[2 * (step % 2)]
                    pg = self.ps[2 * (step % 2) + 1]
                    for agi, pb in ((0, pa), (1, pg)):
                        def w1mm(t, pi=pi, agi=agi, tt4=tt4, pb=pb, wb=wb):
                            last = None
                            for k in range(8):
                                last = t.matmul(pb.ap()[:],
                                                wb[:, k * 1536 + (2 * pi + agi) * 128: k * 1536 + (2 * pi + agi) * 128 + 128],
                                                z1[:, k * T + tt4 * 512: k * T + (tt4 + 1) * 512],
                                                start=(k == 0), stop=(k == 7))
                            return last
                        p.emit("tensor", w1mm,
                               waits=[(sv, z1_t[tt4]), (sg1[qi % 2], sg1[qi % 2].n)] +
                                     self.guard(("gep", 2 * (step % 2) + agi)),
                               incs=[(sm, 1)])
                    w1_t = sm.n
                    p.emit("scalar", lambda sc, i_glob=i_glob, pg=pg, gb=ggb[step % 2]: sc.activation(
                        gb, pg.ap()[:], AF.Gelu, bias=b1v[:, 2 * i_glob + 1:2 * i_glob + 2]),
                        waits=[(sm, w1_t)] + self.guard(("ggb", step % 2)), incs=[(sa, 1)])
                    self.set_guard(("gep", 2 * (step % 2) + 1), sa)
                    hb = h4b[step % 2]
                    p.emit("vector", lambda v, hb=hb, i_glob=i_glob, pa=pa, gb=ggb[step % 2]: v.scalar_tensor_tensor(
                        hb, pa.ap()[:],
                        b1v[:, 2 * i_glob:2 * i_glob + 1], gb,
                        ALU.add, ALU.mult),
                        waits=[(sa, None), (sh4st[step % 2], sh4st[step % 2].n)], incs=[(sv, 1)])
                    self.set_guard(("gep", 2 * (step % 2)), sv)
                    self.set_guard(("ggb", step % 2), sv)
                    p.emit("sync", lambda sy, i_glob=i_glob, tt4=tt4, hb=hb: sy.dma_start(
                        out=d["h4_d"][i_glob * 128:(i_glob + 1) * 128, tt4 * 512:(tt4 + 1) * 512],
                        in_=hb), waits=[(sv, None)], incs=[(sh4st[step % 2], 16)])
                    stc[0] += 1

            emit_q_load(0)
            emit_q_load(1)
            emit_ln(0)
            emit_ln(1)
            emit_gemm_tile(0, 0)
            emit_ln(2)
            emit_gemm_tile(0, 1)
            emit_ln(3)
            emit_gemm_tile(0, 2)
            emit_gemm_tile(0, 3)
            q_cons[0] = sm.n
            for qi in range(1, 4):
                if qi >= 2:
                    emit_q_load(qi)
                for tt4 in range(NT4):
                    emit_gemm_tile(qi, tt4)
                q_cons[qi] = sm.n
            h4st_t = [sh4st[0].n, sh4st[1].n]
            self.ab.reset(m4a)
            w2h = [self.ab.alloc(11 * C) for _ in range(2)]
            h4t = [self.ab.alloc(11 * 512) for _ in range(2)]
            for half in range(2):
                wb2 = w2h[half % 2]
                p.emit("gpsimd", lambda gp, l=l, half=half, wb2=wb2: gp.dma_start(
                    out=wb2.rearrange("p (g n) -> p g n", g=11),
                    in_=d["w2p"][l][half * 1408:(half + 1) * 1408, :].rearrange(
                        "(g p) n -> p g n", p=128)),
                    waits=[(sm, None), (sv, None), (sh4st[0], h4st_t[0]), (sh4st[1], h4st_t[1])],
                    incs=[(sg2[half], 16)])
                for tt4 in range(T // 512):
                    hs = h4t[tt4 % 2]
                    hld = sh4[tt4 % 2]
                    p.emit("sync", lambda sy, half=half, tt4=tt4, hs=hs: sy.dma_start(
                        out=hs.rearrange("p (g n) -> p g n", g=11),
                        in_=d["h4_d"][half * 1408:(half + 1) * 1408, tt4 * 512:(tt4 + 1) * 512].rearrange(
                            "(g p) n -> p g n", p=128)),
                        waits=[(sh4st[0], h4st_t[0]), (sh4st[1], h4st_t[1])] + self.guard(("h4t", tt4 % 2)),
                        incs=[(hld, 16)])
                    for ctO in range(8):
                        pb = self.ps[2 + ctO % 2]
                        def w2mm(t, hs=hs, ctO=ctO, pb=pb, wb2=wb2):
                            last = None
                            for k in range(11):
                                last = t.matmul(pb.ap()[:],
                                                wb2[:, k * C + ctO * 128: k * C + ctO * 128 + 128],
                                                hs[:, k * 512:(k + 1) * 512],
                                                start=(k == 0), stop=(k == 10))
                            return last
                        p.emit("tensor", w2mm, waits=[(hld, hld.n), (sg2[half], sg2[half].n)] +
                               self.guard(("gep", 2 + ctO % 2)), incs=[(sm, 1)])
                        if half == 0:
                            p.emit("vector", lambda v, ctO=ctO, tt4=tt4, pb=pb: v.tensor_tensor(
                                xf[:, ctO * (T + 1) + 1 + tt4 * 512: ctO * (T + 1) + 1 + (tt4 + 1) * 512],
                                xf[:, ctO * (T + 1) + 1 + tt4 * 512: ctO * (T + 1) + 1 + (tt4 + 1) * 512],
                                pb.ap()[:], ALU.add),
                                waits=[(sm, None)], incs=[(sv, 1)])
                        else:
                            p.emit("vector", lambda v, ctO=ctO, tt4=tt4, pb=pb: v.scalar_tensor_tensor(
                                xf[:, ctO * (T + 1) + 1 + tt4 * 512: ctO * (T + 1) + 1 + (tt4 + 1) * 512],
                                pb.ap()[:], b2v[:, ctO:ctO + 1],
                                xf[:, ctO * (T + 1) + 1 + tt4 * 512: ctO * (T + 1) + 1 + (tt4 + 1) * 512],
                                ALU.add, ALU.add),
                                waits=[(sm, None)], incs=[(sv, 1)])
                        self.set_guard(("gep", 2 + ctO % 2), sv)
                    self.set_guard(("h4t", tt4 % 2), sm)
            self.set_guard(("smallvec",), sa, sv)

        for l in range(2):
            attn_stage(l)
            geglu_stage(l)
        self.p4_done = [(sv, sv.n), (sm, sm.n), (sa, sa.n), (sl, sl.n), (sst4, sst4.n)]
        self.af.reset(mf0 + 8 * (T + 1))

    def phase5(self):
        p, T, d = self.p, self.T, self.dram
        sm = self.sem("p5_mm"); sv = self.sem("p5_dv"); sa = self.sem("p5_act")
        s_ld = [self.sem(f"p5_ld{i}") for i in range(2)]
        s_st = [self.sem(f"p5_st{i}") for i in range(2)]
        mb = self.ab.mark()
        gfb = self.ab.alloc(8 * T)
        gtt = self.ab.alloc(C)
        mf = self.af.mark()
        thb = self.af.alloc(C)
        xs_t = [self.af.alloc(C) for _ in range(2)]
        ss_t = [self.af.alloc(C) for _ in range(2)]
        tsc = thb
        thb2 = [thb, ss_t[0]]
        g_half = [0, 0]
        for hf in range(T // C):
            for ct in range(8):
                th = thb2[ct % 2]
                p.emit("scalar", lambda sc, ct=ct, hf=hf, th=th: sc.activation(
                    th, self.xf[:, ct * (T + 1) + 1 + hf * C: ct * (T + 1) + 1 + (hf + 1) * C],
                    AF.Tanh, scale=0.5),
                    waits=(list(self.p4_done) + [(sa, None), (sv, None)]) if (ct == 0 and hf == 0)
                    else ([(sv, sv.n - 1)] if sv.n else []),
                    incs=[(sa, 1)])
                p.emit("vector", lambda v, ct=ct, hf=hf, th=th: v.scalar_tensor_tensor(
                    gfb[:, ct * T + hf * C:(ct) * T + (hf + 1) * C], th, 1.0,
                    self.xf[:, ct * (T + 1) + 1 + hf * C: ct * (T + 1) + 1 + (hf + 1) * C],
                    ALU.add, ALU.mult),
                    waits=[(sa, None)], incs=[(sv, 1)])
            g_half[hf] = sv.n
        g_done = sv.n
        psb = [self.ps[2].ap().bitcast(BF16), self.ps[3].ap().bitcast(BF16)]
        gtt2 = [gtt, self.ab.alloc(C)]
        for tt in range(self.NTT):
            pb = psb[tt % 2]
            def trg(t, tt=tt, pb=pb):
                last = None
                for ct in range(8):
                    last = t.transpose(pb[:, ct * 128:(ct + 1) * 128],
                                       gfb[:, ct * T + tt * 128: ct * T + (tt + 1) * 128],
                                       self.identb)
                return last
            w_ = [(sv, g_half[tt // 8])] + self.guard(("p5ps", tt % 2))
            p.emit("tensor", trg, waits=w_, incs=[(sm, 1)])
            gt = gtt2[tt % 2]
            p.emit("vector", lambda v, pb=pb, gt=gt: v.tensor_copy(gt, pb[:, 0:1024]),
                   waits=[(sm, None)] + self.guard(("p5gt", tt % 2)), incs=[(sv, 1)])
            self.set_guard(("p5ps", tt % 2), sv)
            gt_done = sv.n
            xsl = xs_t[tt % 2]; ssl = ss_t[tt % 2]; osl = xs_t[tt % 2]
            ldw = [(s_st[tt % 2], s_st[tt % 2].n)]
            p.emit("sync", lambda sy, tt=tt, xsl=xsl: sy.dma_start(
                out=xsl, in_=d["x"][tt * 128:(tt + 1) * 128, :]),
                waits=ldw + ([(sv, g_done)] if tt < 2 else []), incs=[(s_ld[tt % 2], 16)])
            p.emit("sync", lambda sy, tt=tt, ssl=ssl: sy.dma_start(
                out=ssl, in_=d["s_d"][tt * 128:(tt + 1) * 128, :]),
                waits=[(sv, g_done)] if tt < 2 else [], incs=[(s_ld[tt % 2], 16)])
            p.emit("vector", lambda v, ssl=ssl, gt=gt: v.tensor_tensor(tsc, ssl, gt, ALU.mult),
                   waits=[(s_ld[tt % 2], s_ld[tt % 2].n), (sv, gt_done), (sv, None)], incs=[(sv, 1)])
            p.emit("vector", lambda v, xsl=xsl, osl=osl: v.scalar_tensor_tensor(
                osl, tsc, 0.5, xsl, ALU.mult, ALU.add),
                waits=[(sv, None)], incs=[(sv, 1)])
            self.set_guard(("p5gt", tt % 2), sv)
            p.emit("sync", lambda sy, tt=tt, osl=osl: sy.dma_start(
                out=d["out"][tt * 128:(tt + 1) * 128, :], in_=osl),
                waits=[(sv, None)], incs=[(s_st[tt % 2], 16)])
        self.af.reset(mf); self.ab.reset(mb)

    def finalize(self):
        nc = self.nc
        p = self.p
        waits = [(s, s.n) for s in self.sems.values() if s.n > 0]
        for i in range(0, len(waits), 8):
            p.emit("sync", lambda sy: None, waits=waits[i:i + 8])
        with nc.Block() as block:
            @block.tensor
            def _(t):
                p.replay("tensor", t)

            @block.vector
            def _(v):
                p.replay("vector", v)

            @block.scalar
            def _(sc):
                p.replay("scalar", sc)

            @block.gpsimd
            def _(gp):
                p.replay("gpsimd", gp)

            @block.sync
            def _(sy):
                p.replay("sync", sy)
        self.es.close()



def host_consts(T):
    out = {}
    out["ident"] = np.eye(128, dtype=np.float32)
    m = (np.arange(128)[:, None] < np.arange(128)[None, :]).astype(np.float32)
    out["maskA"] = np.tile(m, (1, 4))
    r = np.arange(128)
    out["mask_low"] = (r[None, :] <= r[:, None]).astype(np.float32)
    out["mask_up"] = (r[None, :] >= r[:, None]).astype(np.float32)
    d2 = np.arange(0, FDH, 2, dtype=np.float64)
    inv_freq = 1.0 / 10000.0 ** (d2 / FDH)
    t = np.arange(T, dtype=np.float64)
    freqs = t[:, None] * inv_freq[None, :]
    freqs = np.concatenate([freqs, freqs], axis=-1)
    base = (d2 + 0.4 * FDH) / (1.4 * FDH)
    power = (t - T // 2) / (WIN // 2)
    scale = base[None, :] ** power[:, None]
    scale = np.concatenate([scale, scale], axis=-1)
    cos, sin = np.cos(freqs), np.sin(freqs)
    sgn = np.concatenate([-np.ones(FDH // 2), np.ones(FDH // 2)])
    out["ct_q"] = (cos * scale).astype(np.float32)
    out["st_q"] = (sin * scale * sgn[None, :]).astype(np.float32)
    out["ct_k"] = (cos / scale).astype(np.float32)
    out["st_k"] = (sin / scale * sgn[None, :]).astype(np.float32)
    return out


def vec_fm(v):
    return np.ascontiguousarray(v.reshape(-1).reshape(8, 128).T).astype(np.float32)


def prep_core_inputs(inputs, b, T):
    g = lambda k: np.asarray(inputs[k])
    m = {}
    m["x"] = np.ascontiguousarray(g("L0_attn_x")[b, :T, :]).astype(np.float32, copy=False)
    m["tmx"] = vec_fm(g("L0_attn_time_maa_x"))
    m["tmw"] = vec_fm(g("L0_attn_time_maa_w"))
    m["tmk"] = vec_fm(g("L0_attn_time_maa_k"))
    m["tmv"] = vec_fm(g("L0_attn_time_maa_v"))
    m["tmr"] = vec_fm(g("L0_attn_time_maa_r"))
    m["tmg"] = vec_fm(g("L0_attn_time_maa_g"))
    m["tdecay"] = vec_fm(g("L0_attn_time_decay"))
    m["u_"] = vec_fm(g("L0_attn_time_faaaa"))
    m["lnw_"] = vec_fm(g("L0_attn_ln_w"))
    m["lnb_"] = vec_fm(g("L0_attn_ln_b"))
    w1 = g("L0_attn_time_maa_w1")
    m["maa_w1"] = np.ascontiguousarray(w1)
    w2 = g("L0_attn_time_maa_w2")
    m["maa_w2"] = np.ascontiguousarray(w2.transpose(1, 0, 2).reshape(32, 5 * C))
    m["td_w1"] = np.ascontiguousarray(g("L0_attn_time_decay_w1"))
    m["td_w2"] = np.ascontiguousarray(g("L0_attn_time_decay_w2"))
    m["w_r"] = np.ascontiguousarray(g("L0_attn_W_r"))
    m["w_k"] = np.ascontiguousarray(g("L0_attn_W_k"))
    m["w_v"] = np.ascontiguousarray(g("L0_attn_W_v"))
    m["w_g"] = np.ascontiguousarray(g("L0_attn_W_g"))
    m["w_o"] = np.ascontiguousarray(g("L0_attn_W_o"))
    m["lnw_rep"] = np.ascontiguousarray(np.broadcast_to(g("L0_attn_ln_w")[None, :], (128, C))).astype(np.float32)
    m["lnb_rep"] = np.ascontiguousarray(np.broadcast_to(g("L0_attn_ln_b")[None, :], (128, C))).astype(np.float32)
    m["wqkv"] = np.ascontiguousarray(g("L0_ffn_wqkv"))
    m["qscale_rep"] = np.ascontiguousarray(np.broadcast_to(g("L0_ffn_q_scale")[:, None, :], (2, 128, FDH))).astype(np.float32)
    m["kscale_rep"] = np.ascontiguousarray(np.broadcast_to(g("L0_ffn_k_scale")[:, None, :], (2, 128, FDH))).astype(np.float32)
    m["wo"] = np.ascontiguousarray(g("L0_ffn_wo"))
    m["lng"] = np.stack([vec_fm(g("L0_ffn_ln_g")[l]) for l in range(2)])
    m["lnb2"] = np.stack([vec_fm(g("L0_ffn_ln_b")[l]) for l in range(2)])
    m["b2"] = np.stack([vec_fm(g("L0_ffn_b2")[l]) for l in range(2)])
    w1f = g("L0_ffn_w1")
    b1f = g("L0_ffn_b1")
    w1p = np.zeros((2, C, 2 * HIDP), dtype=np.float32)
    b1p = np.zeros((2, 128, 44), dtype=np.float32)
    for l in range(2):
        a = w1f[l][:, :HID]; gg = w1f[l][:, HID:]
        ba = b1f[l][:HID]; bg = b1f[l][HID:]
        for i in range(22):
            lo, hi = i * 128, min((i + 1) * 128, HID)
            n = hi - lo
            if n > 0:
                w1p[l][:, (2 * i) * 128:(2 * i) * 128 + n] = a[:, lo:hi]
                w1p[l][:, (2 * i + 1) * 128:(2 * i + 1) * 128 + n] = gg[:, lo:hi]
                b1p[l][:n, 2 * i] = ba[lo:hi]
                b1p[l][:n, 2 * i + 1] = bg[lo:hi]
    m["w1p"] = w1p
    m["b1p"] = b1p
    w2f = g("L0_ffn_w2")
    w2p = np.zeros((2, HIDP, C), dtype=np.float32)
    w2p[:, :HID, :] = w2f
    m["w2p"] = w2p
    m.update(host_consts(T))
    return m


def build_nc(T=2048, dbg=(), phases=6):
    nc = bass.Bass("TRN2", target_bir_lowering=False, debug=False)
    b = Builder(nc, T, dbg)
    b.setup()
    b.load_consts()
    b.load_rwkv_weights()
    b.phase0()
    if phases > 1:
        b.phase1()
    if phases > 4:
        b.prefetch_ffn0()
    if phases > 2:
        b.phase2(parts=globals().get('_P2_PARTS', 31))
    if phases > 3:
        b.phase3()
    if phases > 4:
        b.phase4()
    if phases > 5:
        b.phase5()
    b.finalize()
    return b, nc


def kernel(**inputs):
    T = 2048
    bld, nc = build_nc(T)
    in_maps = [prep_core_inputs(inputs, b, T) for b in range(8)]
    res = run_bass_kernel_spmd(nc, in_maps, core_ids=list(range(8)))
    out = np.stack([res.results[i]["out"] for i in range(8)], axis=0)
    return out


# version 15
# speedup vs baseline: 1.0201x; 1.0201x over previous
import numpy as np
from contextlib import ExitStack

import concourse.bass as bass
import concourse.mybir as mybir
from concourse.bass_utils import run_bass_kernel_spmd

F32 = mybir.dt.float32
BF16 = mybir.dt.bfloat16
AF = mybir.ActivationFunctionType
ALU = mybir.AluOpType
AX = mybir.AxisListType

ENGINES = ("tensor", "vector", "scalar", "gpsimd", "sync")

C = 1024
H = 16
DH = 64
L = 128
FH = 8
FDH = 64
WIN = 512
INNER = 512
HID = 2730
HIDP = 2816
GN_EPS = 1e-5 * 64.0
LN_EPS = 1e-5


class Sem:
    def __init__(self, handle, name):
        self.h = handle
        self.name = name
        self.n = 0


class Prog:

    def __init__(self):
        self.ops = {e: [] for e in ENGINES}

    def emit(self, engine, fn, waits=(), incs=()):
        waits_f = [(s.h, (s.n if v is None else v)) for (s, v) in waits]
        incs_f = [(s.h, k) for (s, k) in incs]
        for s, k in incs:
            s.n += k
        self.ops[engine].append((waits_f, fn, incs_f))

    def replay(self, engine_name, eng):
        waited = {}
        for waits_f, fn, incs_f in self.ops[engine_name]:
            for h, v in waits_f:
                if v > 0 and waited.get(id(h), -1) < v:
                    eng.wait_ge(h, v)
                    waited[id(h)] = v
            inst = fn(eng)
            for h, k in incs_f:
                assert inst is not None, "op with incs must return an instruction"
                inst.then_inc(h, k)


class Arena:

    def __init__(self, t, cols, dtype):
        self.t = t
        self.cols = cols
        self.dtype = dtype
        self.off = 0
        self.align = 8 if dtype == BF16 else 4
        self.peak = 0

    def alloc(self, w):
        o = self.off
        w_al = -(-w // self.align) * self.align
        assert o + w_al <= self.cols, f"arena overflow {o}+{w} > {self.cols}"
        self.off = o + w_al
        self.peak = max(self.peak, self.off)
        return self.t.ap()[:, o:o + w]

    def mark(self):
        return self.off

    def reset(self, m):
        self.off = m


def ceil_div(a, b):
    return -(-a // b)


class Builder:
    def __init__(self, nc, T, dbg=()):
        self.nc = nc
        self.T = T
        self.NTT = T // 128
        self.NCHK = T // L
        self.NW = T // WIN
        self.dbg = set(dbg)
        self.p = Prog()
        self.es = ExitStack()
        self.sems = {}
        self.dram = {}
        self.dbg_outs = {}

    def set_guard(self, key, *sems):
        if not hasattr(self, "guards"):
            self.guards = {}
        self.guards[key] = [(s, s.n) for s in sems]

    def guard(self, key):
        if not hasattr(self, "guards"):
            self.guards = {}
        return list(self.guards.get(key, []))

    def sem(self, name):
        if name not in self.sems:
            self.sems[name] = Sem(self.es.enter_context(self.nc.semaphore(name)), name)
        return self.sems[name]

    def setup(self):
        nc, T = self.nc, self.T
        def din(name, shape, dt=F32):
            self.dram[name] = nc.dram_tensor(name, list(shape), dt, kind="ExternalInput").ap()
        din("x", (T, C))
        for nm in ("tmx", "tmw", "tmk", "tmv", "tmr", "tmg", "tdecay", "u_", "lnw_", "lnb_"):
            din(nm, (128, 8))
        din("maa_w1", (C, 160))
        din("maa_w2", (32, 5 * C))
        din("td_w1", (C, 64))
        din("td_w2", (64, C))
        din("w_r", (C, C)); din("w_k", (C, C)); din("w_v", (C, C)); din("w_g", (C, C)); din("w_o", (C, C))
        din("lnw_rep", (128, C)); din("lnb_rep", (128, C))
        din("wqkv", (2, C, 3 * INNER))
        din("qscale_rep", (2, 128, FDH)); din("kscale_rep", (2, 128, FDH))
        din("wo", (2, INNER, C))
        din("lng", (2, 128, 8)); din("lnb2", (2, 128, 8)); din("b2", (2, 128, 8))
        din("w1p", (2, C, 2 * HIDP))
        din("b1p", (2, 128, 44))
        din("w2p", (2, HIDP, C))
        din("ident", (128, 128))
        din("maskA", (128, 512))
        din("mask_low", (128, 128))
        din("mask_up", (128, 128))
        din("ct_q", (T, 64)); din("st_q", (T, 64)); din("ct_k", (T, 64)); din("st_k", (T, 64))
        def dscr(name, shape, dt):
            self.dram[name] = nc.dram_tensor(name, list(shape), dt, kind="Internal").ap()
        dscr("rq_d", (2 * C, T), BF16)
        dscr("kk_d", (C, T), BF16)
        dscr("v_d", (T, C), BF16)
        dscr("g2x_d", (T, C), BF16)
        dscr("s_d", (T, C), F32)
        dscr("h4_d", (HIDP, T), BF16)
        self.dram["out"] = nc.dram_tensor("out", [T, C], F32, kind="ExternalOutput").ap()

        ncc = self.nc
        self.af = Arena(self.es.enter_context(ncc.sbuf_tensor("arena_f", [128, 23000], F32)), 23000, F32)
        self.ab = Arena(self.es.enter_context(ncc.sbuf_tensor("arena_b", [128, 56000], BF16)), 56000, BF16)
        self.ps = [self.es.enter_context(ncc.psum_tensor(f"ps{i}", [128, 512], F32)) for i in range(4)]
        self.pw = [self.es.enter_context(ncc.psum_tensor(f"pw{i}", [128, 1024], F32)) for i in range(2)]
        self.wq_lo = self.ab.t.ap()[:, 46900:46900 + 3 * 1536]
        self.wo2 = self.ab.t.ap()[:, 51520:51520 + 4 * C]

        self.consts_small()

    def consts_small(self):
        pass

    def xn_view(self, ct, t0, n, shift=0):
        base = ct * (self.T + 1) + 1 + t0 - shift
        return self.xn[:, base:base + n]

    def dbg_out(self, name, shape, dt=F32):
        if name in self.dbg:
            ap = self.nc.dram_tensor("dbg_" + name, list(shape), dt, kind="ExternalOutput").ap()
            self.dbg_outs[name] = ap
            return ap
        return None

    def phase0(self):
        p, T = self.p, self.T
        d = self.dram
        s_xs = [self.sem(f"p0_x{i}") for i in range(4)]
        s_sta = self.sem("p0_sta")
        s_stv = self.sem("p0_stv")
        s_vch = self.sem("p0_vch")
        s_nv = self.sem("p0_norm")
        s_tr = self.sem("p0_tr")
        s_ev = self.sem("p0_ev")
        m = self.af.mark()
        mb = self.ab.mark()
        xtm = [self.af.alloc(C) for _ in range(4)]
        sq = [self.af.alloc(C) for _ in range(4)]
        stats = self.af.alloc(32)
        xnt = [self.ab.alloc(C) for _ in range(2)]
        p.emit("vector", lambda v: v.memset(
            self.xn.rearrange("p (c t) -> p c t", c=8)[:, :, 0:1], 0.0))
        for tt in range(self.NTT):
            xb = xtm[tt % 4]
            s_x = s_xs[tt % 4]
            waits = [(s_nv, tt - 4 + 1)] if tt >= 4 else []
            p.emit("sync", lambda sy, xb=xb, tt=tt: sy.dma_start(
                out=xb, in_=d["x"][tt * 128:(tt + 1) * 128, :]), waits=waits, incs=[(s_x, 16)])
            sqb = sq[tt % 4]
            col = lambda j, tt=tt: stats[:, (tt % 4) * 8 + j:(tt % 4) * 8 + j + 1]
            ssq, sxs, muv, tmp, var, lnv, rst = (col(j) for j in range(7))
            guards = []
            if tt >= 4:
                guards = [(s_nv, tt - 4 + 1), (s_stv, tt - 4 + 1)]
            p.emit("scalar", lambda sc, xb=xb, sqb=sqb, ssq=ssq: sc.activation(
                sqb, xb, AF.Square, accum_out=ssq),
                waits=[(s_x, None)] + guards + ([(s_sta, 2 * (tt - 4) + 1)] if tt >= 4 else []),
                incs=[(s_sta, 1)])
            p.emit("vector", lambda v, xb=xb, sxs=sxs: v.tensor_reduce(sxs, xb, AX.X, ALU.add),
                   waits=[(s_x, None)] + guards, incs=[(s_vch, 1)])
            p.emit("vector", lambda v, sxs=sxs, muv=muv: v.tensor_scalar_mul(muv, sxs, 1.0 / C),
                   waits=[(s_vch, 3 * tt + 1)], incs=[(s_vch, 1)])
            p.emit("vector", lambda v, muv=muv, tmp=tmp: v.tensor_tensor(tmp, muv, muv, ALU.mult),
                   waits=[(s_vch, 3 * tt + 2)], incs=[(s_vch, 1)])
            p.emit("vector", lambda v, ssq=ssq, tmp=tmp, var=var: v.scalar_tensor_tensor(
                var, ssq, 1.0 / C, tmp, ALU.mult, ALU.subtract),
                waits=[(s_vch, 3 * tt + 3), (s_sta, 2 * tt + 1)], incs=[(s_stv, 1)])
            p.emit("scalar", lambda sc, var=var, lnv=lnv: sc.activation(
                lnv, var, AF.Ln, bias=self.cc[LN_EPS]),
                waits=[(s_stv, tt + 1)], incs=[(self.sem("p0_lnch"), 1)])
            p.emit("scalar", lambda sc, lnv=lnv, rst=rst: sc.activation(rst, lnv, AF.Exp, scale=-0.5),
                   waits=[(self.sem("p0_lnch"), tt + 1)], incs=[(s_sta, 1)])
            xnb = xnt[tt % 2]
            waits = [(s_sta, 2 * tt + 2), (s_vch, 3 * tt + 2)]
            if tt >= 2:
                waits.append((s_ev, (tt - 2 + 1) * 2))
            p.emit("vector", lambda v, xb=xb, xnb=xnb, muv=muv, rst=rst: v.tensor_scalar(
                xnb, xb, muv, rst, ALU.subtract, ALU.mult),
                waits=waits, incs=[(s_nv, 1)])
            for g in range(2):
                bank = self.ps[(tt * 2 + g) % 4].ap().bitcast(BF16)
                def tr_f(t, xnb=xnb, g=g, bank=bank):
                    last = None
                    for j in range(4):
                        ct = g * 4 + j
                        last = t.transpose(bank[:, j * 128:(j + 1) * 128],
                                           xnb[:, ct * 128:(ct + 1) * 128], self.identb)
                    return last
                waits = [(s_nv, tt + 1)]
                if tt * 2 + g >= 4:
                    waits.append((s_ev, (tt * 2 + g) - 4 + 1))
                if tt == 0 and g == 0:
                    waits.append((self.sem("consts_g"), self.cg_sem_target))
                p.emit("tensor", tr_f, waits=waits, incs=[(s_tr, 1)])
                xn4 = self.xn.rearrange("p (c t) -> p c t", c=8)[:, g * 4:(g + 1) * 4,
                                                                1 + tt * 128:1 + (tt + 1) * 128]
                p.emit("vector", lambda v, bank=bank, xn4=xn4: v.tensor_copy(
                    xn4, bank[:, 0:512].rearrange("p (c t) -> p c t", c=4)),
                    waits=[(s_tr, tt * 2 + g + 1)], incs=[(s_ev, 1)])
        self.p0_done = [(s_ev, s_ev.n)]
        dbg = self.dbg_out("xn", (128, 8 * (T + 1)), BF16)
        if dbg is not None:
            p.emit("sync", lambda sy: sy.dma_start(out=dbg, in_=self.xn),
                   waits=[(s_ev, None)], incs=[(self.sem("dbg"), 16)])
        self.af.reset(m)
        self.ab.reset(mb)

    def load_consts(self):
        p, d = self.p, self.dram
        s_c = self.sem("consts")
        self.identb = self.ab.alloc(128)
        self.identf = self.af.alloc(128)
        s_cg = self.sem("consts_g")
        p.emit("gpsimd", lambda gp: gp.dma_start(out=self.identb, in_=d["ident"]), incs=[(s_cg, 16)])
        p.emit("sync", lambda sy: sy.dma_start(out=self.identf, in_=d["ident"]), incs=[(s_c, 16)])
        self.maskA = self.af.alloc(512)
        p.emit("sync", lambda sy: sy.dma_start(out=self.maskA, in_=d["maskA"]), incs=[(s_c, 16)])
        self.mask_low = self.af.alloc(128)
        self.mask_up = self.af.alloc(128)
        p.emit("sync", lambda sy: sy.dma_start(out=self.mask_low, in_=d["mask_low"]), incs=[(s_c, 16)])
        p.emit("sync", lambda sy: sy.dma_start(out=self.mask_up, in_=d["mask_up"]), incs=[(s_c, 16)])
        self.vecs = {}
        for nm in ("tmx", "tmw", "tmk", "tmv", "tmr", "tmg", "tdecay", "u_", "lnw_", "lnb_"):
            t = self.af.alloc(8)
            self.vecs[nm] = t
            p.emit("sync", lambda sy, t=t, nm=nm: sy.dma_start(out=t, in_=d[nm]), incs=[(s_c, 16)])
        self.blockones = self.ab.alloc(2)
        def bo_f(v):
            v.memset(self.blockones[0:64, 0:1], 1.0)
            v.memset(self.blockones[64:128, 0:1], 0.0)
            v.memset(self.blockones[0:64, 1:2], 0.0)
            return v.memset(self.blockones[64:128, 1:2], 1.0)
        s_cv = self.sem("consts_v")
        p.emit("vector", bo_f, incs=[(s_cv, 1)])
        self.onescol = self.ab.alloc(8)
        p.emit("vector", lambda v: v.memset(self.onescol, 1.0), incs=[(s_cv, 1)])
        self.cc = {}
        for val in (LN_EPS, GN_EPS, 1e-30, 0.0):
            t = self.af.alloc(1)
            self.cc[val] = t
            p.emit("vector", lambda v, t=t, val=val: v.memset(t, val), incs=[(s_cv, 1)])
        self.c_sem_target = s_c.n
        self.cv_sem_target = s_cv.n
        self.cg_sem_target = s_cg.n
        self.xn_mark = self.ab.mark()
        self.xn = self.ab.alloc(8 * (self.T + 1))

    def load_rwkv_weights(self):
        p, d = self.p, self.dram
        self.wmark_b = self.ab.mark()
        s_w = self.sem("rwkv_w")
        self.Wslot = [self.ab.alloc(8 * C), self.ab.alloc(8 * C)]
        for slot, nm in ((0, "w_k"), (1, "w_r")):
            t = self.Wslot[slot]
            p.emit("gpsimd", lambda gp, t=t, nm=nm: gp.dma_start(
                out=t.rearrange("p (g n) -> p g n", g=8),
                in_=d[nm].rearrange("(g p) n -> p g n", p=128)), incs=[(s_w, 16)])
        self.maa_w1 = self.ab.alloc(8 * 160)
        p.emit("gpsimd", lambda gp: gp.dma_start(
            out=self.maa_w1.rearrange("p (g n) -> p g n", g=8),
            in_=d["maa_w1"].rearrange("(g p) n -> p g n", p=128)), incs=[(s_w, 16)])
        self.maa_w2 = self.ab.alloc(5 * C)
        p.emit("gpsimd", lambda gp: gp.dma_start(
            out=self.maa_w2[0:32, :], in_=d["maa_w2"]), incs=[(s_w, 16)])
        self.td_w2 = self.ab.alloc(C)[0:64, :]
        p.emit("gpsimd", lambda gp: gp.dma_start(
            out=self.td_w2, in_=d["td_w2"]), incs=[(s_w, 16)])
        self.td_w1 = self.ab.alloc(8 * 64)
        p.emit("gpsimd", lambda gp: gp.dma_start(
            out=self.td_w1.rearrange("p (g n) -> p g n", g=8),
            in_=d["td_w1"].rearrange("(g p) n -> p g n", p=128)), incs=[(s_w, 16)])
        self.rwkv_w = s_w

    def phase1(self):
        p, T, d = self.p, self.T, self.dram
        TS = 256
        NTIL = T // TS
        self.dvec = self.af.alloc(self.NTT * 16)
        self.e3 = self.af.alloc(8 * self.NCHK)
        m = self.af.mark(); mb = self.ab.mark()
        xx2 = [self.af.alloc(8 * TS) for _ in range(2)]
        kf = self.af.alloc(8 * TS)
        ld = self.af.alloc(8 * TS)
        q = self.af.alloc(8 * TS)
        teb = [self.af.alloc(TS) for _ in range(2)]
        e1b = [self.af.alloc(TS) for _ in range(2)]
        e2b = [self.af.alloc(TS) for _ in range(2)]
        zbb = [self.ab.alloc(TS) for _ in range(2)]
        qlb = self.af.alloc(2)
        xxx1 = self.ab.alloc(8 * TS)
        oper = [self.ab.alloc(8 * TS) for _ in range(3)]
        t5f1 = [self.ab.alloc(TS) for _ in range(5)]
        tw = self.ab.alloc(TS)
        rkub = self.ab.alloc(TS)
        fRQ = self.ab.alloc(8 * TS)
        fKK = self.ab.alloc(8 * TS)
        fV = fRQ
        fG = fKK
        msb2 = [self.af.alloc(TS) for _ in range(2)]
        rrb2 = [self.af.alloc(TS) for _ in range(2)]
        sv = self.sem("p1_v")
        sa = self.sem("p1_a")
        sm = self.sem("p1_m")
        sl = self.sem("p1_pool")
        sst = self.sem("p1_st")
        sw2a = self.sem("p1_w2a")
        sw2b = self.sem("p1_w2b")

        s_zro = self.sem("p1_zro")
        p.emit("vector", lambda v: v.memset(fKK, 0.0), waits=list(self.p0_done), incs=[(sv, 1)])
        vv0 = d["rq_d"].rearrange("(c p) t -> p c t", c=8)
        for cz in range(8):
            p.emit("sync", lambda sy, cz=cz: sy.dma_start(
                out=vv0[64:192, cz, :], in_=fKK[:, 0:T]),
                waits=[(sv, None)], incs=[(s_zro, 16)])
        zro_t = s_zro.n

        xxx_done_t = {}

        def emit_xxx(it, pass2):
            t0 = it * TS
            G = self.guard
            par = it % 2
            xx = xx2[par]
            for ct in range(8):
                w0 = G(("xx", par, ct)) + (self.p0_done + [
                    (self.sem("consts"), self.c_sem_target),
                    (self.sem("consts_v"), self.cv_sem_target)]
                    if (it == 0 and not pass2 and ct == 0) else [])
                p.emit("vector", lambda v, ct=ct, t0=t0, xx=xx: v.tensor_tensor(
                    xx[:, ct * TS:(ct + 1) * TS], self.xn_view(ct, t0, TS, shift=1),
                    self.xn_view(ct, t0, TS), ALU.subtract),
                    waits=w0, incs=[(sv, 1)])
                p.emit("vector", lambda v, ct=ct, t0=t0, xx=xx: v.scalar_tensor_tensor(
                    xxx1[:, ct * TS:(ct + 1) * TS], xx[:, ct * TS:(ct + 1) * TS],
                    self.vecs["tmx"][:, ct:ct + 1], self.xn_view(ct, t0, TS),
                    ALU.mult, ALU.add), waits=G(("xxxbuf",)) + [(sv, None)], incs=[(sv, 1)])
            xxx_done_t[it] = sv.n

        def tile_pass(it, pass2):
            t0 = it * TS
            c0g = (it * TS) // L
            G = self.guard; SG = self.set_guard
            par = it % 2
            xx = xx2[par]
            xxx = xxx1
            t5f = t5f1
            xxx_done = xxx_done_t[it]
            for f5 in range(5):
                pb = self.ps[f5 % 2]
                po = 0
                def t5mm(t, f5=f5, pb=pb, po=po):
                    last = None
                    for k in range(8):
                        last = t.matmul(pb.ap()[0:32, po:po + TS],
                                        self.maa_w1[:, k * 160 + f5 * 32:k * 160 + f5 * 32 + 32],
                                        xxx[:, k * TS:(k + 1) * TS],
                                        start=(k == 0), stop=(k == 7))
                    return last
                p.emit("tensor", t5mm, waits=[(sv, xxx_done), (self.rwkv_w, None)] +
                       G(("t5ps", f5 % 2)), incs=[(sm, 1)])
                p.emit("scalar", lambda sc, f5=f5, pb=pb, po=po: sc.activation(
                    t5f[f5][0:32, :], pb.ap()[0:32, po:po + TS], AF.Tanh),
                    waits=[(sm, None)], incs=[(sa, 1)])
                SG(("t5ps", f5 % 2), sa)
            SG(("xxxbuf",), sm)
            fmap = {0: "tmw", 1: "tmk", 2: "tmv", 3: "tmr", 4: "tmg"}
            fs = [2, 4] if pass2 else [0, 1, 3]
            t5_a = {f5: None for f5 in range(5)}
            for fi, f in enumerate(fs):
                nmv = fmap[f]
                ob = oper[fi] if not pass2 else oper[fi % 2]
                t5a_cnt = sa.n
                for ct in range(8):
                    pb = self.ps[2 + ct % 2]
                    p.emit("tensor", lambda t, f=f, ct=ct, pb=pb: t.matmul(
                        pb.ap()[:, 0:TS], self.maa_w2[0:32, f * C + ct * 128:f * C + ct * 128 + 128],
                        t5f[f][0:32, :], start=True, stop=True),
                        waits=[(sa, t5a_cnt)] + G(("pk", ct % 2)), incs=[(sm, 1)])
                    msb = msb2[ct % 2]
                    p.emit("scalar", lambda sc, pb=pb, msb=msb: sc.activation(
                        msb, pb.ap()[:, 0:TS], AF.Copy),
                        waits=[(sm, None)] + G(("msb", ct % 2)), incs=[(sa, 1)])
                    SG(("pk", ct % 2), sa)
                    zb = zbb[ct % 2]
                    p.emit("vector", lambda v, ct=ct, zb=zb, msb=msb, nmv=nmv, xx=xx: v.scalar_tensor_tensor(
                        zb, msb, self.vecs[nmv][:, ct:ct + 1],
                        xx[:, ct * TS:(ct + 1) * TS], ALU.add, ALU.mult),
                        waits=[(sa, None)] + G(("zbb", ct % 2)), incs=[(sv, 1)])
                    SG(("msb", ct % 2), sv)
                    p.emit("vector", lambda v, ct=ct, zb=zb, ob=ob, t0=t0: v.tensor_tensor(
                        ob[:, ct * TS:(ct + 1) * TS], zb, self.xn_view(ct, t0, TS), ALU.add),
                        waits=[(sv, None)] + G(("oper", ob.offset, ct)), incs=[(sl, 1)])
                    SG(("zbb", ct % 2), sl)
            oper_done = sl.n
            if not pass2:
                ob = oper[0]
                def twmm(t):
                    last = None
                    for k in range(8):
                        last = t.matmul(self.pw[0].ap()[0:64, 0:TS],
                                        self.td_w1[:, k * 64:(k + 1) * 64],
                                        ob[:, k * TS:(k + 1) * TS], start=(k == 0), stop=(k == 7))
                    return last
                p.emit("tensor", twmm, waits=[(sl, oper_done - 16)] + G(("pw0",)), incs=[(sm, 1)])
                p.emit("scalar", lambda sc: sc.activation(tw[0:64, :], self.pw[0].ap()[0:64, 0:TS],
                                                          AF.Tanh), waits=[(sm, None)], incs=[(sa, 1)])
                SG(("pw0",), sa)
                tw_done = sa.n
                for ct in range(8):
                    pb = self.ps[2 + ct % 2]
                    p.emit("tensor", lambda t, ct=ct, pb=pb: t.matmul(
                        pb.ap()[:, 0:TS], self.td_w2[:, ct * 128:(ct + 1) * 128], tw[0:64, :],
                        start=True, stop=True),
                        waits=[(sa, tw_done)] + G(("pk", ct % 2)), incs=[(sm, 1)])
                    p.emit("scalar", lambda sc, ct=ct, pb=pb: sc.activation(
                        ld[:, ct * TS:(ct + 1) * TS], pb.ap()[:, 0:TS], AF.Exp,
                        bias=self.vecs["tdecay"][:, ct:ct + 1]),
                        waits=[(sm, None)] + G(("ld", ct)), incs=[(sa, 1)])
                    SG(("pk", ct % 2), sa)
                    for cc_ in range(TS // L):
                        p.emit("vector", lambda v, ct=ct, cc_=cc_: v.tensor_tensor_scan(
                            q[:, ct * TS + cc_ * L: ct * TS + (cc_ + 1) * L],
                            ld[:, ct * TS + cc_ * L: ct * TS + (cc_ + 1) * L],
                            self.cc[0.0].broadcast_to([128, L]), 0.0, ALU.add, ALU.add),
                            waits=[(sa, None)] + G(("q", ct, cc_)), incs=[(sv, 1)])
                    p.emit("vector", lambda v, ct=ct: v.tensor_copy(
                        qlb, q[:, ct * TS + L - 1: (ct + 1) * TS: L]),
                        waits=[(sv, None)] + G(("qlb",)), incs=[(sv, 1)])
                    p.emit("scalar", lambda sc, ct=ct, c0g=c0g: sc.activation(
                        self.e3[:, ct * self.NCHK + c0g: ct * self.NCHK + c0g + TS // L],
                        qlb, AF.Exp, scale=-1.0),
                        waits=[(sv, None)], incs=[(sa, 1)])
                    SG(("qlb",), sa)
                if it + 1 < NTIL:
                    emit_xxx(it + 1, pass2)
                for ct in range(8):
                    pb = self.ps[2 + ct % 2]
                    ob1 = oper[1]
                    def kmm(t, ct=ct, pb=pb, ob1=ob1):
                        last = None
                        for k in range(8):
                            last = t.matmul(pb.ap()[:, 0:TS],
                                            self.Wslot[0][:, k * C + ct * 128: k * C + ct * 128 + 128],
                                            ob1[:, k * TS:(k + 1) * TS], start=(k == 0), stop=(k == 7))
                        return last
                    p.emit("tensor", kmm, waits=[(sl, oper_done - 8), (s_zro, zro_t)] +
                           G(("pk", ct % 2)), incs=[(sm, 1)])
                    p.emit("scalar", lambda sc, ct=ct, pb=pb: sc.activation(
                        kf[:, ct * TS:(ct + 1) * TS], pb.ap()[:, 0:TS], AF.Copy),
                        waits=[(sm, None)] + G(("kf", ct)), incs=[(sa, 1)])
                    SG(("pk", ct % 2), sa)
                    p.emit("scalar", lambda sc, ct=ct, e2=e2b[ct % 2]: sc.activation(
                        e2, q[:, ct * TS:(ct + 1) * TS], AF.Exp),
                        waits=[(sv, None)] + G(("e2b", ct % 2)), incs=[(sa, 1)])
                    p.emit("vector", lambda v, ct=ct, e2=e2b[ct % 2]: v.tensor_tensor(
                        fKK[:, ct * TS:(ct + 1) * TS], kf[:, ct * TS:(ct + 1) * TS], e2, ALU.mult),
                        waits=[(sa, None)] + G(("fKK", ct)), incs=[(sv, 1)])
                    SG(("e2b", ct % 2), sv)
                    SG(("q", ct, 0), sv); SG(("q", ct, 1), sv)
                for ct in range(8):
                    pb = self.ps[2 + ct % 2]
                    ob2 = oper[2]
                    def rmm(t, ct=ct, pb=pb, ob2=ob2):
                        last = None
                        for k in range(8):
                            last = t.matmul(pb.ap()[:, 0:TS],
                                            self.Wslot[1][:, k * C + ct * 128: k * C + ct * 128 + 128],
                                            ob2[:, k * TS:(k + 1) * TS], start=(k == 0), stop=(k == 7))
                        return last
                    p.emit("tensor", rmm, waits=[(sl, oper_done)] + G(("pk", ct % 2)), incs=[(sm, 1)])
                    p.emit("vector", lambda v, ct=ct, te=teb[ct % 2]: v.tensor_tensor(
                        te, ld[:, ct * TS:(ct + 1) * TS], q[:, ct * TS:(ct + 1) * TS], ALU.subtract),
                        waits=[(sv, None)] + G(("teb", ct % 2)), incs=[(sv, 1)])
                    SG(("ld", ct), sv)
                    p.emit("scalar", lambda sc, ct=ct, te=teb[ct % 2], e1=e1b[ct % 2]: sc.activation(
                        e1, te, AF.Exp), waits=[(sv, None)] + G(("e1b", ct % 2)), incs=[(sa, 1)])
                    SG(("teb", ct % 2), sa)
                    rrb = rrb2[ct % 2]
                    p.emit("scalar", lambda sc, pb=pb, rrb=rrb: sc.activation(
                        rrb, pb.ap()[:, 0:TS], AF.Copy),
                        waits=[(sm, None)] + G(("rrb", ct % 2)), incs=[(sa, 1)])
                    SG(("pk", ct % 2), sa)
                    p.emit("vector", lambda v, ct=ct, rrb=rrb, e1=e1b[ct % 2]: v.tensor_tensor(
                        fRQ[:, ct * TS:(ct + 1) * TS], rrb, e1, ALU.mult),
                        waits=[(sa, None)] + G(("fRQ", ct)), incs=[(sv, 1)])
                    p.emit("vector", lambda v, ct=ct, rrb=rrb: v.scalar_tensor_tensor(
                        rkub, rrb, self.vecs["u_"][:, ct:ct + 1],
                        kf[:, ct * TS:(ct + 1) * TS], ALU.mult, ALU.mult),
                        waits=[(sv, None)] + G(("rkub",)), incs=[(sv, 1)])
                    SG(("rrb", ct % 2), sv)
                    SG(("e1b", ct % 2), sv)
                    SG(("kf", ct), sv)
                    for ts_ in range(2):
                        tt = it * 2 + ts_
                        p.emit("tensor", lambda t, ts_=ts_, ct=ct: t.matmul(
                            self.pw[1].ap()[:, 960:962], rkub[:, ts_ * 128:(ts_ + 1) * 128],
                            self.blockones, start=True, stop=True),
                            waits=[(sv, None)] + G(("pw1d",)), incs=[(sm, 1)])
                        p.emit("vector", lambda v, tt=tt, ct=ct: v.tensor_copy(
                            self.dvec[:, tt * 16 + 2 * ct: tt * 16 + 2 * ct + 2],
                            self.pw[1].ap()[:, 960:962]),
                            waits=[(sm, None)], incs=[(sv, 1)])
                        SG(("pw1d",), sv)
                    SG(("rkub",), sm)
                vv = d["rq_d"].rearrange("(c p) t -> p c t", c=8)
                p.emit("sync", lambda sy, t0=t0, vv=vv: sy.dma_start(
                    out=vv[0:64, :, t0:t0 + TS],
                    in_=fRQ[0:64, :].rearrange("p (g t) -> p g t", g=8)),
                    waits=[(sv, None)], incs=[(sst, 16)])
                p.emit("sync", lambda sy, t0=t0, vv=vv: sy.dma_start(
                    out=vv[192:256, :, t0:t0 + TS],
                    in_=fRQ[64:128, :].rearrange("p (g t) -> p g t", g=8)),
                    waits=[(sv, None)], incs=[(sst, 16)])
                p.emit("sync", lambda sy, it=it: sy.dma_start(
                    out=d["kk_d"].rearrange("(g p) t -> p g t", p=128)[:, :, t0:t0 + TS],
                    in_=fKK.rearrange("p (g t) -> p g t", g=8)),
                    waits=[(sv, None)], incs=[(sst, 16)])
                sst_t = sst.n
                for ct in range(8):
                    SG(("fRQ", ct), sst); SG(("fKK", ct), sst)
                    SG(("xx", par, ct), sl)
            else:
                if it + 1 < NTIL:
                    emit_xxx(it + 1, pass2)
                for fi, (f, dst, silu) in enumerate(((2, fV, False), (4, fG, True))):
                    ob = oper[fi % 2]
                    wslot = self.Wslot[0] if f == 2 else self.Wslot[1]
                    for ts_ in range(2):
                        pb = self.pw[ts_ % 2]
                        def vmm(t, ob=ob, wslot=wslot, ts_=ts_, pb=pb):
                            last = None
                            for k in range(8):
                                for nh in range(2):
                                    last = t.matmul(
                                        pb.ap()[:, nh * 512:(nh + 1) * 512],
                                        ob[:, k * TS + ts_ * 128: k * TS + ts_ * 128 + 128],
                                        wslot[:, k * C + nh * 512: k * C + (nh + 1) * 512],
                                        start=(k == 0), stop=(k == 7))
                            return last
                        p.emit("tensor", vmm,
                               waits=[(sl, oper_done - 16 + (fi + 1) * 8),
                                      ((sw2a if f == 2 else sw2b), 16)] + G(("pwv", ts_ % 2)),
                               incs=[(sm, 1)])
                        if not silu:
                            p.emit("scalar", lambda sc, ts_=ts_, pb=pb, dst=dst: sc.activation(
                                dst[:, ts_ * C:(ts_ + 1) * C], pb.ap()[:, 0:C], AF.Copy),
                                waits=[(sm, None)] + G(("fV", ts_)), incs=[(sa, 1)])
                            SG(("pwv", ts_ % 2), sa)
                        else:
                            th = kf[:, ts_ * C:(ts_ + 1) * C]
                            p.emit("scalar", lambda sc, ts_=ts_, pb=pb, th=th: sc.activation(
                                th, pb.ap()[:, 0:C], AF.Tanh, scale=0.5),
                                waits=[(sm, None)] + G(("thv", ts_)), incs=[(sa, 1)])
                            p.emit("vector", lambda v, ts_=ts_, pb=pb, th=th, dst=dst: v.scalar_tensor_tensor(
                                dst[:, ts_ * C:(ts_ + 1) * C], th, 1.0, pb.ap()[:, 0:C],
                                ALU.add, ALU.mult),
                                waits=[(sa, None)] + G(("fG", ts_)), incs=[(sv, 1)])
                            SG(("pwv", ts_ % 2), sv)
                            SG(("thv", ts_), sv)
                p.emit("sync", lambda sy, it=it: sy.dma_start(
                    out=d["v_d"][t0:t0 + TS, :].rearrange("(g p) c -> p g c", p=128),
                    in_=fV.rearrange("p (g c) -> p g c", g=2)),
                    waits=[(sa, None)], incs=[(sst, 16)])
                p.emit("sync", lambda sy, it=it: sy.dma_start(
                    out=d["g2x_d"][t0:t0 + TS, :].rearrange("(g p) c -> p g c", p=128),
                    in_=fG.rearrange("p (g c) -> p g c", g=2)),
                    waits=[(sv, None)], incs=[(sst, 16)])
                for ts_ in range(2):
                    SG(("fV", ts_), sst); SG(("fG", ts_), sst)
                for ct in range(8):
                    SG(("xx", par, ct), sl)

        emit_xxx(0, False)
        for it in range(NTIL):
            tile_pass(it, pass2=False)
        for ts_ in range(2):
            self.set_guard(("fV", ts_), sst)
            self.set_guard(("fG", ts_), sst)
            self.set_guard(("thv", ts_), sv)
            self.set_guard(("pwv", ts_), sv)
        p.emit("gpsimd", lambda gp: gp.dma_start(
            out=self.Wslot[0].rearrange("p (g n) -> p g n", g=8),
            in_=d["w_v"].rearrange("(g p) n -> p g n", p=128)),
            waits=[(sm, None)], incs=[(sw2a, 16)])
        p.emit("gpsimd", lambda gp: gp.dma_start(
            out=self.Wslot[1].rearrange("p (g n) -> p g n", g=8),
            in_=d["w_g"].rearrange("(g p) n -> p g n", p=128)),
            waits=[(sm, None)], incs=[(sw2b, 16)])
        emit_xxx(0, True)
        for it in range(NTIL):
            tile_pass(it, pass2=True)
        self.p1_done = [(sst, sst.n), (sv, sv.n), (sa, sa.n), (sm, sm.n), (sl, sl.n)]
        for nm, buf in (("dvec", self.dvec), ("e3", self.e3)):
            dbg = self.dbg_out(nm, (128, buf.shape[1]))
            if dbg is not None:
                p.emit("sync", lambda sy, dbg=dbg, buf=buf: sy.dma_start(out=dbg, in_=buf),
                       waits=self.p1_done, incs=[(self.sem("dbg"), 16)])
        self.af.reset(m); self.ab.reset(mb)

    def prefetch_ffn0(self):
        p, d = self.p, self.dram
        s = self.sem("p4_gw")
        p.emit("gpsimd", lambda gp: gp.dma_start(
            out=self.wq_lo.rearrange("p (g n) -> p g n", g=3),
            in_=d["wqkv"][0].rearrange("(g p) n -> p g n", p=128)[:, 5:8, :]),
            waits=list(self.p1_done), incs=[(s, 16)])
        p.emit("gpsimd", lambda gp: gp.dma_start(
            out=self.wo2.rearrange("p (g n) -> p g n", g=4),
            in_=d["wo"][0].rearrange("(g p) n -> p g n", p=128)),
            waits=[], incs=[(s, 16)])

    def phase2(self, parts=31):
        p, T, d = self.p, self.T, self.dram
        NC = self.NCHK
        self.ab.reset(self.wmark_b)
        m = self.af.mark(); mb = self.ab.mark()
        self.yng = self.ab.alloc(self.NTT * C)
        self.p2_streams_mark = self.ab.mark()
        rqc = [self.ab.alloc(2 * C) for _ in range(2)]
        kkc = [self.ab.alloc(C) for _ in range(2)]
        vc = [self.ab.alloc(C) for _ in range(2)]
        gc = [self.ab.alloc(C) for _ in range(2)]
        s_rq = [self.sem(f"p2_rq{i}") for i in range(2)]
        s_kk = [self.sem(f"p2_kk{i}") for i in range(2)]
        s_v = [self.sem(f"p2_v{i}") for i in range(2)]
        s_g = [self.sem(f"p2_g{i}") for i in range(2)]
        kkt = self.ab.alloc(C)
        absb = self.ab.alloc(2048)
        Sst = self.af.alloc(8 * DH)
        stp = self.af.alloc(8 * DH)
        Sbf = self.ab.alloc(8 * DH)
        ysb = self.af.alloc(C)
        sqb = self.af.alloc(C)
        self.lnw_rep = self.af.alloc(C)
        self.lnb_rep = self.af.alloc(C)
        s_lnr = self.sem("p2_lnr")
        p.emit("sync", lambda sy: sy.dma_start(out=self.lnw_rep, in_=d["lnw_rep"]),
               waits=list(self.p1_done), incs=[(s_lnr, 16)])
        p.emit("sync", lambda sy: sy.dma_start(out=self.lnb_rep, in_=d["lnb_rep"]),
               waits=[], incs=[(s_lnr, 16)])
        st16 = self.af.alloc(7 * 16)
        col16 = lambda j: st16[:, j * 16:(j + 1) * 16]
        sm = self.sem("p2_mm"); sv = self.sem("p2_dv"); sa = self.sem("p2_act"); sl = self.sem("p2_pl")
        p.emit("vector", lambda v: v.memset(Sst, 0.0), waits=self.p1_done, incs=[(sv, 1)])
        wave_consumed = [None] * (NC + 2)

        def emit_loads(c, ldw):
            sl_ = c % 2
            p.emit("sync", lambda sy, c=c, b=rqc[sl_]: sy.dma_start(
                out=b.rearrange("p (g t) -> p g t", g=16),
                in_=d["rq_d"].rearrange("(g p) t -> p g t", p=128)[:, :, c * L:(c + 1) * L]),
                waits=ldw, incs=[(s_rq[sl_], 16)])
            p.emit("sync", lambda sy, c=c, b=kkc[sl_]: sy.dma_start(
                out=b.rearrange("p (g t) -> p g t", g=8),
                in_=d["kk_d"].rearrange("(g p) t -> p g t", p=128)[:, :, c * L:(c + 1) * L]),
                waits=ldw, incs=[(s_kk[sl_], 16)])
            p.emit("sync", lambda sy, c=c, b=vc[sl_]: sy.dma_start(
                out=b, in_=d["v_d"][c * L:(c + 1) * L, :]),
                waits=ldw, incs=[(s_v[sl_], 16)])
            p.emit("sync", lambda sy, c=c, b=gc[sl_]: sy.dma_start(
                out=b, in_=d["g2x_d"][c * L:(c + 1) * L, :]),
                waits=ldw, incs=[(s_g[sl_], 16)])

        emit_loads(0, list(self.p1_done))
        if NC > 1:
            emit_loads(1, list(self.p1_done))
        for c in range(NC):
            sl_ = c % 2
            RQ, KK, V, G = rqc[sl_], kkc[sl_], vc[sl_], gc[sl_]
            pw0b = self.pw[0].ap().bitcast(BF16)
            def trf(t, KK=KK):
                last = None
                for j in range(8):
                    last = t.transpose(pw0b[:, j * 128:(j + 1) * 128],
                                       KK[:, j * 128:(j + 1) * 128], self.identb)
                return last
            if not (parts & 1):
                continue
            p.emit("tensor", trf, waits=[(s_kk[sl_], s_kk[sl_].n)] + self.guard(("p2pw0",)),
                   incs=[(sm, 1)])
            p.emit("vector", lambda v: v.tensor_copy(kkt, pw0b[:, 0:1024]), waits=[(sm, None)], incs=[(sv, 1)])
            kkt_done = sv.n
            p.emit("vector", lambda v: v.tensor_copy(Sbf, Sst), waits=[(sv, None), (sm, None)],
                   incs=[(sv, 1)])
            bank_heads = [(self.ps[0], (0, 2, 4, 6)), (self.ps[1], (8, 10, 12, 14)),
                          (self.ps[2], (1, 3, 5, 7)), (self.ps[3], (9, 11, 13, 15))]
            for bank, heads in (bank_heads if parts & 2 else []):
                def amm(t, bank=bank, heads=heads, RQ=RQ, KK=KK):
                    last = None
                    for hh, h in enumerate(heads):
                        ct = h // 2; r0 = 64 * (h % 2)
                        last = t.matmul(bank.ap()[:, hh * 128:(hh + 1) * 128],
                                        KK[r0:r0 + 64, ct * 128:(ct + 1) * 128],
                                        RQ[r0:r0 + 64, h * 128:(h + 1) * 128],
                                        start=True, stop=True)
                    return last
                p.emit("tensor", amm, waits=[(s_rq[sl_], s_rq[sl_].n), (s_kk[sl_], s_kk[sl_].n)] +
                       self.guard(("Aps", bank_heads.index((bank, heads)) if False else id(bank))),
                       incs=[(sm, 1)])
                p.emit("vector", lambda v, bank=bank, heads=heads: v.tensor_tensor(
                    absb.rearrange("p (h t) -> p h t", h=16)[:, heads[0]:heads[0] + 7:2, :],
                    bank.ap()[:].rearrange("p (h t) -> p h t", h=4),
                    self.maskA.rearrange("p (h t) -> p h t", h=4), ALU.mult),
                    waits=[(sm, None)], incs=[(sv, 1)])
                self.set_guard(("Aps", id(bank)), sv)
            aev_done = sv.n
            for h in (range(16) if parts & 4 else []):
                ct = h // 2; r0 = 64 * (h % 2)
                def ymm(t, h=h, ct=ct, r0=r0, RQ=RQ, V=V):
                    t.matmul(self.pw[0].ap()[:, h * 64:(h + 1) * 64],
                             absb[:, h * 128:(h + 1) * 128],
                             V[:, h * 64:(h + 1) * 64], start=True, stop=False)
                    return t.matmul(self.pw[0].ap()[:, h * 64:(h + 1) * 64],
                                    RQ[:, h * 128:(h + 1) * 128],
                                    Sbf[:, ct * 64:(ct + 1) * 64],
                                    start=False, stop=True)
                p.emit("tensor", ymm, waits=[(sv, aev_done), (s_v[sl_], s_v[sl_].n)] +
                       self.guard(("p2pw0",)), incs=[(sm, 1)])
            y_done = sm.n
            for h in (range(16) if parts & 4 else []):
                ct = h // 2; r0 = 64 * (h % 2)
                p.emit("tensor", lambda t, h=h, ct=ct, r0=r0, V=V: t.matmul(
                    self.pw[1].ap()[0:64, h * 64:(h + 1) * 64],
                    kkt[:, ct * 128 + r0: ct * 128 + r0 + 64],
                    V[:, h * 64:(h + 1) * 64], start=True, stop=True),
                    waits=[(sv, kkt_done)], incs=[(sm, 1)])
            u_done = sm.n
            for h in (range(16) if parts & 8 else []):
                p.emit("vector", lambda v, h=h, c=c, V=V: v.scalar_tensor_tensor(
                    ysb[:, h * 64:(h + 1) * 64], V[:, h * 64:(h + 1) * 64],
                    self.dvec[:, c * 16 + h: c * 16 + h + 1],
                    self.pw[0].ap()[:, h * 64:(h + 1) * 64], ALU.mult, ALU.add),
                    waits=[(sm, y_done), (sv, None), (sl, None)], incs=[(sv, 1)])
            yev_done = sv.n
            self.set_guard(("p2pw0",), sv)
            if parts & 8:
                uview = self.pw[1].ap()[0:64, :].rearrange("p (h x e) -> p h x e", h=8, x=2)
                p.emit("vector", lambda v, uview=uview: v.tensor_tensor(
                    stp[0:64, :].rearrange("p (h e) -> p h e", h=8),
                    Sst[0:64, :].rearrange("p (h e) -> p h e", h=8),
                    uview[:, :, 0, :], ALU.add),
                    waits=[(sm, u_done)], incs=[(sv, 1)])
                p.emit("vector", lambda v, uview=uview: v.tensor_tensor(
                    stp[64:128, :].rearrange("p (h e) -> p h e", h=8),
                    Sst[64:128, :].rearrange("p (h e) -> p h e", h=8),
                    uview[:, :, 1, :], ALU.add),
                    waits=[(sm, u_done), (sv, None)], incs=[(sv, 1)])
                e3v = self.e3.rearrange("p (ct n) -> p ct n", ct=8)[:, :, c:c + 1]
                p.emit("vector", lambda v, e3v=e3v: v.tensor_tensor(
                    Sst.rearrange("p (ct e) -> p ct e", ct=8), stp.rearrange("p (ct e) -> p ct e", ct=8),
                    e3v.broadcast_to([128, 8, DH]), ALU.mult),
                    waits=[(sv, None)], incs=[(sv, 1)])
            if not (parts & 16):
                wave_consumed[c] = (sv.n, sm.n)
                if c + 2 < NC:
                    emit_loads(c + 2, [(sv, sv.n), (sm, sm.n)])
                continue
            p.emit("gpsimd", lambda g_: g_.tensor_tensor(sqb, ysb, ysb, ALU.mult),
                   waits=[(sv, yev_done), (sv, None)], incs=[(sl, 1)])
            ysum, sqs, mu, msq, var, lnv, rstd = (col16(j) for j in range(7))
            p.emit("vector", lambda v: v.tensor_reduce(
                ysum, ysb.rearrange("p (h j) -> p h j", h=16), AX.X, ALU.add),
                waits=[(sv, yev_done)], incs=[(sv, 1)])
            p.emit("vector", lambda v: v.tensor_reduce(
                sqs, sqb.rearrange("p (h j) -> p h j", h=16), AX.X, ALU.add),
                waits=[(sl, None)], incs=[(sv, 1)])
            p.emit("vector", lambda v: v.tensor_scalar_mul(mu, ysum, 1.0 / DH),
                   waits=[(sv, None)], incs=[(sv, 1)])
            p.emit("vector", lambda v: v.tensor_tensor(msq, mu, mu, ALU.mult),
                   waits=[(sv, None)], incs=[(sv, 1)])
            p.emit("vector", lambda v: v.scalar_tensor_tensor(
                var, sqs, 1.0 / DH, msq, ALU.mult, ALU.subtract),
                waits=[(sv, None)], incs=[(sv, 1)])
            p.emit("scalar", lambda sc: sc.activation(lnv, var, AF.Ln, bias=self.cc[GN_EPS]),
                   waits=[(sv, None)], incs=[(sa, 1)])
            p.emit("scalar", lambda sc: sc.activation(rstd, lnv, AF.Exp, scale=-0.5),
                   waits=[(sa, None)], incs=[(sa, 1)])
            for h in range(16):
                p.emit("vector", lambda v, h=h: v.tensor_scalar(
                    sqb[:, h * 64:(h + 1) * 64], ysb[:, h * 64:(h + 1) * 64],
                    mu[:, h:h + 1], rstd[:, h:h + 1], ALU.subtract, ALU.mult),
                    waits=[(sa, None), (sl, None)], incs=[(sv, 1)])
            p.emit("vector", lambda v: v.tensor_tensor(ysb, sqb, self.lnw_rep, ALU.mult),
                   waits=[(sv, None), (sl, None), (self.sem("p2_lnr"), 32)], incs=[(sv, 1)])
            p.emit("vector", lambda v: v.tensor_tensor(sqb, ysb, self.lnb_rep, ALU.add),
                   waits=[(sv, None)], incs=[(sl, 1)])
            p.emit("vector", lambda v, c=c, G=G: v.scalar_tensor_tensor(
                self.yng[:, c * C:(c + 1) * C], sqb, 0.5, G, ALU.mult, ALU.mult),
                waits=[(sl, None), (s_g[sl_], s_g[sl_].n)], incs=[(sv, 1)])
            wave_consumed[c] = (sv.n, sm.n)
            if c + 2 < NC:
                emit_loads(c + 2, [(sv, sv.n), (sm, sm.n)])
        self.p2_done = [(sv, sv.n), (sm, sm.n), (sa, sa.n), (sl, sl.n)]
        dbg = self.dbg_out("yng", (128, self.NTT * C), BF16)
        if dbg is not None:
            p.emit("sync", lambda sy, dbg=dbg: sy.dma_start(out=dbg, in_=self.yng),
                   waits=self.p2_done, incs=[(self.sem("dbg"), 16)])
        self.af.reset(m)
        self.p2_mb = mb

    def phase3(self):
        p, T, d = self.p, self.T, self.dram
        m = self.af.mark()
        self.ab.reset(self.p2_streams_mark)
        yngf = self.ab.alloc(C)
        self.wo_sb = self.ab.alloc(8 * C)
        s_til = [self.af.alloc(C) for _ in range(2)]
        sm = self.sem("p3_mm"); sv = self.sem("p3_dv"); sa = self.sem("p3_act")
        ssts = [self.sem(f"p3_st{i}") for i in range(2)]
        ps3b = self.ps[3].ap().bitcast(BF16)
        s_wo = self.sem("p2_wo")
        p.emit("gpsimd", lambda gp: gp.dma_start(
            out=self.wo_sb.rearrange("p (g n) -> p g n", g=8),
            in_=d["w_o"].rearrange("(g p) n -> p g n", p=128)),
            waits=list(self.p2_done), incs=[(s_wo, 16)])
        for tt in range(self.NTT):
            def trf(t, tt=tt):
                last = None
                for ct in range(8):
                    last = t.transpose(ps3b[:, ct * 128:(ct + 1) * 128],
                                       self.yng[:, tt * C + ct * 128: tt * C + (ct + 1) * 128],
                                       self.identb)
                return last
            w_ = [(sv, None)] + (self.p2_done if tt == 0 else [])
            p.emit("tensor", trf, waits=w_, incs=[(sm, 1)])
            p.emit("vector", lambda v: v.tensor_copy(yngf, ps3b), waits=[(sm, None)], incs=[(sv, 1)])
            def womm(t, tt=tt):
                last = None
                for ct in range(8):
                    for nh in range(2):
                        last = t.matmul(self.pw[0].ap()[:, nh * 512:(nh + 1) * 512],
                                        yngf[:, ct * 128:(ct + 1) * 128],
                                        self.wo_sb[:, ct * C + nh * 512: ct * C + nh * 512 + 512],
                                        start=(ct == 0), stop=(ct == 7))
                return last
            p.emit("tensor", womm, waits=[(sv, None), (s_wo, 16), (sa, None)], incs=[(sm, 1)])
            stl = s_til[tt % 2]
            sst = ssts[tt % 2]
            w_ = [(sm, None)]
            if tt >= 2:
                w_.append((sst, sst.n))
            p.emit("scalar", lambda sc, stl=stl: sc.activation(stl, self.pw[0].ap()[:], AF.Copy),
                   waits=w_, incs=[(sa, 1)])
            p.emit("sync", lambda sy, tt=tt, stl=stl: sy.dma_start(
                out=d["s_d"][tt * 128:(tt + 1) * 128, :], in_=stl),
                waits=[(sa, None)], incs=[(sst, 16)])
        self.p3_done = [(ssts[0], ssts[0].n), (ssts[1], ssts[1].n), (sm, sm.n), (sv, sv.n), (sa, sa.n)]
        self.af.reset(m)
        self.ab.reset(self.p2_mb)

    def phase4(self):
        p, T, d = self.p, self.T, self.dram
        NW = self.NW
        sm = self.sem("p4_mm"); sv = self.sem("p4_dv"); sa = self.sem("p4_act"); sl = self.sem("p4_pl")
        sld = self.sem("p4_ld")
        sgw = self.sem("p4_gw")
        sg1 = [self.sem(f"p4_g1{i}") for i in range(2)]
        sg2 = [self.sem(f"p4_g2{i}") for i in range(2)]
        sh4 = [self.sem(f"p4_h4{i}") for i in range(2)]
        sst4 = self.sem("p4_st")
        mf0 = self.af.mark()
        xf = self.af.alloc(8 * (T + 1))
        self.xf = xf
        for ct in range(8):
            p.emit("vector", lambda v, ct=ct: v.tensor_copy(
                xf[:, ct * (T + 1):(ct + 1) * (T + 1)], self.xn[:, ct * (T + 1):(ct + 1) * (T + 1)]),
                waits=(list(self.p3_done) + [(sv, None)]) if ct == 0 else [], incs=[(sv, 1)])
        xf_init = sv.n
        self.ab.reset(self.xn_mark)
        qtm = self.af.alloc(512); q2b = self.af.alloc(512)
        rot1 = self.af.alloc(512); rot2 = self.af.alloc(512)
        s8 = self.af.alloc(8); l8 = self.af.alloc(8); r8 = self.af.alloc(8)
        r_mu = qtm[:, 0:512]; r_e2 = q2b[:, 0:512]
        r_ms = rot1[:, 0:512]; r_va = rot2[:, 0:512]
        r_rs = self.af.alloc(512); r_msb = self.af.alloc(512)
        onesrow = self.af.alloc(128)
        p.emit("vector", lambda v: v.memset(onesrow[0:1, :], 1.0), incs=[(sv, 1)])
        qsc = self.af.alloc(FDH); ksc = self.af.alloc(FDH)
        lngv = self.af.alloc(8); lnbv = self.af.alloc(8); b2v = self.af.alloc(8)
        b1v = self.af.alloc(44)
        scr1 = self.af.alloc(512); scr2 = self.af.alloc(512)
        rcp4 = self.af.alloc(4)
        m4a = self.ab.mark()

        def attn_stage(l):
            self.ab.reset(m4a)
            wq_hi = self.ab.alloc(5 * 3 * INNER)
            wo_sb2 = self.wo2
            p.emit("gpsimd", lambda gp, l=l: gp.dma_start(
                out=wq_hi.rearrange("p (g n) -> p g n", g=5),
                in_=d["wqkv"][l].rearrange("(g p) n -> p g n", p=128)[:, 0:5, :]),
                waits=[(sm, None), (sv, None)], incs=[(sgw, 16)])
            if l > 0:
                p.emit("gpsimd", lambda gp, l=l: gp.dma_start(
                    out=self.wq_lo.rearrange("p (g n) -> p g n", g=3),
                    in_=d["wqkv"][l].rearrange("(g p) n -> p g n", p=128)[:, 5:8, :]),
                    waits=[(sm, None), (sv, None)], incs=[(sgw, 16)])
                p.emit("gpsimd", lambda gp, l=l: gp.dma_start(
                    out=wo_sb2.rearrange("p (g n) -> p g n", g=4),
                    in_=d["wo"][l].rearrange("(g p) n -> p g n", p=128)),
                    waits=[], incs=[(sgw, 16)])
            tabs_raw = {}
            for nm in ("ct_q", "st_q", "ct_k", "st_k"):
                t = self.ab.alloc(self.NTT * FDH)
                tabs_raw[nm] = t
                p.emit("gpsimd", lambda gp, t=t, nm=nm: gp.dma_start(
                    out=t.rearrange("p (g n) -> p g n", g=self.NTT),
                    in_=d[nm].rearrange("(g p) n -> p g n", p=128)),
                    waits=([(sv, xf_init)] + list(self.p3_done)) if l == 0 else
                          [(sm, None), (sv, None)],
                    incs=[(sgw, 16)])
            gw_t = sgw.n
            for nm, t in (("lng", lngv), ("lnb2", lnbv), ("b2", b2v), ("b1p", b1v)):
                p.emit("sync", lambda sy, nm=nm, t=t, l=l: sy.dma_start(out=t, in_=d[nm][l]),
                       waits=self.guard(("smallvec",)), incs=[(sld, 16)])
            ld_t = sld.n
            tabs = tabs_raw
            qscb = self.ab.alloc(FDH)
            kscb = self.ab.alloc(FDH)
            p.emit("gpsimd", lambda gp, l=l: gp.dma_start(out=qscb, in_=d["qscale_rep"][l]),
                   waits=[(sm, None), (sv, None)], incs=[(sgw, 16)])
            p.emit("gpsimd", lambda gp, l=l: gp.dma_start(out=kscb, in_=d["kscale_rep"][l]),
                   waits=[], incs=[(sgw, 16)])
            gw_t = sgw.n
            for nm, scv in (("ct_q", qscb), ("st_q", qscb), ("ct_k", kscb), ("st_k", kscb)):
                p.emit("vector", lambda v, nm=nm, scv=scv: v.tensor_tensor(
                    tabs_raw[nm].rearrange("p (g e) -> p g e", g=self.NTT),
                    tabs_raw[nm].rearrange("p (g e) -> p g e", g=self.NTT),
                    scv.unsqueeze(1).broadcast_to([128, self.NTT, FDH]), ALU.mult),
                    waits=[(sgw, gw_t), (sv, None)], incs=[(sv, 1)])
            zwin = self.ab.alloc(8 * 512)
            kfm = [self.ab.alloc(4 * 512) for _ in range(3)]
            vtl = [self.ab.alloc(4 * 520) for _ in range(3)]
            qfm = self.ab.alloc(4 * 512)
            attn2 = [self.ab.alloc(2560) for _ in range(2)]
            otm = self.ab.alloc(4 * 512)
            b_rotB = otm
            ofm = self.ab.alloc(4 * 512)
            b_sqB = ofm
            q_tm = self.ab.alloc(2048)
            k_tm = self.ab.alloc(2048)
            s32 = self.af.alloc(32)
            xbnd = self.ab.alloc(8 * NW)
            assert self.ab.off <= 46900, f"attn allocs {self.ab.off} hit prefetch region"
            for rwi in range(3):
                p.emit("vector", lambda v, rwi=rwi: v.memset(
                    vtl[rwi].rearrange("p (s h e) -> p s h e", s=4, h=8)[:, :, :, 64:65], 1.0),
                    waits=[(sv, None), (sm, None)], incs=[(sv, 1)])
            p.emit("vector", lambda v: v.tensor_copy(
                xbnd.rearrange("p (c w2) -> p c w2", c=8),
                xf.rearrange("p (c t) -> p c t", c=8, t=T + 1)[:, :, 0:T:512]),
                waits=[(sv, None), (sm, None)], incs=[(sv, 1)])
            xbnd_done = sv.n

            def spans(w):
                out = []
                if w >= 1:
                    for kt in range(4):
                        out.append((kt, (w - 1) % 3, kt, 0, 128 * (kt + 1), "low", 128 * kt))
                for kt in range(4, 8):
                    tq0 = 128 * (kt - 4)
                    out.append((kt, w % 3, kt - 4, tq0, 512 - tq0, "up", 0))
                return out

            ps3b = self.ps[3].ap().bitcast(BF16)

            qk_evac = {}
            qk_done = {}

            def front(w):
                sp = spans(w)
                offs = {}
                o = 0
                for e in sp:
                    offs[e[0]] = o
                    o += e[4]
                for ct in range(8):
                    sh = 1 if ct >= 4 else 0
                    p.emit("vector", lambda v, ct=ct, w=w, sh=sh: v.tensor_copy(
                        zwin[:, ct * 512:(ct + 1) * 512],
                        xf[:, ct * (T + 1) + 1 + w * 512 - sh: ct * (T + 1) + 1 + (w + 1) * 512 - sh]),
                        waits=[(sv, None)] + self.guard(("zwin", ct)), incs=[(sv, 1)])
                for ct in range(4, 8):
                    p.emit("vector", lambda v, ct=ct, w=w: v.tensor_copy(
                        zwin[:, ct * 512:ct * 512 + 1], xbnd[:, ct * NW + w: ct * NW + w + 1]),
                        waits=[(sv, None)], incs=[(sv, 1)])
                zdone = sv.n
                for ts_ in range(4):
                    for wh, pb in ((0, self.ps[0]), (1, self.ps[1]), (2, self.ps[2])):
                        def qkvmm(t, ts_=ts_, wh=wh, pb=pb):
                            last = None
                            for k in range(8):
                                wsrc = (wq_hi[:, k * 1536 + wh * 512: k * 1536 + (wh + 1) * 512]
                                        if k < 5 else
                                        self.wq_lo[:, (k - 5) * 1536 + wh * 512: (k - 5) * 1536 + (wh + 1) * 512])
                                last = t.matmul(pb.ap()[:],
                                                zwin[:, k * 512 + ts_ * 128: k * 512 + ts_ * 128 + 128],
                                                wsrc,
                                                start=(k == 0), stop=(k == 7))
                            return last
                        p.emit("tensor", qkvmm,
                               waits=[(sv, zdone), (sgw, gw_t)] + self.guard(("ps", wh)),
                               incs=[(sm, 1)])
                    p.emit("scalar", lambda sc, ts_=ts_, w=w: sc.activation(
                        vtl[w % 3].rearrange("p (s h e) -> p s h e", s=4, h=8)[:, ts_, :, 0:64],
                        self.ps[2].ap()[:].rearrange("p (h e) -> p h e", h=8), AF.Copy),
                        waits=[(sm, None)] + self.guard(("vtlrd", w % 3)), incs=[(sa, 1)])
                    self.set_guard(("ps", 2), sa)
                    p.emit("scalar", lambda sc, ts_=ts_: sc.activation(
                        q_tm[:, ts_ * 512:(ts_ + 1) * 512], self.ps[0].ap()[:], AF.Copy),
                        waits=[(sm, None)] + self.guard(("q_tm",)), incs=[(sa, 1)])
                    self.set_guard(("ps", 0), sa)
                    p.emit("scalar", lambda sc, ts_=ts_: sc.activation(
                        k_tm[:, ts_ * 512:(ts_ + 1) * 512], self.ps[1].ap()[:], AF.Copy),
                        waits=[(sm, None)] + self.guard(("k_tm",)), incs=[(sa, 1)])
                    self.set_guard(("ps", 1), sa)
                qk_evac[w] = sa.n
                for ct in range(8):
                    self.set_guard(('zwin', ct), sm)
                return sp, offs

            def chains(w):
                    for which, src_tm, ct_t, st_t, dstfm in (
                            (0, q_tm, "ct_q", "st_q", qfm),
                            (1, k_tm, "ct_k", "st_k", kfm[w % 3])):
                        p.emit("scalar", lambda sc, src_tm=src_tm: sc.activation(
                            b_sqB, src_tm, AF.Square),
                            waits=[(sa, qk_evac[w])] + self.guard(("b_sq",)), incs=[(sa, 1)])
                        p.emit("vector", lambda v: v.tensor_reduce(
                            s32, b_sqB.rearrange("p (g e) -> p g e", g=32), AX.X, ALU.add),
                            waits=[(sa, None)] + self.guard(("s32",)), incs=[(sv, 1)])
                        p.emit("scalar", lambda sc: sc.activation(s32, s32, AF.Ln, bias=self.cc[1e-30]),
                               waits=[(sv, None)], incs=[(sa, 1)])
                        p.emit("scalar", lambda sc: sc.activation(s32, s32, AF.Exp, scale=-0.5),
                               waits=[(sa, None)], incs=[(sa, 1)])
                        p.emit("vector", lambda v: v.tensor_scalar_min(s32, s32, 1e12),
                               waits=[(sa, None)], incs=[(sv, 1)])
                        p.emit("vector", lambda v, src_tm=src_tm: v.tensor_tensor(
                            b_sqB.rearrange("p (g e) -> p g e", g=32),
                            src_tm.rearrange("p (g e) -> p g e", g=32),
                            s32.unsqueeze(2).broadcast_to([128, 32, FDH]), ALU.mult),
                            waits=[(sv, None)], incs=[(sv, 1)])
                        ctab = tabs[ct_t][:, w * 256:(w + 1) * 256]
                        stab = tabs[st_t][:, w * 256:(w + 1) * 256]
                        p.emit("vector", lambda v, src_tm=src_tm, ctab=ctab: v.tensor_tensor(
                            src_tm.rearrange("p (s h e) -> p s h e", s=4, h=8),
                            b_sqB.rearrange("p (s h e) -> p s h e", s=4, h=8),
                            ctab.rearrange("p (s e) -> p s e", s=4).unsqueeze(2).broadcast_to(
                                [128, 4, 8, FDH]), ALU.mult),
                            waits=[(sv, None)], incs=[(sv, 1)])
                        for half in range(2):
                            p.emit("vector", lambda v, half=half, stab=stab: v.tensor_tensor(
                                b_rotB.rearrange("p (s h x e) -> p s h x e", s=4, h=8, x=2)[:, :, :, half, :],
                                b_sqB.rearrange("p (s h x e) -> p s h x e", s=4, h=8, x=2)[:, :, :, 1 - half, :],
                                stab.rearrange("p (s x e) -> p s x e", s=4, x=2)[:, :, half, :]
                                .unsqueeze(2).broadcast_to([128, 4, 8, 32]), ALU.mult),
                                waits=[(sv, None)], incs=[(sv, 1)])
                        p.emit("vector", lambda v, src_tm=src_tm: v.tensor_tensor(
                            b_sqB, src_tm, b_rotB, ALU.add),
                            waits=[(sv, None)], incs=[(sv, 1)])
                        fin_v = sv.n
                        for ts_ in range(4):
                            po = which * 512
                            def trq(t, ts_=ts_, po=po):
                                last = None
                                for ic in range(4):
                                    last = t.transpose(ps3b[:, po + ic * 128: po + (ic + 1) * 128],
                                                       b_sqB[:, ts_ * 512 + ic * 128: ts_ * 512 + (ic + 1) * 128],
                                                       self.identb)
                                return last
                            p.emit("tensor", trq, waits=[(sv, fin_v)] + self.guard(("ps3b", which)),
                                   incs=[(sm, 1)])
                            p.emit("vector", lambda v, ts_=ts_, dstfm=dstfm, po=po: v.tensor_copy(
                                dstfm.rearrange("p (ic t2) -> p ic t2", ic=4)[:, :, ts_ * 128:(ts_ + 1) * 128],
                                ps3b[:, po:po + 512].rearrange("p (ic t2) -> p ic t2", ic=4)),
                                waits=[(sm, None)], incs=[(sv, 1)])
                            self.set_guard(("ps3b", which), sv)
                        if which == 0:
                            self.set_guard(("q_tm",), sm)
                        else:
                            self.set_guard(("k_tm",), sm)
                        self.set_guard(("b_sq",), sm)
                        self.set_guard(("s32",), sv)
                    qk_done[w] = sv.n

            def attn(w, sp, offs):
                    sslot = {0: self.pw[1].ap()[:, 0:512], 1: self.pw[1].ap()[:, 512:1024]}
                    mask_done = {}

                    def emit_sblock(h):
                        ab_ = attn2[h % 2]
                        r0 = 64 * (h % 2); ic = h // 2
                        for ei, (kt, rw, ts_, tq0, span, mk, mko) in enumerate(sp):
                            slot = sslot[h % 2][:, 0:span]
                            kf_src = kfm[rw]
                            p.emit("tensor", lambda t, kf_src=kf_src, r0=r0, ic=ic, ts_=ts_, tq0=tq0, span=span, slot=slot: t.matmul(
                                slot, kf_src[r0:r0 + 64, ic * 512 + ts_ * 128: ic * 512 + (ts_ + 1) * 128],
                                qfm[r0:r0 + 64, ic * 512 + tq0: ic * 512 + tq0 + span],
                                start=True, stop=True),
                                waits=[(sv, qk_done[w])] + self.guard(("sslot", h % 2)),
                                incs=[(sm, 1)])
                            oo = offs[kt]
                            p.emit("scalar", lambda sc, slot=slot, ab_=ab_, oo=oo, span=span: sc.activation(
                                ab_[:, oo:oo + span], slot, AF.Exp, scale=8.0),
                                waits=[(sm, None)] + self.guard(("attn2", h % 2)), incs=[(sa, 1)])
                            self.set_guard(("sslot", h % 2), sa)
                            msk = self.mask_low if mk == "low" else self.mask_up
                            p.emit("vector", lambda v, ab_=ab_, oo=oo, mko=mko, msk=msk: v.tensor_tensor(
                                ab_[:, oo + mko: oo + mko + 128],
                                ab_[:, oo + mko: oo + mko + 128], msk, ALU.mult),
                                waits=[(sa, None)], incs=[(sv, 1)])
                        mask_done[h] = sv.n

                    def emit_oblock(h):
                        ab_ = attn2[h % 2]
                        for mi in range(4):
                            oslot = self.pw[0].ap()[:, (mi % 2) * 512:(mi % 2) * 512 + 65]
                            kts = [e for e in sp if e[3] <= mi * 128 < e[3] + e[4]]
                            def omm(t, kts=kts, mi=mi, oslot=oslot, ab_=ab_, h=h, offs=dict(offs)):
                                last = None
                                for j, (kt, rw, ts_, tq0, span, mk, mko) in enumerate(kts):
                                    last = t.matmul(oslot,
                                                    ab_[:, offs[kt] + mi * 128 - tq0: offs[kt] + mi * 128 - tq0 + 128],
                                                    vtl[rw].rearrange("p (s h2 e) -> p s h2 e", s=4, h2=8)[:, ts_, h, :],
                                                    start=(j == 0), stop=(j == len(kts) - 1))
                                return last
                            p.emit("tensor", omm, waits=[(sv, mask_done[h])] + self.guard(("oslot", mi % 2)),
                                   incs=[(sm, 1)])
                            p.emit("vector", lambda v, oslot=oslot, mi=mi: v.reciprocal(
                                rcp4[:, mi:mi + 1], oslot[:, 64:65]),
                                waits=[(sm, None)], incs=[(sv, 1)])
                            p.emit("vector", lambda v, oslot=oslot, mi=mi, h=h: v.tensor_single_scalar(
                                otm[:, mi * 512 + h * 64: mi * 512 + (h + 1) * 64],
                                oslot[:, 0:64], rcp4[:, mi:mi + 1], ALU.mult),
                                waits=[(sv, None)], incs=[(sv, 1)])
                            self.set_guard(("oslot", mi % 2), sv)
                        self.set_guard(("attn2", h % 2), sm)

                    for h in range(8):
                        emit_sblock(h)
                        if h > 0:
                            emit_oblock(h - 1)
                    emit_oblock(7)
                    o_done = sv.n
                    for mi in range(4):
                        po = 0
                        def tro(t, mi=mi, po=po):
                            last = None
                            for ic in range(4):
                                last = t.transpose(ps3b[:, po + ic * 128: po + (ic + 1) * 128],
                                                   otm[:, mi * 512 + ic * 128: mi * 512 + (ic + 1) * 128],
                                                   self.identb)
                            return last
                        p.emit("tensor", tro, waits=[(sv, o_done)] + self.guard(("ps3b", 0)),
                               incs=[(sm, 1)])
                        p.emit("vector", lambda v, mi=mi, po=po: v.tensor_copy(
                            ofm.rearrange("p (ic t2) -> p ic t2", ic=4)[:, :, mi * 128:(mi + 1) * 128],
                            ps3b[:, po:po + 512].rearrange("p (ic t2) -> p ic t2", ic=4)),
                            waits=[(sm, None)], incs=[(sv, 1)])
                        self.set_guard(("ps3b", 0), sv)
                    ofm_done = sv.n
                    for ctO in range(8):
                        wslice = self.pw[1].ap()[:, (ctO % 2) * 512:(ctO % 2) * 512 + 512]
                        def womm(t, ctO=ctO, wslice=wslice):
                            last = None
                            for icg in range(4):
                                last = t.matmul(wslice,
                                                wo_sb2[:, icg * C + ctO * 128: icg * C + ctO * 128 + 128],
                                                ofm[:, icg * 512:(icg + 1) * 512],
                                                start=(icg == 0), stop=(icg == 3))
                            return last
                        p.emit("tensor", womm,
                               waits=[(sv, ofm_done), (sgw, gw_t)] + self.guard(("sslot", ctO % 2)),
                               incs=[(sm, 1)])
                        p.emit("vector", lambda v, ctO=ctO, w=w, wslice=wslice: v.tensor_tensor(
                            xf[:, ctO * (T + 1) + 1 + w * 512: ctO * (T + 1) + 1 + (w + 1) * 512],
                            xf[:, ctO * (T + 1) + 1 + w * 512: ctO * (T + 1) + 1 + (w + 1) * 512],
                            wslice, ALU.add),
                            waits=[(sm, None)], incs=[(sv, 1)])
                        self.set_guard(("sslot", ctO % 2), sv)
                    self.set_guard(("vtlrd", (w + 2) % 3), sm)
                    self.set_guard(("b_sq",), sm)

            state = {}
            state[0] = front(0)
            chains(0)
            for w in range(NW):
                if w + 1 < NW:
                    state[w + 1] = front(w + 1)
                attn(w, *state[w])
                if w + 1 < NW:
                    chains(w + 1)

        def geglu_stage(l):
            self.ab.reset(m4a)
            ge_m = sm.n; ge_v = sv.n
            z1 = self.ab.alloc(8 * 512 * (T // 512))
            zbuf = self.ab.alloc(8 * 512)
            zqb = self.ab.alloc(8 * 512)
            ggb = [self.ab.alloc(512) for _ in range(2)]
            h4b = [self.ab.alloc(512) for _ in range(2)]
            w1q = [self.ab.alloc(8 * 1536) for _ in range(2)]
            sh4st = [self.sem(f"p4_h4st{l}{i}") for i in range(2)]
            NT4 = T // 512
            z1_t = [None] * NT4
            pw0row = self.pw[0].ap()[0:1, 512:1024]
            pw1row = self.pw[1].ap()[0:1, 512:1024]
            pw0bc = self.pw[0].ap()[:, 0:512]
            pw1bc = self.pw[1].ap()[:, 0:512]
            def emit_ln(tt4):
                for ct in range(8):
                    sh = 1 if ct >= 4 else 0
                    w_ = [(sv, None)] + self.guard(("zbuf",))
                    if tt4 == 0 and ct == 0:
                        w_.append((sm, ge_m))
                    p.emit("vector", lambda v, ct=ct, tt4=tt4, sh=sh: v.tensor_copy(
                        zbuf[:, ct * 512:(ct + 1) * 512],
                        xf[:, ct * (T + 1) + 1 + tt4 * 512 - sh: ct * (T + 1) + 1 + (tt4 + 1) * 512 - sh]),
                        waits=w_, incs=[(sv, 1)])
                    p.emit("scalar", lambda sc, ct=ct: sc.activation(
                        zqb[:, ct * 512:(ct + 1) * 512], zbuf[:, ct * 512:(ct + 1) * 512], AF.Square),
                        waits=[(sv, None)] + self.guard(("zqb",)), incs=[(sa, 1)])
                zd = sv.n; zqd = sa.n
                def statmm(t, out, buf):
                    last = None
                    for k in range(8):
                        last = t.matmul(out, self.onescol[:, 0:1],
                                        buf[:, k * 512:(k + 1) * 512], start=(k == 0), stop=(k == 7))
                    return last
                p.emit("tensor", lambda t: statmm(t, pw0row, zbuf),
                       waits=[(sv, zd)] + self.guard(("geps", 0)), incs=[(sm, 1)])
                p.emit("tensor", lambda t: statmm(t, pw1row, zqb),
                       waits=[(sa, zqd)] + self.guard(("geps", 1)), incs=[(sm, 1)])
                mm_t = sm.n
                self.set_guard(("zbuf",), sm)
                self.set_guard(("zqb",), sm)
                p.emit("vector", lambda v: v.tensor_scalar_mul(r_mu[0:1, :], pw0row, 1.0 / C),
                       waits=[(sm, mm_t), (sv, None)], incs=[(sv, 1)])
                p.emit("vector", lambda v: v.tensor_scalar_mul(r_e2[0:1, :], pw1row, 1.0 / C),
                       waits=[(sv, None)], incs=[(sv, 1)])
                self.set_guard(("geps", 0), sv)
                self.set_guard(("geps", 1), sv)
                p.emit("vector", lambda v: v.tensor_tensor(r_ms[0:1, :], r_mu[0:1, :], r_mu[0:1, :], ALU.mult),
                       waits=[(sv, None)], incs=[(sv, 1)])
                p.emit("vector", lambda v: v.tensor_tensor(r_va[0:1, :], r_e2[0:1, :], r_ms[0:1, :], ALU.subtract),
                       waits=[(sv, None)], incs=[(sv, 1)])
                p.emit("scalar", lambda sc: sc.activation(r_va[0:1, :], r_va[0:1, :], AF.Ln,
                                                          bias=self.cc[LN_EPS][0:1, :]),
                       waits=[(sv, None)], incs=[(sa, 1)])
                p.emit("scalar", lambda sc: sc.activation(r_rs[0:1, :], r_va[0:1, :], AF.Exp, scale=-0.5),
                       waits=[(sa, None)], incs=[(sa, 1)])
                p.emit("vector", lambda v: v.tensor_tensor(r_msb[0:1, :], r_mu[0:1, :], r_rs[0:1, :], ALU.mult),
                       waits=[(sa, None), (sv, None)], incs=[(sv, 1)])
                rows_done_v = sv.n; rows_done_a = sa.n
                p.emit("tensor", lambda t: t.matmul(pw0bc, onesrow[0:1, :], r_rs[0:1, :],
                                                    start=True, stop=True),
                       waits=[(sa, rows_done_a), (sv, rows_done_v)] + self.guard(("geps", 2)),
                       incs=[(sm, 1)])
                p.emit("tensor", lambda t: t.matmul(pw1bc, onesrow[0:1, :], r_msb[0:1, :],
                                                    start=True, stop=True),
                       waits=[(sv, rows_done_v)] + self.guard(("geps", 3)), incs=[(sm, 1)])
                bc_t = sm.n
                for ct in range(8):
                    sh = 1 if ct >= 4 else 0
                    p.emit("vector", lambda v, ct=ct, tt4=tt4, sh=sh: v.tensor_tensor(
                        scr1, xf[:, ct * (T + 1) + 1 + tt4 * 512 - sh: ct * (T + 1) + 1 + (tt4 + 1) * 512 - sh],
                        pw0bc, ALU.mult),
                        waits=[(sm, bc_t), (sv, None)], incs=[(sv, 1)])
                    p.emit("vector", lambda v: v.tensor_tensor(scr2, scr1, pw1bc, ALU.subtract),
                           waits=[(sv, None)], incs=[(sv, 1)])
                    p.emit("vector", lambda v, ct=ct, tt4=tt4: v.tensor_scalar(
                        z1[:, ct * T + tt4 * 512: ct * T + (tt4 + 1) * 512], scr2,
                        lngv[:, ct:ct + 1], lnbv[:, ct:ct + 1], ALU.mult, ALU.add),
                        waits=[(sv, None), (sld, None)], incs=[(sv, 1)])
                self.set_guard(("geps", 2), sv)
                self.set_guard(("geps", 3), sv)
                z1_t[tt4] = sv.n
            quarters = [(0, 6), (6, 5), (11, 6), (17, 5)]
            q_cons = [None] * 6
            stc = [0]

            def emit_q_load(qi):
                p0, np_ = quarters[qi]
                wb = w1q[qi % 2]
                qw = np_ * 256
                ldw = [(sm, ge_m), (sv, ge_v)] if qi < 2 else [(sm, q_cons[qi - 2])]
                p.emit("gpsimd", lambda gp, l=l, p0=p0, qw=qw, wb=wb: gp.dma_start(
                    out=wb.rearrange("p (g n) -> p g n", g=8)[:, :, 0:qw],
                    in_=d["w1p"][l][:, p0 * 256: p0 * 256 + qw].rearrange(
                        "(g p) n -> p g n", p=128)),
                    waits=ldw, incs=[(sg1[qi % 2], 16)])

            def emit_gemm_tile(qi, tt4):
                p0, np_ = quarters[qi]
                wb = w1q[qi % 2]
                for pi in range(np_):
                    step = stc[0]
                    i_glob = p0 + pi
                    pa = self.ps[2 * (step % 2)]
                    pg = self.ps[2 * (step % 2) + 1]
                    for agi, pb in ((0, pa), (1, pg)):
                        def w1mm(t, pi=pi, agi=agi, tt4=tt4, pb=pb, wb=wb):
                            last = None
                            for k in range(8):
                                last = t.matmul(pb.ap()[:],
                                                wb[:, k * 1536 + (2 * pi + agi) * 128: k * 1536 + (2 * pi + agi) * 128 + 128],
                                                z1[:, k * T + tt4 * 512: k * T + (tt4 + 1) * 512],
                                                start=(k == 0), stop=(k == 7))
                            return last
                        p.emit("tensor", w1mm,
                               waits=[(sv, z1_t[tt4]), (sg1[qi % 2], sg1[qi % 2].n)] +
                                     self.guard(("gep", 2 * (step % 2) + agi)),
                               incs=[(sm, 1)])
                    w1_t = sm.n
                    p.emit("scalar", lambda sc, i_glob=i_glob, pg=pg, gb=ggb[step % 2]: sc.activation(
                        gb, pg.ap()[:], AF.Gelu, bias=b1v[:, 2 * i_glob + 1:2 * i_glob + 2]),
                        waits=[(sm, w1_t)] + self.guard(("ggb", step % 2)), incs=[(sa, 1)])
                    self.set_guard(("gep", 2 * (step % 2) + 1), sa)
                    hb = h4b[step % 2]
                    p.emit("vector", lambda v, hb=hb, i_glob=i_glob, pa=pa, gb=ggb[step % 2]: v.scalar_tensor_tensor(
                        hb, pa.ap()[:],
                        b1v[:, 2 * i_glob:2 * i_glob + 1], gb,
                        ALU.add, ALU.mult),
                        waits=[(sa, None), (sh4st[step % 2], sh4st[step % 2].n)], incs=[(sv, 1)])
                    self.set_guard(("gep", 2 * (step % 2)), sv)
                    self.set_guard(("ggb", step % 2), sv)
                    p.emit("sync", lambda sy, i_glob=i_glob, tt4=tt4, hb=hb: sy.dma_start(
                        out=d["h4_d"][i_glob * 128:(i_glob + 1) * 128, tt4 * 512:(tt4 + 1) * 512],
                        in_=hb), waits=[(sv, None)], incs=[(sh4st[step % 2], 16)])
                    stc[0] += 1

            emit_q_load(0)
            emit_q_load(1)
            emit_ln(0)
            emit_ln(1)
            emit_gemm_tile(0, 0)
            emit_ln(2)
            emit_gemm_tile(0, 1)
            emit_ln(3)
            emit_gemm_tile(0, 2)
            emit_gemm_tile(0, 3)
            q_cons[0] = sm.n
            for qi in range(1, 4):
                if qi >= 2:
                    emit_q_load(qi)
                for tt4 in range(NT4):
                    emit_gemm_tile(qi, tt4)
                q_cons[qi] = sm.n
            h4st_t = [sh4st[0].n, sh4st[1].n]
            self.ab.reset(m4a)
            w2h = [self.ab.alloc(11 * C) for _ in range(2)]
            h4t = [self.ab.alloc(11 * 512) for _ in range(2)]
            for half in range(2):
                wb2 = w2h[half % 2]
                p.emit("gpsimd", lambda gp, l=l, half=half, wb2=wb2: gp.dma_start(
                    out=wb2.rearrange("p (g n) -> p g n", g=11),
                    in_=d["w2p"][l][half * 1408:(half + 1) * 1408, :].rearrange(
                        "(g p) n -> p g n", p=128)),
                    waits=[(sm, None), (sv, None), (sh4st[0], h4st_t[0]), (sh4st[1], h4st_t[1])],
                    incs=[(sg2[half], 16)])
                for tt4 in range(T // 512):
                    hs = h4t[tt4 % 2]
                    hld = sh4[tt4 % 2]
                    p.emit("sync", lambda sy, half=half, tt4=tt4, hs=hs: sy.dma_start(
                        out=hs.rearrange("p (g n) -> p g n", g=11),
                        in_=d["h4_d"][half * 1408:(half + 1) * 1408, tt4 * 512:(tt4 + 1) * 512].rearrange(
                            "(g p) n -> p g n", p=128)),
                        waits=[(sh4st[0], h4st_t[0]), (sh4st[1], h4st_t[1])] + self.guard(("h4t", tt4 % 2)),
                        incs=[(hld, 16)])
                    for ctO in range(8):
                        pb = self.ps[2 + ctO % 2]
                        def w2mm(t, hs=hs, ctO=ctO, pb=pb, wb2=wb2):
                            last = None
                            for k in range(11):
                                last = t.matmul(pb.ap()[:],
                                                wb2[:, k * C + ctO * 128: k * C + ctO * 128 + 128],
                                                hs[:, k * 512:(k + 1) * 512],
                                                start=(k == 0), stop=(k == 10))
                            return last
                        p.emit("tensor", w2mm, waits=[(hld, hld.n), (sg2[half], sg2[half].n)] +
                               self.guard(("gep", 2 + ctO % 2)), incs=[(sm, 1)])
                        if half == 0:
                            p.emit("vector", lambda v, ctO=ctO, tt4=tt4, pb=pb: v.tensor_tensor(
                                xf[:, ctO * (T + 1) + 1 + tt4 * 512: ctO * (T + 1) + 1 + (tt4 + 1) * 512],
                                xf[:, ctO * (T + 1) + 1 + tt4 * 512: ctO * (T + 1) + 1 + (tt4 + 1) * 512],
                                pb.ap()[:], ALU.add),
                                waits=[(sm, None)], incs=[(sv, 1)])
                        else:
                            p.emit("vector", lambda v, ctO=ctO, tt4=tt4, pb=pb: v.scalar_tensor_tensor(
                                xf[:, ctO * (T + 1) + 1 + tt4 * 512: ctO * (T + 1) + 1 + (tt4 + 1) * 512],
                                pb.ap()[:], b2v[:, ctO:ctO + 1],
                                xf[:, ctO * (T + 1) + 1 + tt4 * 512: ctO * (T + 1) + 1 + (tt4 + 1) * 512],
                                ALU.add, ALU.add),
                                waits=[(sm, None)], incs=[(sv, 1)])
                        self.set_guard(("gep", 2 + ctO % 2), sv)
                    self.set_guard(("h4t", tt4 % 2), sm)
            self.set_guard(("smallvec",), sa, sv)

        for l in range(2):
            attn_stage(l)
            geglu_stage(l)
        self.p4_done = [(sv, sv.n), (sm, sm.n), (sa, sa.n), (sl, sl.n), (sst4, sst4.n)]
        self.af.reset(mf0 + 8 * (T + 1))

    def phase5(self):
        p, T, d = self.p, self.T, self.dram
        sm = self.sem("p5_mm"); sv = self.sem("p5_dv"); sa = self.sem("p5_act")
        s_ld = [self.sem(f"p5_ld{i}") for i in range(2)]
        s_st = [self.sem(f"p5_st{i}") for i in range(2)]
        mb = self.ab.mark()
        gfb = self.ab.alloc(8 * T)
        gtt = self.ab.alloc(C)
        mf = self.af.mark()
        thb = self.af.alloc(C)
        xs_t = [self.af.alloc(C) for _ in range(2)]
        ss_t = [self.af.alloc(C) for _ in range(2)]
        tsc = thb
        thb2 = [thb, ss_t[0]]
        g_half = [0, 0]
        for hf in range(T // C):
            for ct in range(8):
                th = thb2[ct % 2]
                p.emit("scalar", lambda sc, ct=ct, hf=hf, th=th: sc.activation(
                    th, self.xf[:, ct * (T + 1) + 1 + hf * C: ct * (T + 1) + 1 + (hf + 1) * C],
                    AF.Tanh, scale=0.5),
                    waits=(list(self.p4_done) + [(sa, None), (sv, None)]) if (ct == 0 and hf == 0)
                    else ([(sv, sv.n - 1)] if sv.n else []),
                    incs=[(sa, 1)])
                p.emit("vector", lambda v, ct=ct, hf=hf, th=th: v.scalar_tensor_tensor(
                    gfb[:, ct * T + hf * C:(ct) * T + (hf + 1) * C], th, 1.0,
                    self.xf[:, ct * (T + 1) + 1 + hf * C: ct * (T + 1) + 1 + (hf + 1) * C],
                    ALU.add, ALU.mult),
                    waits=[(sa, None)], incs=[(sv, 1)])
            g_half[hf] = sv.n
        g_done = sv.n
        psb = [self.ps[2].ap().bitcast(BF16), self.ps[3].ap().bitcast(BF16)]
        gtt2 = [gtt, self.ab.alloc(C)]
        for tt in range(self.NTT):
            pb = psb[tt % 2]
            def trg(t, tt=tt, pb=pb):
                last = None
                for ct in range(8):
                    last = t.transpose(pb[:, ct * 128:(ct + 1) * 128],
                                       gfb[:, ct * T + tt * 128: ct * T + (tt + 1) * 128],
                                       self.identb)
                return last
            w_ = [(sv, g_half[tt // 8])] + self.guard(("p5ps", tt % 2))
            p.emit("tensor", trg, waits=w_, incs=[(sm, 1)])
            gt = gtt2[tt % 2]
            p.emit("vector", lambda v, pb=pb, gt=gt: v.tensor_copy(gt, pb[:, 0:1024]),
                   waits=[(sm, None)] + self.guard(("p5gt", tt % 2)), incs=[(sv, 1)])
            self.set_guard(("p5ps", tt % 2), sv)
            gt_done = sv.n
            xsl = xs_t[tt % 2]; ssl = ss_t[tt % 2]; osl = xs_t[tt % 2]
            ldw = [(s_st[tt % 2], s_st[tt % 2].n)]
            p.emit("sync", lambda sy, tt=tt, xsl=xsl: sy.dma_start(
                out=xsl, in_=d["x"][tt * 128:(tt + 1) * 128, :]),
                waits=ldw + ([(sv, g_done)] if tt < 2 else []), incs=[(s_ld[tt % 2], 16)])
            p.emit("sync", lambda sy, tt=tt, ssl=ssl: sy.dma_start(
                out=ssl, in_=d["s_d"][tt * 128:(tt + 1) * 128, :]),
                waits=[(sv, g_done)] if tt < 2 else [], incs=[(s_ld[tt % 2], 16)])
            p.emit("vector", lambda v, ssl=ssl, gt=gt: v.tensor_tensor(tsc, ssl, gt, ALU.mult),
                   waits=[(s_ld[tt % 2], s_ld[tt % 2].n), (sv, gt_done), (sv, None)], incs=[(sv, 1)])
            p.emit("vector", lambda v, xsl=xsl, osl=osl: v.scalar_tensor_tensor(
                osl, tsc, 0.5, xsl, ALU.mult, ALU.add),
                waits=[(sv, None)], incs=[(sv, 1)])
            self.set_guard(("p5gt", tt % 2), sv)
            p.emit("sync", lambda sy, tt=tt, osl=osl: sy.dma_start(
                out=d["out"][tt * 128:(tt + 1) * 128, :], in_=osl),
                waits=[(sv, None)], incs=[(s_st[tt % 2], 16)])
        self.af.reset(mf); self.ab.reset(mb)

    def finalize(self):
        nc = self.nc
        p = self.p
        waits = [(s, s.n) for s in self.sems.values() if s.n > 0]
        for i in range(0, len(waits), 8):
            p.emit("sync", lambda sy: None, waits=waits[i:i + 8])
        with nc.Block() as block:
            @block.tensor
            def _(t):
                p.replay("tensor", t)

            @block.vector
            def _(v):
                p.replay("vector", v)

            @block.scalar
            def _(sc):
                p.replay("scalar", sc)

            @block.gpsimd
            def _(gp):
                p.replay("gpsimd", gp)

            @block.sync
            def _(sy):
                p.replay("sync", sy)
        self.es.close()



def host_consts(T):
    out = {}
    out["ident"] = np.eye(128, dtype=np.float32)
    m = (np.arange(128)[:, None] < np.arange(128)[None, :]).astype(np.float32)
    out["maskA"] = np.tile(m, (1, 4))
    r = np.arange(128)
    out["mask_low"] = (r[None, :] <= r[:, None]).astype(np.float32)
    out["mask_up"] = (r[None, :] >= r[:, None]).astype(np.float32)
    d2 = np.arange(0, FDH, 2, dtype=np.float64)
    inv_freq = 1.0 / 10000.0 ** (d2 / FDH)
    t = np.arange(T, dtype=np.float64)
    freqs = t[:, None] * inv_freq[None, :]
    freqs = np.concatenate([freqs, freqs], axis=-1)
    base = (d2 + 0.4 * FDH) / (1.4 * FDH)
    power = (t - T // 2) / (WIN // 2)
    scale = base[None, :] ** power[:, None]
    scale = np.concatenate([scale, scale], axis=-1)
    cos, sin = np.cos(freqs), np.sin(freqs)
    sgn = np.concatenate([-np.ones(FDH // 2), np.ones(FDH // 2)])
    out["ct_q"] = (cos * scale).astype(np.float32)
    out["st_q"] = (sin * scale * sgn[None, :]).astype(np.float32)
    out["ct_k"] = (cos / scale).astype(np.float32)
    out["st_k"] = (sin / scale * sgn[None, :]).astype(np.float32)
    return out


def vec_fm(v):
    return np.ascontiguousarray(v.reshape(-1).reshape(8, 128).T).astype(np.float32)


def prep_core_inputs(inputs, b, T):
    g = lambda k: np.asarray(inputs[k])
    m = {}
    m["x"] = np.ascontiguousarray(g("L0_attn_x")[b, :T, :]).astype(np.float32, copy=False)
    m["tmx"] = vec_fm(g("L0_attn_time_maa_x"))
    m["tmw"] = vec_fm(g("L0_attn_time_maa_w"))
    m["tmk"] = vec_fm(g("L0_attn_time_maa_k"))
    m["tmv"] = vec_fm(g("L0_attn_time_maa_v"))
    m["tmr"] = vec_fm(g("L0_attn_time_maa_r"))
    m["tmg"] = vec_fm(g("L0_attn_time_maa_g"))
    m["tdecay"] = vec_fm(g("L0_attn_time_decay"))
    m["u_"] = vec_fm(g("L0_attn_time_faaaa"))
    m["lnw_"] = vec_fm(g("L0_attn_ln_w"))
    m["lnb_"] = vec_fm(g("L0_attn_ln_b"))
    w1 = g("L0_attn_time_maa_w1")
    m["maa_w1"] = np.ascontiguousarray(w1)
    w2 = g("L0_attn_time_maa_w2")
    m["maa_w2"] = np.ascontiguousarray(w2.transpose(1, 0, 2).reshape(32, 5 * C))
    m["td_w1"] = np.ascontiguousarray(g("L0_attn_time_decay_w1"))
    m["td_w2"] = np.ascontiguousarray(g("L0_attn_time_decay_w2"))
    m["w_r"] = np.ascontiguousarray(g("L0_attn_W_r"))
    m["w_k"] = np.ascontiguousarray(g("L0_attn_W_k"))
    m["w_v"] = np.ascontiguousarray(g("L0_attn_W_v"))
    m["w_g"] = np.ascontiguousarray(g("L0_attn_W_g"))
    m["w_o"] = np.ascontiguousarray(g("L0_attn_W_o"))
    m["lnw_rep"] = np.ascontiguousarray(np.broadcast_to(g("L0_attn_ln_w")[None, :], (128, C))).astype(np.float32)
    m["lnb_rep"] = np.ascontiguousarray(np.broadcast_to(g("L0_attn_ln_b")[None, :], (128, C))).astype(np.float32)
    m["wqkv"] = np.ascontiguousarray(g("L0_ffn_wqkv"))
    m["qscale_rep"] = np.ascontiguousarray(np.broadcast_to(g("L0_ffn_q_scale")[:, None, :], (2, 128, FDH))).astype(np.float32)
    m["kscale_rep"] = np.ascontiguousarray(np.broadcast_to(g("L0_ffn_k_scale")[:, None, :], (2, 128, FDH))).astype(np.float32)
    m["wo"] = np.ascontiguousarray(g("L0_ffn_wo"))
    m["lng"] = np.stack([vec_fm(g("L0_ffn_ln_g")[l]) for l in range(2)])
    m["lnb2"] = np.stack([vec_fm(g("L0_ffn_ln_b")[l]) for l in range(2)])
    m["b2"] = np.stack([vec_fm(g("L0_ffn_b2")[l]) for l in range(2)])
    w1f = g("L0_ffn_w1")
    b1f = g("L0_ffn_b1")
    w1p = np.zeros((2, C, 2 * HIDP), dtype=np.float32)
    b1p = np.zeros((2, 128, 44), dtype=np.float32)
    for l in range(2):
        a = w1f[l][:, :HID]; gg = w1f[l][:, HID:]
        ba = b1f[l][:HID]; bg = b1f[l][HID:]
        for i in range(22):
            lo, hi = i * 128, min((i + 1) * 128, HID)
            n = hi - lo
            if n > 0:
                w1p[l][:, (2 * i) * 128:(2 * i) * 128 + n] = a[:, lo:hi]
                w1p[l][:, (2 * i + 1) * 128:(2 * i + 1) * 128 + n] = gg[:, lo:hi]
                b1p[l][:n, 2 * i] = ba[lo:hi]
                b1p[l][:n, 2 * i + 1] = bg[lo:hi]
    m["w1p"] = w1p
    m["b1p"] = b1p
    w2f = g("L0_ffn_w2")
    w2p = np.zeros((2, HIDP, C), dtype=np.float32)
    w2p[:, :HID, :] = w2f
    m["w2p"] = w2p
    m.update(host_consts(T))
    return m


def build_nc(T=2048, dbg=(), phases=6):
    nc = bass.Bass("TRN2", target_bir_lowering=False, debug=False)
    b = Builder(nc, T, dbg)
    b.setup()
    b.load_consts()
    b.load_rwkv_weights()
    b.phase0()
    if phases > 1:
        b.phase1()
    if phases > 4:
        b.prefetch_ffn0()
    if phases > 2:
        b.phase2(parts=globals().get('_P2_PARTS', 31))
    if phases > 3:
        b.phase3()
    if phases > 4:
        b.phase4()
    if phases > 5:
        b.phase5()
    b.finalize()
    return b, nc


def kernel(**inputs):
    T = 2048
    bld, nc = build_nc(T)
    in_maps = [prep_core_inputs(inputs, b, T) for b in range(8)]
    res = run_bass_kernel_spmd(nc, in_maps, core_ids=list(range(8)))
    out = np.stack([res.results[i]["out"] for i in range(8)], axis=0)
    return out


# version 16 (final)
# speedup vs baseline: 1.0285x; 1.0082x over previous
import numpy as np
from contextlib import ExitStack

import concourse.bass as bass
import concourse.mybir as mybir
from concourse.bass_utils import run_bass_kernel_spmd

F32 = mybir.dt.float32
BF16 = mybir.dt.bfloat16
AF = mybir.ActivationFunctionType
ALU = mybir.AluOpType
AX = mybir.AxisListType

ENGINES = ("tensor", "vector", "scalar", "gpsimd", "sync")

C = 1024
H = 16
DH = 64
L = 128
FH = 8
FDH = 64
WIN = 512
INNER = 512
HID = 2730
HIDP = 2816
GN_EPS = 1e-5 * 64.0
LN_EPS = 1e-5


class Sem:
    def __init__(self, handle, name):
        self.h = handle
        self.name = name
        self.n = 0


class Prog:

    def __init__(self):
        self.ops = {e: [] for e in ENGINES}

    def emit(self, engine, fn, waits=(), incs=()):
        waits_f = [(s.h, (s.n if v is None else v)) for (s, v) in waits]
        incs_f = [(s.h, k) for (s, k) in incs]
        for s, k in incs:
            s.n += k
        self.ops[engine].append((waits_f, fn, incs_f))

    def replay(self, engine_name, eng):
        waited = {}
        for waits_f, fn, incs_f in self.ops[engine_name]:
            for h, v in waits_f:
                if v > 0 and waited.get(id(h), -1) < v:
                    eng.wait_ge(h, v)
                    waited[id(h)] = v
            inst = fn(eng)
            for h, k in incs_f:
                assert inst is not None, "op with incs must return an instruction"
                inst.then_inc(h, k)


class Arena:

    def __init__(self, t, cols, dtype):
        self.t = t
        self.cols = cols
        self.dtype = dtype
        self.off = 0
        self.align = 8 if dtype == BF16 else 4
        self.peak = 0

    def alloc(self, w):
        o = self.off
        w_al = -(-w // self.align) * self.align
        assert o + w_al <= self.cols, f"arena overflow {o}+{w} > {self.cols}"
        self.off = o + w_al
        self.peak = max(self.peak, self.off)
        return self.t.ap()[:, o:o + w]

    def mark(self):
        return self.off

    def reset(self, m):
        self.off = m


def ceil_div(a, b):
    return -(-a // b)


class Builder:
    def __init__(self, nc, T, dbg=()):
        self.nc = nc
        self.T = T
        self.NTT = T // 128
        self.NCHK = T // L
        self.NW = T // WIN
        self.dbg = set(dbg)
        self.p = Prog()
        self.es = ExitStack()
        self.sems = {}
        self.dram = {}
        self.dbg_outs = {}

    def set_guard(self, key, *sems):
        if not hasattr(self, "guards"):
            self.guards = {}
        self.guards[key] = [(s, s.n) for s in sems]

    def guard(self, key):
        if not hasattr(self, "guards"):
            self.guards = {}
        return list(self.guards.get(key, []))

    def sem(self, name):
        if name not in self.sems:
            self.sems[name] = Sem(self.es.enter_context(self.nc.semaphore(name)), name)
        return self.sems[name]

    def setup(self):
        nc, T = self.nc, self.T
        def din(name, shape, dt=F32):
            self.dram[name] = nc.dram_tensor(name, list(shape), dt, kind="ExternalInput").ap()
        din("x", (T, C))
        for nm in ("tmx", "tmw", "tmk", "tmv", "tmr", "tmg", "tdecay", "u_", "lnw_", "lnb_"):
            din(nm, (128, 8))
        din("maa_w1", (C, 160))
        din("maa_w2", (32, 5 * C))
        din("td_w1", (C, 64))
        din("td_w2", (64, C))
        din("w_r", (C, C)); din("w_k", (C, C)); din("w_v", (C, C)); din("w_g", (C, C)); din("w_o", (C, C))
        din("lnw_rep", (128, C)); din("lnb_rep", (128, C))
        din("wqkv", (2, C, 3 * INNER))
        din("qscale_rep", (2, 128, FDH)); din("kscale_rep", (2, 128, FDH))
        din("wo", (2, INNER, C))
        din("lng", (2, 128, 8)); din("lnb2", (2, 128, 8)); din("b2", (2, 128, 8))
        din("w1p", (2, C, 2 * HIDP))
        din("b1p", (2, 128, 44))
        din("w2p", (2, HIDP, C))
        din("ident", (128, 128))
        din("maskA", (128, 512))
        din("mask_low", (128, 128))
        din("mask_up", (128, 128))
        din("ct_q", (T, 64)); din("st_q", (T, 64)); din("ct_k", (T, 64)); din("st_k", (T, 64))
        def dscr(name, shape, dt):
            self.dram[name] = nc.dram_tensor(name, list(shape), dt, kind="Internal").ap()
        dscr("rq_d", (2 * C, T), BF16)
        dscr("kk_d", (C, T), BF16)
        dscr("v_d", (T, C), BF16)
        dscr("g2x_d", (T, C), BF16)
        dscr("s_d", (T, C), F32)
        dscr("h4_d", (HIDP, T), BF16)
        self.dram["out"] = nc.dram_tensor("out", [T, C], F32, kind="ExternalOutput").ap()

        ncc = self.nc
        self.af = Arena(self.es.enter_context(ncc.sbuf_tensor("arena_f", [128, 23000], F32)), 23000, F32)
        self.ab = Arena(self.es.enter_context(ncc.sbuf_tensor("arena_b", [128, 56000], BF16)), 56000, BF16)
        self.ps = [self.es.enter_context(ncc.psum_tensor(f"ps{i}", [128, 512], F32)) for i in range(4)]
        self.pw = [self.es.enter_context(ncc.psum_tensor(f"pw{i}", [128, 1024], F32)) for i in range(2)]
        self.wq_lo = self.ab.t.ap()[:, 46900:46900 + 3 * 1536]
        self.wo2 = self.ab.t.ap()[:, 51520:51520 + 4 * C]

        self.consts_small()

    def consts_small(self):
        pass

    def xn_view(self, ct, t0, n, shift=0):
        base = ct * (self.T + 1) + 1 + t0 - shift
        return self.xn[:, base:base + n]

    def dbg_out(self, name, shape, dt=F32):
        if name in self.dbg:
            ap = self.nc.dram_tensor("dbg_" + name, list(shape), dt, kind="ExternalOutput").ap()
            self.dbg_outs[name] = ap
            return ap
        return None

    def phase0(self):
        p, T = self.p, self.T
        d = self.dram
        s_xs = [self.sem(f"p0_x{i}") for i in range(4)]
        s_sta = self.sem("p0_sta")
        s_stv = self.sem("p0_stv")
        s_vch = self.sem("p0_vch")
        s_nv = self.sem("p0_norm")
        s_tr = self.sem("p0_tr")
        s_ev = self.sem("p0_ev")
        m = self.af.mark()
        mb = self.ab.mark()
        xtm = [self.af.alloc(C) for _ in range(4)]
        sq = [self.af.alloc(C) for _ in range(4)]
        stats = self.af.alloc(32)
        xnt = [self.ab.alloc(C) for _ in range(2)]
        p.emit("vector", lambda v: v.memset(
            self.xn.rearrange("p (c t) -> p c t", c=8)[:, :, 0:1], 0.0))
        for tt in range(self.NTT):
            xb = xtm[tt % 4]
            s_x = s_xs[tt % 4]
            waits = [(s_nv, tt - 4 + 1)] if tt >= 4 else []
            p.emit("sync", lambda sy, xb=xb, tt=tt: sy.dma_start(
                out=xb, in_=d["x"][tt * 128:(tt + 1) * 128, :]), waits=waits, incs=[(s_x, 16)])
            sqb = sq[tt % 4]
            col = lambda j, tt=tt: stats[:, (tt % 4) * 8 + j:(tt % 4) * 8 + j + 1]
            ssq, sxs, muv, tmp, var, lnv, rst = (col(j) for j in range(7))
            guards = []
            if tt >= 4:
                guards = [(s_nv, tt - 4 + 1), (s_stv, tt - 4 + 1)]
            p.emit("scalar", lambda sc, xb=xb, sqb=sqb, ssq=ssq: sc.activation(
                sqb, xb, AF.Square, accum_out=ssq),
                waits=[(s_x, None)] + guards + ([(s_sta, 2 * (tt - 4) + 1)] if tt >= 4 else []),
                incs=[(s_sta, 1)])
            p.emit("vector", lambda v, xb=xb, sxs=sxs: v.tensor_reduce(sxs, xb, AX.X, ALU.add),
                   waits=[(s_x, None)] + guards, incs=[(s_vch, 1)])
            p.emit("vector", lambda v, sxs=sxs, muv=muv: v.tensor_scalar_mul(muv, sxs, 1.0 / C),
                   waits=[(s_vch, 3 * tt + 1)], incs=[(s_vch, 1)])
            p.emit("vector", lambda v, muv=muv, tmp=tmp: v.tensor_tensor(tmp, muv, muv, ALU.mult),
                   waits=[(s_vch, 3 * tt + 2)], incs=[(s_vch, 1)])
            p.emit("vector", lambda v, ssq=ssq, tmp=tmp, var=var: v.scalar_tensor_tensor(
                var, ssq, 1.0 / C, tmp, ALU.mult, ALU.subtract),
                waits=[(s_vch, 3 * tt + 3), (s_sta, 2 * tt + 1)], incs=[(s_stv, 1)])
            p.emit("scalar", lambda sc, var=var, lnv=lnv: sc.activation(
                lnv, var, AF.Ln, bias=self.cc[LN_EPS]),
                waits=[(s_stv, tt + 1)], incs=[(self.sem("p0_lnch"), 1)])
            p.emit("scalar", lambda sc, lnv=lnv, rst=rst: sc.activation(rst, lnv, AF.Exp, scale=-0.5),
                   waits=[(self.sem("p0_lnch"), tt + 1)], incs=[(s_sta, 1)])
            xnb = xnt[tt % 2]
            waits = [(s_sta, 2 * tt + 2), (s_vch, 3 * tt + 2)]
            if tt >= 2:
                waits.append((s_ev, (tt - 2 + 1) * 2))
            p.emit("vector", lambda v, xb=xb, xnb=xnb, muv=muv, rst=rst: v.tensor_scalar(
                xnb, xb, muv, rst, ALU.subtract, ALU.mult),
                waits=waits, incs=[(s_nv, 1)])
            for g in range(2):
                bank = self.ps[(tt * 2 + g) % 4].ap().bitcast(BF16)
                def tr_f(t, xnb=xnb, g=g, bank=bank):
                    last = None
                    for j in range(4):
                        ct = g * 4 + j
                        last = t.transpose(bank[:, j * 128:(j + 1) * 128],
                                           xnb[:, ct * 128:(ct + 1) * 128], self.identb)
                    return last
                waits = [(s_nv, tt + 1)]
                if tt * 2 + g >= 4:
                    waits.append((s_ev, (tt * 2 + g) - 4 + 1))
                if tt == 0 and g == 0:
                    waits.append((self.sem("consts_g"), self.cg_sem_target))
                p.emit("tensor", tr_f, waits=waits, incs=[(s_tr, 1)])
                xn4 = self.xn.rearrange("p (c t) -> p c t", c=8)[:, g * 4:(g + 1) * 4,
                                                                1 + tt * 128:1 + (tt + 1) * 128]
                p.emit("vector", lambda v, bank=bank, xn4=xn4: v.tensor_copy(
                    xn4, bank[:, 0:512].rearrange("p (c t) -> p c t", c=4)),
                    waits=[(s_tr, tt * 2 + g + 1)], incs=[(s_ev, 1)])
        self.p0_done = [(s_ev, s_ev.n)]
        dbg = self.dbg_out("xn", (128, 8 * (T + 1)), BF16)
        if dbg is not None:
            p.emit("sync", lambda sy: sy.dma_start(out=dbg, in_=self.xn),
                   waits=[(s_ev, None)], incs=[(self.sem("dbg"), 16)])
        self.af.reset(m)
        self.ab.reset(mb)

    def load_consts(self):
        p, d = self.p, self.dram
        s_c = self.sem("consts")
        self.identb = self.ab.alloc(128)
        self.identf = self.af.alloc(128)
        s_cg = self.sem("consts_g")
        p.emit("gpsimd", lambda gp: gp.dma_start(out=self.identb, in_=d["ident"]), incs=[(s_cg, 16)])
        p.emit("sync", lambda sy: sy.dma_start(out=self.identf, in_=d["ident"]), incs=[(s_c, 16)])
        self.maskA = self.af.alloc(512)
        p.emit("sync", lambda sy: sy.dma_start(out=self.maskA, in_=d["maskA"]), incs=[(s_c, 16)])
        self.mask_low = self.af.alloc(128)
        self.mask_up = self.af.alloc(128)
        p.emit("sync", lambda sy: sy.dma_start(out=self.mask_low, in_=d["mask_low"]), incs=[(s_c, 16)])
        p.emit("sync", lambda sy: sy.dma_start(out=self.mask_up, in_=d["mask_up"]), incs=[(s_c, 16)])
        self.vecs = {}
        for nm in ("tmx", "tmw", "tmk", "tmv", "tmr", "tmg", "tdecay", "u_", "lnw_", "lnb_"):
            t = self.af.alloc(8)
            self.vecs[nm] = t
            p.emit("sync", lambda sy, t=t, nm=nm: sy.dma_start(out=t, in_=d[nm]), incs=[(s_c, 16)])
        self.blockones = self.ab.alloc(2)
        def bo_f(v):
            v.memset(self.blockones[0:64, 0:1], 1.0)
            v.memset(self.blockones[64:128, 0:1], 0.0)
            v.memset(self.blockones[0:64, 1:2], 0.0)
            return v.memset(self.blockones[64:128, 1:2], 1.0)
        s_cv = self.sem("consts_v")
        p.emit("vector", bo_f, incs=[(s_cv, 1)])
        self.onescol = self.ab.alloc(8)
        p.emit("vector", lambda v: v.memset(self.onescol, 1.0), incs=[(s_cv, 1)])
        self.cc = {}
        for val in (LN_EPS, GN_EPS, 1e-30, 0.0):
            t = self.af.alloc(1)
            self.cc[val] = t
            p.emit("vector", lambda v, t=t, val=val: v.memset(t, val), incs=[(s_cv, 1)])
        self.c_sem_target = s_c.n
        self.cv_sem_target = s_cv.n
        self.cg_sem_target = s_cg.n
        self.xn_mark = self.ab.mark()
        self.xn = self.ab.alloc(8 * (self.T + 1))

    def load_rwkv_weights(self):
        p, d = self.p, self.dram
        self.wmark_b = self.ab.mark()
        s_w = self.sem("rwkv_w")
        self.Wslot = [self.ab.alloc(8 * C), self.ab.alloc(8 * C)]
        for slot, nm in ((0, "w_k"), (1, "w_r")):
            t = self.Wslot[slot]
            p.emit("gpsimd", lambda gp, t=t, nm=nm: gp.dma_start(
                out=t.rearrange("p (g n) -> p g n", g=8),
                in_=d[nm].rearrange("(g p) n -> p g n", p=128)), incs=[(s_w, 16)])
        self.maa_w1 = self.ab.alloc(8 * 160)
        p.emit("gpsimd", lambda gp: gp.dma_start(
            out=self.maa_w1.rearrange("p (g n) -> p g n", g=8),
            in_=d["maa_w1"].rearrange("(g p) n -> p g n", p=128)), incs=[(s_w, 16)])
        self.maa_w2 = self.ab.alloc(5 * C)
        p.emit("gpsimd", lambda gp: gp.dma_start(
            out=self.maa_w2[0:32, :], in_=d["maa_w2"]), incs=[(s_w, 16)])
        self.td_w2 = self.ab.alloc(C)[0:64, :]
        p.emit("gpsimd", lambda gp: gp.dma_start(
            out=self.td_w2, in_=d["td_w2"]), incs=[(s_w, 16)])
        self.td_w1 = self.ab.alloc(8 * 64)
        p.emit("gpsimd", lambda gp: gp.dma_start(
            out=self.td_w1.rearrange("p (g n) -> p g n", g=8),
            in_=d["td_w1"].rearrange("(g p) n -> p g n", p=128)), incs=[(s_w, 16)])
        self.rwkv_w = s_w

    def phase1(self):
        p, T, d = self.p, self.T, self.dram
        TS = 256
        NTIL = T // TS
        self.dvec = self.af.alloc(self.NTT * 16)
        self.e3 = self.af.alloc(8 * self.NCHK)
        m = self.af.mark(); mb = self.ab.mark()
        xx2 = [self.af.alloc(8 * TS) for _ in range(2)]
        kf = self.af.alloc(8 * TS)
        ld = self.af.alloc(8 * TS)
        q = self.af.alloc(8 * TS)
        teb = [self.af.alloc(TS) for _ in range(2)]
        e1b = [self.af.alloc(TS) for _ in range(2)]
        e2b = [self.af.alloc(TS) for _ in range(2)]
        zbb = [self.ab.alloc(TS) for _ in range(2)]
        qlb = self.af.alloc(2)
        xxx1 = self.ab.alloc(8 * TS)
        oper = [self.ab.alloc(8 * TS) for _ in range(3)]
        t5f1 = [self.ab.alloc(TS) for _ in range(5)]
        tw = self.ab.alloc(TS)
        rkub = self.ab.alloc(TS)
        fRQ = self.ab.alloc(8 * TS)
        fKK = self.ab.alloc(8 * TS)
        fV = fRQ
        fG = fKK
        msb2 = [self.af.alloc(TS) for _ in range(2)]
        rrb2 = [self.af.alloc(TS) for _ in range(2)]
        sv = self.sem("p1_v")
        sa = self.sem("p1_a")
        sm = self.sem("p1_m")
        sl = self.sem("p1_pool")
        sst = self.sem("p1_st")
        sw2a = self.sem("p1_w2a")
        sw2b = self.sem("p1_w2b")

        s_zro = self.sem("p1_zro")
        p.emit("vector", lambda v: v.memset(fKK, 0.0), waits=list(self.p0_done), incs=[(sv, 1)])
        vv0 = d["rq_d"].rearrange("(c p) t -> p c t", c=8)
        for cz in range(8):
            p.emit("sync", lambda sy, cz=cz: sy.dma_start(
                out=vv0[64:192, cz, :], in_=fKK[:, 0:T]),
                waits=[(sv, None)], incs=[(s_zro, 16)])
        zro_t = s_zro.n

        xxx_done_t = {}

        def emit_xxx(it, pass2):
            t0 = it * TS
            G = self.guard
            par = it % 2
            xx = xx2[par]
            for ct in range(8):
                w0 = G(("xx", par, ct)) + (self.p0_done + [
                    (self.sem("consts"), self.c_sem_target),
                    (self.sem("consts_v"), self.cv_sem_target)]
                    if (it == 0 and not pass2 and ct == 0) else [])
                p.emit("vector", lambda v, ct=ct, t0=t0, xx=xx: v.tensor_tensor(
                    xx[:, ct * TS:(ct + 1) * TS], self.xn_view(ct, t0, TS, shift=1),
                    self.xn_view(ct, t0, TS), ALU.subtract),
                    waits=w0, incs=[(sv, 1)])
                p.emit("vector", lambda v, ct=ct, t0=t0, xx=xx: v.scalar_tensor_tensor(
                    xxx1[:, ct * TS:(ct + 1) * TS], xx[:, ct * TS:(ct + 1) * TS],
                    self.vecs["tmx"][:, ct:ct + 1], self.xn_view(ct, t0, TS),
                    ALU.mult, ALU.add), waits=G(("xxxbuf",)) + [(sv, None)], incs=[(sv, 1)])
            xxx_done_t[it] = sv.n

        def tile_pass(it, pass2):
            t0 = it * TS
            c0g = (it * TS) // L
            G = self.guard; SG = self.set_guard
            par = it % 2
            xx = xx2[par]
            xxx = xxx1
            t5f = t5f1
            xxx_done = xxx_done_t[it]
            for f5 in range(5):
                pb = self.ps[f5 % 2]
                po = 0
                def t5mm(t, f5=f5, pb=pb, po=po):
                    last = None
                    for k in range(8):
                        last = t.matmul(pb.ap()[0:32, po:po + TS],
                                        self.maa_w1[:, k * 160 + f5 * 32:k * 160 + f5 * 32 + 32],
                                        xxx[:, k * TS:(k + 1) * TS],
                                        start=(k == 0), stop=(k == 7))
                    return last
                p.emit("tensor", t5mm, waits=[(sv, xxx_done), (self.rwkv_w, None)] +
                       G(("t5ps", f5 % 2)), incs=[(sm, 1)])
                p.emit("scalar", lambda sc, f5=f5, pb=pb, po=po: sc.activation(
                    t5f[f5][0:32, :], pb.ap()[0:32, po:po + TS], AF.Tanh),
                    waits=[(sm, None)], incs=[(sa, 1)])
                SG(("t5ps", f5 % 2), sa)
            SG(("xxxbuf",), sm)
            fmap = {0: "tmw", 1: "tmk", 2: "tmv", 3: "tmr", 4: "tmg"}
            fs = [2, 4] if pass2 else [0, 1, 3]
            t5_a = {f5: None for f5 in range(5)}
            for fi, f in enumerate(fs):
                nmv = fmap[f]
                ob = oper[fi] if not pass2 else oper[fi % 2]
                t5a_cnt = sa.n
                for ct in range(8):
                    pb = self.ps[2 + ct % 2]
                    p.emit("tensor", lambda t, f=f, ct=ct, pb=pb: t.matmul(
                        pb.ap()[:, 0:TS], self.maa_w2[0:32, f * C + ct * 128:f * C + ct * 128 + 128],
                        t5f[f][0:32, :], start=True, stop=True),
                        waits=[(sa, t5a_cnt)] + G(("pk", ct % 2)), incs=[(sm, 1)])
                    msb = msb2[ct % 2]
                    p.emit("scalar", lambda sc, pb=pb, msb=msb: sc.activation(
                        msb, pb.ap()[:, 0:TS], AF.Copy),
                        waits=[(sm, None)] + G(("msb", ct % 2)), incs=[(sa, 1)])
                    SG(("pk", ct % 2), sa)
                    zb = zbb[ct % 2]
                    p.emit("vector", lambda v, ct=ct, zb=zb, msb=msb, nmv=nmv, xx=xx: v.scalar_tensor_tensor(
                        zb, msb, self.vecs[nmv][:, ct:ct + 1],
                        xx[:, ct * TS:(ct + 1) * TS], ALU.add, ALU.mult),
                        waits=[(sa, None)] + G(("zbb", ct % 2)), incs=[(sv, 1)])
                    SG(("msb", ct % 2), sv)
                    p.emit("vector", lambda v, ct=ct, zb=zb, ob=ob, t0=t0: v.tensor_tensor(
                        ob[:, ct * TS:(ct + 1) * TS], zb, self.xn_view(ct, t0, TS), ALU.add),
                        waits=[(sv, None)] + G(("oper", ob.offset, ct)), incs=[(sl, 1)])
                    SG(("zbb", ct % 2), sl)
            oper_done = sl.n
            if not pass2:
                ob = oper[0]
                def twmm(t):
                    last = None
                    for k in range(8):
                        last = t.matmul(self.pw[0].ap()[0:64, 0:TS],
                                        self.td_w1[:, k * 64:(k + 1) * 64],
                                        ob[:, k * TS:(k + 1) * TS], start=(k == 0), stop=(k == 7))
                    return last
                p.emit("tensor", twmm, waits=[(sl, oper_done - 16)] + G(("pw0",)), incs=[(sm, 1)])
                p.emit("scalar", lambda sc: sc.activation(tw[0:64, :], self.pw[0].ap()[0:64, 0:TS],
                                                          AF.Tanh), waits=[(sm, None)], incs=[(sa, 1)])
                SG(("pw0",), sa)
                tw_done = sa.n
                for ct in range(8):
                    pb = self.ps[2 + ct % 2]
                    p.emit("tensor", lambda t, ct=ct, pb=pb: t.matmul(
                        pb.ap()[:, 0:TS], self.td_w2[:, ct * 128:(ct + 1) * 128], tw[0:64, :],
                        start=True, stop=True),
                        waits=[(sa, tw_done)] + G(("pk", ct % 2)), incs=[(sm, 1)])
                    p.emit("scalar", lambda sc, ct=ct, pb=pb: sc.activation(
                        ld[:, ct * TS:(ct + 1) * TS], pb.ap()[:, 0:TS], AF.Exp,
                        bias=self.vecs["tdecay"][:, ct:ct + 1]),
                        waits=[(sm, None)] + G(("ld", ct)), incs=[(sa, 1)])
                    SG(("pk", ct % 2), sa)
                    for cc_ in range(TS // L):
                        p.emit("vector", lambda v, ct=ct, cc_=cc_: v.tensor_tensor_scan(
                            q[:, ct * TS + cc_ * L: ct * TS + (cc_ + 1) * L],
                            ld[:, ct * TS + cc_ * L: ct * TS + (cc_ + 1) * L],
                            self.cc[0.0].broadcast_to([128, L]), 0.0, ALU.add, ALU.add),
                            waits=[(sa, None)] + G(("q", ct, cc_)), incs=[(sv, 1)])
                    p.emit("vector", lambda v, ct=ct: v.tensor_copy(
                        qlb, q[:, ct * TS + L - 1: (ct + 1) * TS: L]),
                        waits=[(sv, None)] + G(("qlb",)), incs=[(sv, 1)])
                    p.emit("scalar", lambda sc, ct=ct, c0g=c0g: sc.activation(
                        self.e3[:, ct * self.NCHK + c0g: ct * self.NCHK + c0g + TS // L],
                        qlb, AF.Exp, scale=-1.0),
                        waits=[(sv, None)], incs=[(sa, 1)])
                    SG(("qlb",), sa)
                if it + 1 < NTIL:
                    emit_xxx(it + 1, pass2)
                for ct in range(8):
                    pb = self.ps[2 + ct % 2]
                    ob1 = oper[1]
                    def kmm(t, ct=ct, pb=pb, ob1=ob1):
                        last = None
                        for k in range(8):
                            last = t.matmul(pb.ap()[:, 0:TS],
                                            self.Wslot[0][:, k * C + ct * 128: k * C + ct * 128 + 128],
                                            ob1[:, k * TS:(k + 1) * TS], start=(k == 0), stop=(k == 7))
                        return last
                    p.emit("tensor", kmm, waits=[(sl, oper_done - 8), (s_zro, zro_t)] +
                           G(("pk", ct % 2)), incs=[(sm, 1)])
                    p.emit("scalar", lambda sc, ct=ct, pb=pb: sc.activation(
                        kf[:, ct * TS:(ct + 1) * TS], pb.ap()[:, 0:TS], AF.Copy),
                        waits=[(sm, None)] + G(("kf", ct)), incs=[(sa, 1)])
                    SG(("pk", ct % 2), sa)
                    p.emit("scalar", lambda sc, ct=ct, e2=e2b[ct % 2]: sc.activation(
                        e2, q[:, ct * TS:(ct + 1) * TS], AF.Exp),
                        waits=[(sv, None)] + G(("e2b", ct % 2)), incs=[(sa, 1)])
                    p.emit("vector", lambda v, ct=ct, e2=e2b[ct % 2]: v.tensor_tensor(
                        fKK[:, ct * TS:(ct + 1) * TS], kf[:, ct * TS:(ct + 1) * TS], e2, ALU.mult),
                        waits=[(sa, None)] + G(("fKK", ct)), incs=[(sv, 1)])
                    SG(("e2b", ct % 2), sv)
                    SG(("q", ct, 0), sv); SG(("q", ct, 1), sv)
                for ct in range(8):
                    pb = self.ps[2 + ct % 2]
                    ob2 = oper[2]
                    def rmm(t, ct=ct, pb=pb, ob2=ob2):
                        last = None
                        for k in range(8):
                            last = t.matmul(pb.ap()[:, 0:TS],
                                            self.Wslot[1][:, k * C + ct * 128: k * C + ct * 128 + 128],
                                            ob2[:, k * TS:(k + 1) * TS], start=(k == 0), stop=(k == 7))
                        return last
                    p.emit("tensor", rmm, waits=[(sl, oper_done)] + G(("pk", ct % 2)), incs=[(sm, 1)])
                    p.emit("vector", lambda v, ct=ct, te=teb[ct % 2]: v.tensor_tensor(
                        te, ld[:, ct * TS:(ct + 1) * TS], q[:, ct * TS:(ct + 1) * TS], ALU.subtract),
                        waits=[(sv, None)] + G(("teb", ct % 2)), incs=[(sv, 1)])
                    SG(("ld", ct), sv)
                    p.emit("scalar", lambda sc, ct=ct, te=teb[ct % 2], e1=e1b[ct % 2]: sc.activation(
                        e1, te, AF.Exp), waits=[(sv, None)] + G(("e1b", ct % 2)), incs=[(sa, 1)])
                    SG(("teb", ct % 2), sa)
                    rrb = rrb2[ct % 2]
                    p.emit("scalar", lambda sc, pb=pb, rrb=rrb: sc.activation(
                        rrb, pb.ap()[:, 0:TS], AF.Copy),
                        waits=[(sm, None)] + G(("rrb", ct % 2)), incs=[(sa, 1)])
                    SG(("pk", ct % 2), sa)
                    p.emit("vector", lambda v, ct=ct, rrb=rrb, e1=e1b[ct % 2]: v.tensor_tensor(
                        fRQ[:, ct * TS:(ct + 1) * TS], rrb, e1, ALU.mult),
                        waits=[(sa, None)] + G(("fRQ", ct)), incs=[(sv, 1)])
                    p.emit("vector", lambda v, ct=ct, rrb=rrb: v.scalar_tensor_tensor(
                        rkub, rrb, self.vecs["u_"][:, ct:ct + 1],
                        kf[:, ct * TS:(ct + 1) * TS], ALU.mult, ALU.mult),
                        waits=[(sv, None)] + G(("rkub",)), incs=[(sv, 1)])
                    SG(("rrb", ct % 2), sv)
                    SG(("e1b", ct % 2), sv)
                    SG(("kf", ct), sv)
                    for ts_ in range(2):
                        tt = it * 2 + ts_
                        p.emit("tensor", lambda t, ts_=ts_, ct=ct: t.matmul(
                            self.pw[1].ap()[:, 960:962], rkub[:, ts_ * 128:(ts_ + 1) * 128],
                            self.blockones, start=True, stop=True),
                            waits=[(sv, None)] + G(("pw1d",)), incs=[(sm, 1)])
                        p.emit("vector", lambda v, tt=tt, ct=ct: v.tensor_copy(
                            self.dvec[:, tt * 16 + 2 * ct: tt * 16 + 2 * ct + 2],
                            self.pw[1].ap()[:, 960:962]),
                            waits=[(sm, None)], incs=[(sv, 1)])
                        SG(("pw1d",), sv)
                    SG(("rkub",), sm)
                vv = d["rq_d"].rearrange("(c p) t -> p c t", c=8)
                p.emit("sync", lambda sy, t0=t0, vv=vv: sy.dma_start(
                    out=vv[0:64, :, t0:t0 + TS],
                    in_=fRQ[0:64, :].rearrange("p (g t) -> p g t", g=8)),
                    waits=[(sv, None)], incs=[(sst, 16)])
                p.emit("sync", lambda sy, t0=t0, vv=vv: sy.dma_start(
                    out=vv[192:256, :, t0:t0 + TS],
                    in_=fRQ[64:128, :].rearrange("p (g t) -> p g t", g=8)),
                    waits=[(sv, None)], incs=[(sst, 16)])
                p.emit("sync", lambda sy, it=it: sy.dma_start(
                    out=d["kk_d"].rearrange("(g p) t -> p g t", p=128)[:, :, t0:t0 + TS],
                    in_=fKK.rearrange("p (g t) -> p g t", g=8)),
                    waits=[(sv, None)], incs=[(sst, 16)])
                sst_t = sst.n
                for ct in range(8):
                    SG(("fRQ", ct), sst); SG(("fKK", ct), sst)
                    SG(("xx", par, ct), sl)
            else:
                if it + 1 < NTIL:
                    emit_xxx(it + 1, pass2)
                for fi, (f, dst, silu) in enumerate(((2, fV, False), (4, fG, True))):
                    ob = oper[fi % 2]
                    wslot = self.Wslot[0] if f == 2 else self.Wslot[1]
                    for ts_ in range(2):
                        pb = self.pw[ts_ % 2]
                        def vmm(t, ob=ob, wslot=wslot, ts_=ts_, pb=pb):
                            last = None
                            for k in range(8):
                                for nh in range(2):
                                    last = t.matmul(
                                        pb.ap()[:, nh * 512:(nh + 1) * 512],
                                        ob[:, k * TS + ts_ * 128: k * TS + ts_ * 128 + 128],
                                        wslot[:, k * C + nh * 512: k * C + (nh + 1) * 512],
                                        start=(k == 0), stop=(k == 7))
                            return last
                        p.emit("tensor", vmm,
                               waits=[(sl, oper_done - 16 + (fi + 1) * 8),
                                      ((sw2a if f == 2 else sw2b), 16)] + G(("pwv", ts_ % 2)),
                               incs=[(sm, 1)])
                        if not silu:
                            p.emit("scalar", lambda sc, ts_=ts_, pb=pb, dst=dst: sc.activation(
                                dst[:, ts_ * C:(ts_ + 1) * C], pb.ap()[:, 0:C], AF.Copy),
                                waits=[(sm, None)] + G(("fV", ts_)), incs=[(sa, 1)])
                            SG(("pwv", ts_ % 2), sa)
                        else:
                            th = kf[:, ts_ * C:(ts_ + 1) * C]
                            p.emit("scalar", lambda sc, ts_=ts_, pb=pb, th=th: sc.activation(
                                th, pb.ap()[:, 0:C], AF.Tanh, scale=0.5),
                                waits=[(sm, None)] + G(("thv", ts_)), incs=[(sa, 1)])
                            p.emit("vector", lambda v, ts_=ts_, pb=pb, th=th, dst=dst: v.scalar_tensor_tensor(
                                dst[:, ts_ * C:(ts_ + 1) * C], th, 1.0, pb.ap()[:, 0:C],
                                ALU.add, ALU.mult),
                                waits=[(sa, None)] + G(("fG", ts_)), incs=[(sv, 1)])
                            SG(("pwv", ts_ % 2), sv)
                            SG(("thv", ts_), sv)
                p.emit("sync", lambda sy, it=it: sy.dma_start(
                    out=d["v_d"][t0:t0 + TS, :].rearrange("(g p) c -> p g c", p=128),
                    in_=fV.rearrange("p (g c) -> p g c", g=2)),
                    waits=[(sa, None)], incs=[(sst, 16)])
                p.emit("sync", lambda sy, it=it: sy.dma_start(
                    out=d["g2x_d"][t0:t0 + TS, :].rearrange("(g p) c -> p g c", p=128),
                    in_=fG.rearrange("p (g c) -> p g c", g=2)),
                    waits=[(sv, None)], incs=[(sst, 16)])
                for ts_ in range(2):
                    SG(("fV", ts_), sst); SG(("fG", ts_), sst)
                for ct in range(8):
                    SG(("xx", par, ct), sl)

        emit_xxx(0, False)
        for it in range(NTIL):
            tile_pass(it, pass2=False)
        for ts_ in range(2):
            self.set_guard(("fV", ts_), sst)
            self.set_guard(("fG", ts_), sst)
            self.set_guard(("thv", ts_), sv)
            self.set_guard(("pwv", ts_), sv)
        p.emit("gpsimd", lambda gp: gp.dma_start(
            out=self.Wslot[0].rearrange("p (g n) -> p g n", g=8),
            in_=d["w_v"].rearrange("(g p) n -> p g n", p=128)),
            waits=[(sm, None)], incs=[(sw2a, 16)])
        p.emit("gpsimd", lambda gp: gp.dma_start(
            out=self.Wslot[1].rearrange("p (g n) -> p g n", g=8),
            in_=d["w_g"].rearrange("(g p) n -> p g n", p=128)),
            waits=[(sm, None)], incs=[(sw2b, 16)])
        emit_xxx(0, True)
        for it in range(NTIL):
            tile_pass(it, pass2=True)
        self.p1_done = [(sst, sst.n), (sv, sv.n), (sa, sa.n), (sm, sm.n), (sl, sl.n)]
        for nm, buf in (("dvec", self.dvec), ("e3", self.e3)):
            dbg = self.dbg_out(nm, (128, buf.shape[1]))
            if dbg is not None:
                p.emit("sync", lambda sy, dbg=dbg, buf=buf: sy.dma_start(out=dbg, in_=buf),
                       waits=self.p1_done, incs=[(self.sem("dbg"), 16)])
        self.af.reset(m); self.ab.reset(mb)

    def prefetch_ffn0(self):
        p, d = self.p, self.dram
        s = self.sem("p4_gw")
        p.emit("gpsimd", lambda gp: gp.dma_start(
            out=self.wq_lo.rearrange("p (g n) -> p g n", g=3),
            in_=d["wqkv"][0].rearrange("(g p) n -> p g n", p=128)[:, 5:8, :]),
            waits=list(self.p1_done), incs=[(s, 16)])
        p.emit("gpsimd", lambda gp: gp.dma_start(
            out=self.wo2.rearrange("p (g n) -> p g n", g=4),
            in_=d["wo"][0].rearrange("(g p) n -> p g n", p=128)),
            waits=[], incs=[(s, 16)])

    def phase2(self, parts=31):
        p, T, d = self.p, self.T, self.dram
        NC = self.NCHK
        self.ab.reset(self.wmark_b)
        m = self.af.mark(); mb = self.ab.mark()
        self.yng = self.ab.alloc(self.NTT * C)
        self.p2_streams_mark = self.ab.mark()
        rqc = [self.ab.alloc(2 * C) for _ in range(2)]
        kkc = [self.ab.alloc(C) for _ in range(2)]
        vc = [self.ab.alloc(C) for _ in range(2)]
        gc = [self.ab.alloc(C) for _ in range(2)]
        s_rq = [self.sem(f"p2_rq{i}") for i in range(2)]
        s_kk = [self.sem(f"p2_kk{i}") for i in range(2)]
        s_v = [self.sem(f"p2_v{i}") for i in range(2)]
        s_g = [self.sem(f"p2_g{i}") for i in range(2)]
        kkt = self.ab.alloc(C)
        absb = self.ab.alloc(2048)
        Sst = self.af.alloc(8 * DH)
        stp = self.af.alloc(8 * DH)
        Sbf = self.ab.alloc(8 * DH)
        ysb = self.af.alloc(C)
        sqb = self.af.alloc(C)
        self.lnw_rep = self.af.alloc(C)
        self.lnb_rep = self.af.alloc(C)
        s_lnr = self.sem("p2_lnr")
        p.emit("sync", lambda sy: sy.dma_start(out=self.lnw_rep, in_=d["lnw_rep"]),
               waits=list(self.p1_done), incs=[(s_lnr, 16)])
        p.emit("sync", lambda sy: sy.dma_start(out=self.lnb_rep, in_=d["lnb_rep"]),
               waits=[], incs=[(s_lnr, 16)])
        st16 = self.af.alloc(7 * 16)
        col16 = lambda j: st16[:, j * 16:(j + 1) * 16]
        sm = self.sem("p2_mm"); sv = self.sem("p2_dv"); sa = self.sem("p2_act"); sl = self.sem("p2_pl")
        p.emit("vector", lambda v: v.memset(Sst, 0.0), waits=self.p1_done, incs=[(sv, 1)])
        wave_consumed = [None] * (NC + 2)

        def emit_loads(c, ldw):
            sl_ = c % 2
            p.emit("sync", lambda sy, c=c, b=rqc[sl_]: sy.dma_start(
                out=b.rearrange("p (g t) -> p g t", g=16),
                in_=d["rq_d"].rearrange("(g p) t -> p g t", p=128)[:, :, c * L:(c + 1) * L]),
                waits=ldw, incs=[(s_rq[sl_], 16)])
            p.emit("sync", lambda sy, c=c, b=kkc[sl_]: sy.dma_start(
                out=b.rearrange("p (g t) -> p g t", g=8),
                in_=d["kk_d"].rearrange("(g p) t -> p g t", p=128)[:, :, c * L:(c + 1) * L]),
                waits=ldw, incs=[(s_kk[sl_], 16)])
            p.emit("sync", lambda sy, c=c, b=vc[sl_]: sy.dma_start(
                out=b, in_=d["v_d"][c * L:(c + 1) * L, :]),
                waits=ldw, incs=[(s_v[sl_], 16)])
            p.emit("sync", lambda sy, c=c, b=gc[sl_]: sy.dma_start(
                out=b, in_=d["g2x_d"][c * L:(c + 1) * L, :]),
                waits=ldw, incs=[(s_g[sl_], 16)])

        emit_loads(0, list(self.p1_done))
        if NC > 1:
            emit_loads(1, list(self.p1_done))
        for c in range(NC):
            sl_ = c % 2
            RQ, KK, V, G = rqc[sl_], kkc[sl_], vc[sl_], gc[sl_]
            pw0b = self.pw[0].ap().bitcast(BF16)
            def trf(t, KK=KK):
                last = None
                for j in range(8):
                    last = t.transpose(pw0b[:, j * 128:(j + 1) * 128],
                                       KK[:, j * 128:(j + 1) * 128], self.identb)
                return last
            if not (parts & 1):
                continue
            p.emit("tensor", trf, waits=[(s_kk[sl_], s_kk[sl_].n)] + self.guard(("p2pw0",)),
                   incs=[(sm, 1)])
            p.emit("scalar", lambda sc: sc.activation(kkt, pw0b[:, 0:1024], AF.Copy),
                   waits=[(sm, None)], incs=[(sa, 1)])
            kkt_done = sa.n
            p.emit("scalar", lambda sc: sc.activation(Sbf, Sst, AF.Copy),
                   waits=[(sv, None), (sm, None)], incs=[(sa, 1)])
            sbf_done = sa.n
            bank_heads = [(self.ps[0], (0, 2, 4, 6)), (self.ps[1], (8, 10, 12, 14)),
                          (self.ps[2], (1, 3, 5, 7)), (self.ps[3], (9, 11, 13, 15))]
            for bank, heads in (bank_heads if parts & 2 else []):
                def amm(t, bank=bank, heads=heads, RQ=RQ, KK=KK):
                    last = None
                    for hh, h in enumerate(heads):
                        ct = h // 2; r0 = 64 * (h % 2)
                        last = t.matmul(bank.ap()[:, hh * 128:(hh + 1) * 128],
                                        KK[r0:r0 + 64, ct * 128:(ct + 1) * 128],
                                        RQ[r0:r0 + 64, h * 128:(h + 1) * 128],
                                        start=True, stop=True)
                    return last
                p.emit("tensor", amm, waits=[(s_rq[sl_], s_rq[sl_].n), (s_kk[sl_], s_kk[sl_].n)] +
                       self.guard(("Aps", bank_heads.index((bank, heads)) if False else id(bank))),
                       incs=[(sm, 1)])
                p.emit("vector", lambda v, bank=bank, heads=heads: v.tensor_tensor(
                    absb.rearrange("p (h t) -> p h t", h=16)[:, heads[0]:heads[0] + 7:2, :],
                    bank.ap()[:].rearrange("p (h t) -> p h t", h=4),
                    self.maskA.rearrange("p (h t) -> p h t", h=4), ALU.mult),
                    waits=[(sm, None)], incs=[(sv, 1)])
                self.set_guard(("Aps", id(bank)), sv)
            aev_done = sv.n
            for h in (range(16) if parts & 4 else []):
                ct = h // 2; r0 = 64 * (h % 2)
                def ymm(t, h=h, ct=ct, r0=r0, RQ=RQ, V=V):
                    t.matmul(self.pw[0].ap()[:, h * 64:(h + 1) * 64],
                             absb[:, h * 128:(h + 1) * 128],
                             V[:, h * 64:(h + 1) * 64], start=True, stop=False)
                    return t.matmul(self.pw[0].ap()[:, h * 64:(h + 1) * 64],
                                    RQ[:, h * 128:(h + 1) * 128],
                                    Sbf[:, ct * 64:(ct + 1) * 64],
                                    start=False, stop=True)
                p.emit("tensor", ymm, waits=[(sv, aev_done), (sa, sbf_done), (s_v[sl_], s_v[sl_].n)] +
                       self.guard(("p2pw0",)), incs=[(sm, 1)])
            y_done = sm.n
            for h in (range(16) if parts & 4 else []):
                ct = h // 2; r0 = 64 * (h % 2)
                p.emit("tensor", lambda t, h=h, ct=ct, r0=r0, V=V: t.matmul(
                    self.pw[1].ap()[0:64, h * 64:(h + 1) * 64],
                    kkt[:, ct * 128 + r0: ct * 128 + r0 + 64],
                    V[:, h * 64:(h + 1) * 64], start=True, stop=True),
                    waits=[(sa, kkt_done)], incs=[(sm, 1)])
            u_done = sm.n
            for h in (range(16) if parts & 8 else []):
                p.emit("vector", lambda v, h=h, c=c, V=V: v.scalar_tensor_tensor(
                    ysb[:, h * 64:(h + 1) * 64], V[:, h * 64:(h + 1) * 64],
                    self.dvec[:, c * 16 + h: c * 16 + h + 1],
                    self.pw[0].ap()[:, h * 64:(h + 1) * 64], ALU.mult, ALU.add),
                    waits=[(sm, y_done), (sv, None), (sl, None)], incs=[(sv, 1)])
            yev_done = sv.n
            self.set_guard(("p2pw0",), sv)
            if parts & 8:
                uview = self.pw[1].ap()[0:64, :].rearrange("p (h x e) -> p h x e", h=8, x=2)
                p.emit("vector", lambda v, uview=uview: v.tensor_tensor(
                    stp[0:64, :].rearrange("p (h e) -> p h e", h=8),
                    Sst[0:64, :].rearrange("p (h e) -> p h e", h=8),
                    uview[:, :, 0, :], ALU.add),
                    waits=[(sm, u_done), (sa, sbf_done)], incs=[(sv, 1)])
                p.emit("vector", lambda v, uview=uview: v.tensor_tensor(
                    stp[64:128, :].rearrange("p (h e) -> p h e", h=8),
                    Sst[64:128, :].rearrange("p (h e) -> p h e", h=8),
                    uview[:, :, 1, :], ALU.add),
                    waits=[(sm, u_done), (sv, None)], incs=[(sv, 1)])
                e3v = self.e3.rearrange("p (ct n) -> p ct n", ct=8)[:, :, c:c + 1]
                p.emit("vector", lambda v, e3v=e3v: v.tensor_tensor(
                    Sst.rearrange("p (ct e) -> p ct e", ct=8), stp.rearrange("p (ct e) -> p ct e", ct=8),
                    e3v.broadcast_to([128, 8, DH]), ALU.mult),
                    waits=[(sv, None)], incs=[(sv, 1)])
            if not (parts & 16):
                wave_consumed[c] = (sv.n, sm.n)
                if c + 2 < NC:
                    emit_loads(c + 2, [(sv, sv.n), (sm, sm.n)])
                continue
            p.emit("gpsimd", lambda g_: g_.tensor_tensor(sqb, ysb, ysb, ALU.mult),
                   waits=[(sv, yev_done), (sv, None)], incs=[(sl, 1)])
            ysum, sqs, mu, msq, var, lnv, rstd = (col16(j) for j in range(7))
            p.emit("vector", lambda v: v.tensor_reduce(
                ysum, ysb.rearrange("p (h j) -> p h j", h=16), AX.X, ALU.add),
                waits=[(sv, yev_done)], incs=[(sv, 1)])
            p.emit("vector", lambda v: v.tensor_reduce(
                sqs, sqb.rearrange("p (h j) -> p h j", h=16), AX.X, ALU.add),
                waits=[(sl, None)], incs=[(sv, 1)])
            p.emit("vector", lambda v: v.tensor_scalar_mul(mu, ysum, 1.0 / DH),
                   waits=[(sv, None)], incs=[(sv, 1)])
            p.emit("vector", lambda v: v.tensor_tensor(msq, mu, mu, ALU.mult),
                   waits=[(sv, None)], incs=[(sv, 1)])
            p.emit("vector", lambda v: v.scalar_tensor_tensor(
                var, sqs, 1.0 / DH, msq, ALU.mult, ALU.subtract),
                waits=[(sv, None)], incs=[(sv, 1)])
            p.emit("scalar", lambda sc: sc.activation(lnv, var, AF.Ln, bias=self.cc[GN_EPS]),
                   waits=[(sv, None)], incs=[(sa, 1)])
            p.emit("scalar", lambda sc: sc.activation(rstd, lnv, AF.Exp, scale=-0.5),
                   waits=[(sa, None)], incs=[(sa, 1)])
            for h in range(16):
                p.emit("vector", lambda v, h=h: v.tensor_scalar(
                    sqb[:, h * 64:(h + 1) * 64], ysb[:, h * 64:(h + 1) * 64],
                    mu[:, h:h + 1], rstd[:, h:h + 1], ALU.subtract, ALU.mult),
                    waits=[(sa, None), (sl, None)], incs=[(sv, 1)])
            p.emit("vector", lambda v: v.tensor_tensor(ysb, sqb, self.lnw_rep, ALU.mult),
                   waits=[(sv, None), (sl, None), (self.sem("p2_lnr"), 32)], incs=[(sv, 1)])
            p.emit("vector", lambda v: v.tensor_tensor(sqb, ysb, self.lnb_rep, ALU.add),
                   waits=[(sv, None)], incs=[(sl, 1)])
            p.emit("vector", lambda v, c=c, G=G: v.scalar_tensor_tensor(
                self.yng[:, c * C:(c + 1) * C], sqb, 0.5, G, ALU.mult, ALU.mult),
                waits=[(sl, None), (s_g[sl_], s_g[sl_].n)], incs=[(sv, 1)])
            wave_consumed[c] = (sv.n, sm.n)
            if c + 2 < NC:
                emit_loads(c + 2, [(sv, sv.n), (sm, sm.n)])
        self.p2_done = [(sv, sv.n), (sm, sm.n), (sa, sa.n), (sl, sl.n)]
        dbg = self.dbg_out("yng", (128, self.NTT * C), BF16)
        if dbg is not None:
            p.emit("sync", lambda sy, dbg=dbg: sy.dma_start(out=dbg, in_=self.yng),
                   waits=self.p2_done, incs=[(self.sem("dbg"), 16)])
        self.af.reset(m)
        self.p2_mb = mb

    def phase3(self):
        p, T, d = self.p, self.T, self.dram
        m = self.af.mark()
        self.ab.reset(self.p2_streams_mark)
        yngf = self.ab.alloc(C)
        self.wo_sb = self.ab.alloc(8 * C)
        s_til = [self.af.alloc(C) for _ in range(2)]
        sm = self.sem("p3_mm"); sv = self.sem("p3_dv"); sa = self.sem("p3_act")
        ssts = [self.sem(f"p3_st{i}") for i in range(2)]
        ps3b = self.ps[3].ap().bitcast(BF16)
        s_wo = self.sem("p2_wo")
        p.emit("gpsimd", lambda gp: gp.dma_start(
            out=self.wo_sb.rearrange("p (g n) -> p g n", g=8),
            in_=d["w_o"].rearrange("(g p) n -> p g n", p=128)),
            waits=list(self.p2_done), incs=[(s_wo, 16)])
        for tt in range(self.NTT):
            def trf(t, tt=tt):
                last = None
                for ct in range(8):
                    last = t.transpose(ps3b[:, ct * 128:(ct + 1) * 128],
                                       self.yng[:, tt * C + ct * 128: tt * C + (ct + 1) * 128],
                                       self.identb)
                return last
            w_ = [(sv, None)] + (self.p2_done if tt == 0 else [])
            p.emit("tensor", trf, waits=w_, incs=[(sm, 1)])
            p.emit("vector", lambda v: v.tensor_copy(yngf, ps3b), waits=[(sm, None)], incs=[(sv, 1)])
            def womm(t, tt=tt):
                last = None
                for ct in range(8):
                    for nh in range(2):
                        last = t.matmul(self.pw[0].ap()[:, nh * 512:(nh + 1) * 512],
                                        yngf[:, ct * 128:(ct + 1) * 128],
                                        self.wo_sb[:, ct * C + nh * 512: ct * C + nh * 512 + 512],
                                        start=(ct == 0), stop=(ct == 7))
                return last
            p.emit("tensor", womm, waits=[(sv, None), (s_wo, 16), (sa, None)], incs=[(sm, 1)])
            stl = s_til[tt % 2]
            sst = ssts[tt % 2]
            w_ = [(sm, None)]
            if tt >= 2:
                w_.append((sst, sst.n))
            p.emit("scalar", lambda sc, stl=stl: sc.activation(stl, self.pw[0].ap()[:], AF.Copy),
                   waits=w_, incs=[(sa, 1)])
            p.emit("sync", lambda sy, tt=tt, stl=stl: sy.dma_start(
                out=d["s_d"][tt * 128:(tt + 1) * 128, :], in_=stl),
                waits=[(sa, None)], incs=[(sst, 16)])
        self.p3_done = [(ssts[0], ssts[0].n), (ssts[1], ssts[1].n), (sm, sm.n), (sv, sv.n), (sa, sa.n)]
        self.af.reset(m)
        self.ab.reset(self.p2_mb)

    def phase4(self):
        p, T, d = self.p, self.T, self.dram
        NW = self.NW
        sm = self.sem("p4_mm"); sv = self.sem("p4_dv"); sa = self.sem("p4_act"); sl = self.sem("p4_pl")
        sld = self.sem("p4_ld")
        sgw = self.sem("p4_gw")
        sg1 = [self.sem(f"p4_g1{i}") for i in range(2)]
        sg2 = [self.sem(f"p4_g2{i}") for i in range(2)]
        sh4 = [self.sem(f"p4_h4{i}") for i in range(2)]
        sst4 = self.sem("p4_st")
        mf0 = self.af.mark()
        xf = self.af.alloc(8 * (T + 1))
        self.xf = xf
        for ct in range(8):
            p.emit("vector", lambda v, ct=ct: v.tensor_copy(
                xf[:, ct * (T + 1):(ct + 1) * (T + 1)], self.xn[:, ct * (T + 1):(ct + 1) * (T + 1)]),
                waits=(list(self.p3_done) + [(sv, None)]) if ct == 0 else [], incs=[(sv, 1)])
        xf_init = sv.n
        self.ab.reset(self.xn_mark)
        qtm = self.af.alloc(512); q2b = self.af.alloc(512)
        rot1 = self.af.alloc(512); rot2 = self.af.alloc(512)
        s8 = self.af.alloc(8); l8 = self.af.alloc(8); r8 = self.af.alloc(8)
        r_mu = qtm[:, 0:512]; r_e2 = q2b[:, 0:512]
        r_ms = rot1[:, 0:512]; r_va = rot2[:, 0:512]
        r_rs = self.af.alloc(512); r_msb = self.af.alloc(512)
        onesrow = self.af.alloc(128)
        p.emit("vector", lambda v: v.memset(onesrow[0:1, :], 1.0), incs=[(sv, 1)])
        qsc = self.af.alloc(FDH); ksc = self.af.alloc(FDH)
        lngv = self.af.alloc(8); lnbv = self.af.alloc(8); b2v = self.af.alloc(8)
        b1v = self.af.alloc(44)
        scr1 = self.af.alloc(512); scr2 = self.af.alloc(512)
        rcp4 = self.af.alloc(4)
        m4a = self.ab.mark()

        def attn_stage(l):
            self.ab.reset(m4a)
            wq_hi = self.ab.alloc(5 * 3 * INNER)
            wo_sb2 = self.wo2
            p.emit("gpsimd", lambda gp, l=l: gp.dma_start(
                out=wq_hi.rearrange("p (g n) -> p g n", g=5),
                in_=d["wqkv"][l].rearrange("(g p) n -> p g n", p=128)[:, 0:5, :]),
                waits=[(sm, None), (sv, None)], incs=[(sgw, 16)])
            tabs_raw = {}
            for nm in ("ct_q", "st_q", "ct_k", "st_k"):
                t = self.ab.alloc(self.NTT * FDH)
                tabs_raw[nm] = t
                p.emit("gpsimd", lambda gp, t=t, nm=nm: gp.dma_start(
                    out=t.rearrange("p (g n) -> p g n", g=self.NTT),
                    in_=d[nm].rearrange("(g p) n -> p g n", p=128)),
                    waits=([(sv, xf_init)] + list(self.p3_done)) if l == 0 else
                          [(sm, None), (sv, None)],
                    incs=[(sgw, 16)])
            gw_t = sgw.n
            for nm, t in (("lng", lngv), ("lnb2", lnbv), ("b2", b2v), ("b1p", b1v)):
                p.emit("sync", lambda sy, nm=nm, t=t, l=l: sy.dma_start(out=t, in_=d[nm][l]),
                       waits=self.guard(("smallvec",)), incs=[(sld, 16)])
            ld_t = sld.n
            tabs = tabs_raw
            qscb = self.ab.alloc(FDH)
            kscb = self.ab.alloc(FDH)
            p.emit("gpsimd", lambda gp, l=l: gp.dma_start(out=qscb, in_=d["qscale_rep"][l]),
                   waits=[(sm, None), (sv, None)], incs=[(sgw, 16)])
            p.emit("gpsimd", lambda gp, l=l: gp.dma_start(out=kscb, in_=d["kscale_rep"][l]),
                   waits=[], incs=[(sgw, 16)])
            gw_t = sgw.n
            for nm, scv in (("ct_q", qscb), ("st_q", qscb), ("ct_k", kscb), ("st_k", kscb)):
                p.emit("vector", lambda v, nm=nm, scv=scv: v.tensor_tensor(
                    tabs_raw[nm].rearrange("p (g e) -> p g e", g=self.NTT),
                    tabs_raw[nm].rearrange("p (g e) -> p g e", g=self.NTT),
                    scv.unsqueeze(1).broadcast_to([128, self.NTT, FDH]), ALU.mult),
                    waits=[(sgw, gw_t), (sv, None)], incs=[(sv, 1)])
            zwin = self.ab.alloc(8 * 512)
            kfm = [self.ab.alloc(4 * 512) for _ in range(3)]
            vtl = [self.ab.alloc(4 * 520) for _ in range(3)]
            qfm = self.ab.alloc(4 * 512)
            attn2 = [self.ab.alloc(2560) for _ in range(2)]
            otm = self.ab.alloc(4 * 512)
            b_rotB = otm
            ofm = self.ab.alloc(4 * 512)
            b_sqB = ofm
            q_tm = self.ab.alloc(2048)
            k_tm = self.ab.alloc(2048)
            s32 = self.af.alloc(32)
            xbnd = self.ab.alloc(8 * NW)
            assert self.ab.off <= 46900, f"attn allocs {self.ab.off} hit prefetch region"
            for rwi in range(3):
                p.emit("vector", lambda v, rwi=rwi: v.memset(
                    vtl[rwi].rearrange("p (s h e) -> p s h e", s=4, h=8)[:, :, :, 64:65], 1.0),
                    waits=[(sv, None), (sm, None)], incs=[(sv, 1)])
            p.emit("vector", lambda v: v.tensor_copy(
                xbnd.rearrange("p (c w2) -> p c w2", c=8),
                xf.rearrange("p (c t) -> p c t", c=8, t=T + 1)[:, :, 0:T:512]),
                waits=[(sv, None), (sm, None)], incs=[(sv, 1)])
            xbnd_done = sv.n

            def spans(w):
                out = []
                if w >= 1:
                    for kt in range(4):
                        out.append((kt, (w - 1) % 3, kt, 0, 128 * (kt + 1), "low", 128 * kt))
                for kt in range(4, 8):
                    tq0 = 128 * (kt - 4)
                    out.append((kt, w % 3, kt - 4, tq0, 512 - tq0, "up", 0))
                return out

            ps3b = self.ps[3].ap().bitcast(BF16)

            qk_evac = {}
            qk_done = {}

            def front(w):
                sp = spans(w)
                offs = {}
                o = 0
                for e in sp:
                    offs[e[0]] = o
                    o += e[4]
                for ct in range(8):
                    sh = 1 if ct >= 4 else 0
                    p.emit("vector", lambda v, ct=ct, w=w, sh=sh: v.tensor_copy(
                        zwin[:, ct * 512:(ct + 1) * 512],
                        xf[:, ct * (T + 1) + 1 + w * 512 - sh: ct * (T + 1) + 1 + (w + 1) * 512 - sh]),
                        waits=[(sv, None)] + self.guard(("zwin", ct)), incs=[(sv, 1)])
                for ct in range(4, 8):
                    p.emit("vector", lambda v, ct=ct, w=w: v.tensor_copy(
                        zwin[:, ct * 512:ct * 512 + 1], xbnd[:, ct * NW + w: ct * NW + w + 1]),
                        waits=[(sv, None)], incs=[(sv, 1)])
                zdone = sv.n
                for ts_ in range(4):
                    for wh, pb in ((0, self.ps[0]), (1, self.ps[1]), (2, self.ps[2])):
                        def qkvmm(t, ts_=ts_, wh=wh, pb=pb):
                            last = None
                            for k in range(8):
                                wsrc = (wq_hi[:, k * 1536 + wh * 512: k * 1536 + (wh + 1) * 512]
                                        if k < 5 else
                                        self.wq_lo[:, (k - 5) * 1536 + wh * 512: (k - 5) * 1536 + (wh + 1) * 512])
                                last = t.matmul(pb.ap()[:],
                                                zwin[:, k * 512 + ts_ * 128: k * 512 + ts_ * 128 + 128],
                                                wsrc,
                                                start=(k == 0), stop=(k == 7))
                            return last
                        p.emit("tensor", qkvmm,
                               waits=[(sv, zdone), (sgw, gw_t)] + self.guard(("ps", wh)),
                               incs=[(sm, 1)])
                    p.emit("scalar", lambda sc, ts_=ts_, w=w: sc.activation(
                        vtl[w % 3].rearrange("p (s h e) -> p s h e", s=4, h=8)[:, ts_, :, 0:64],
                        self.ps[2].ap()[:].rearrange("p (h e) -> p h e", h=8), AF.Copy),
                        waits=[(sm, None)] + self.guard(("vtlrd", w % 3)), incs=[(sa, 1)])
                    self.set_guard(("ps", 2), sa)
                    p.emit("scalar", lambda sc, ts_=ts_: sc.activation(
                        q_tm[:, ts_ * 512:(ts_ + 1) * 512], self.ps[0].ap()[:], AF.Copy),
                        waits=[(sm, None)] + self.guard(("q_tm",)), incs=[(sa, 1)])
                    self.set_guard(("ps", 0), sa)
                    p.emit("scalar", lambda sc, ts_=ts_: sc.activation(
                        k_tm[:, ts_ * 512:(ts_ + 1) * 512], self.ps[1].ap()[:], AF.Copy),
                        waits=[(sm, None)] + self.guard(("k_tm",)), incs=[(sa, 1)])
                    self.set_guard(("ps", 1), sa)
                qk_evac[w] = sa.n
                for ct in range(8):
                    self.set_guard(('zwin', ct), sm)
                return sp, offs

            def chains(w):
                    for which, src_tm, ct_t, st_t, dstfm in (
                            (0, q_tm, "ct_q", "st_q", qfm),
                            (1, k_tm, "ct_k", "st_k", kfm[w % 3])):
                        p.emit("scalar", lambda sc, src_tm=src_tm: sc.activation(
                            b_sqB, src_tm, AF.Square),
                            waits=[(sa, qk_evac[w])] + self.guard(("b_sq",)), incs=[(sa, 1)])
                        p.emit("vector", lambda v: v.tensor_reduce(
                            s32, b_sqB.rearrange("p (g e) -> p g e", g=32), AX.X, ALU.add),
                            waits=[(sa, None)] + self.guard(("s32",)), incs=[(sv, 1)])
                        p.emit("scalar", lambda sc: sc.activation(s32, s32, AF.Ln, bias=self.cc[1e-30]),
                               waits=[(sv, None)], incs=[(sa, 1)])
                        p.emit("scalar", lambda sc: sc.activation(s32, s32, AF.Exp, scale=-0.5),
                               waits=[(sa, None)], incs=[(sa, 1)])
                        p.emit("vector", lambda v: v.tensor_scalar_min(s32, s32, 1e12),
                               waits=[(sa, None)], incs=[(sv, 1)])
                        p.emit("vector", lambda v, src_tm=src_tm: v.tensor_tensor(
                            b_sqB.rearrange("p (g e) -> p g e", g=32),
                            src_tm.rearrange("p (g e) -> p g e", g=32),
                            s32.unsqueeze(2).broadcast_to([128, 32, FDH]), ALU.mult),
                            waits=[(sv, None)], incs=[(sv, 1)])
                        ctab = tabs[ct_t][:, w * 256:(w + 1) * 256]
                        stab = tabs[st_t][:, w * 256:(w + 1) * 256]
                        p.emit("vector", lambda v, src_tm=src_tm, ctab=ctab: v.tensor_tensor(
                            src_tm.rearrange("p (s h e) -> p s h e", s=4, h=8),
                            b_sqB.rearrange("p (s h e) -> p s h e", s=4, h=8),
                            ctab.rearrange("p (s e) -> p s e", s=4).unsqueeze(2).broadcast_to(
                                [128, 4, 8, FDH]), ALU.mult),
                            waits=[(sv, None)], incs=[(sv, 1)])
                        for half in range(2):
                            p.emit("vector", lambda v, half=half, stab=stab: v.tensor_tensor(
                                b_rotB.rearrange("p (s h x e) -> p s h x e", s=4, h=8, x=2)[:, :, :, half, :],
                                b_sqB.rearrange("p (s h x e) -> p s h x e", s=4, h=8, x=2)[:, :, :, 1 - half, :],
                                stab.rearrange("p (s x e) -> p s x e", s=4, x=2)[:, :, half, :]
                                .unsqueeze(2).broadcast_to([128, 4, 8, 32]), ALU.mult),
                                waits=[(sv, None)], incs=[(sv, 1)])
                        p.emit("vector", lambda v, src_tm=src_tm: v.tensor_tensor(
                            b_sqB, src_tm, b_rotB, ALU.add),
                            waits=[(sv, None)], incs=[(sv, 1)])
                        fin_v = sv.n
                        for ts_ in range(4):
                            po = which * 512
                            def trq(t, ts_=ts_, po=po):
                                last = None
                                for ic in range(4):
                                    last = t.transpose(ps3b[:, po + ic * 128: po + (ic + 1) * 128],
                                                       b_sqB[:, ts_ * 512 + ic * 128: ts_ * 512 + (ic + 1) * 128],
                                                       self.identb)
                                return last
                            p.emit("tensor", trq, waits=[(sv, fin_v)] + self.guard(("ps3b", which)),
                                   incs=[(sm, 1)])
                            p.emit("vector", lambda v, ts_=ts_, dstfm=dstfm, po=po: v.tensor_copy(
                                dstfm.rearrange("p (ic t2) -> p ic t2", ic=4)[:, :, ts_ * 128:(ts_ + 1) * 128],
                                ps3b[:, po:po + 512].rearrange("p (ic t2) -> p ic t2", ic=4)),
                                waits=[(sm, None)], incs=[(sv, 1)])
                            self.set_guard(("ps3b", which), sv)
                        if which == 0:
                            self.set_guard(("q_tm",), sm)
                        else:
                            self.set_guard(("k_tm",), sm)
                        self.set_guard(("b_sq",), sm)
                        self.set_guard(("s32",), sv)
                    qk_done[w] = sv.n

            def attn(w, sp, offs):
                    sslot = {0: self.pw[1].ap()[:, 0:512], 1: self.pw[1].ap()[:, 512:1024]}
                    mask_done = {}

                    def emit_sblock(h):
                        ab_ = attn2[h % 2]
                        r0 = 64 * (h % 2); ic = h // 2
                        for ei, (kt, rw, ts_, tq0, span, mk, mko) in enumerate(sp):
                            slot = sslot[h % 2][:, 0:span]
                            kf_src = kfm[rw]
                            p.emit("tensor", lambda t, kf_src=kf_src, r0=r0, ic=ic, ts_=ts_, tq0=tq0, span=span, slot=slot: t.matmul(
                                slot, kf_src[r0:r0 + 64, ic * 512 + ts_ * 128: ic * 512 + (ts_ + 1) * 128],
                                qfm[r0:r0 + 64, ic * 512 + tq0: ic * 512 + tq0 + span],
                                start=True, stop=True),
                                waits=[(sv, qk_done[w])] + self.guard(("sslot", h % 2)),
                                incs=[(sm, 1)])
                            oo = offs[kt]
                            p.emit("scalar", lambda sc, slot=slot, ab_=ab_, oo=oo, span=span: sc.activation(
                                ab_[:, oo:oo + span], slot, AF.Exp, scale=8.0),
                                waits=[(sm, None)] + self.guard(("attn2", h % 2)), incs=[(sa, 1)])
                            self.set_guard(("sslot", h % 2), sa)
                            msk = self.mask_low if mk == "low" else self.mask_up
                            p.emit("vector", lambda v, ab_=ab_, oo=oo, mko=mko, msk=msk: v.tensor_tensor(
                                ab_[:, oo + mko: oo + mko + 128],
                                ab_[:, oo + mko: oo + mko + 128], msk, ALU.mult),
                                waits=[(sa, None)], incs=[(sv, 1)])
                        mask_done[h] = sv.n

                    def emit_oblock(h):
                        ab_ = attn2[h % 2]
                        for mi in range(4):
                            oslot = self.pw[0].ap()[:, (mi % 2) * 512:(mi % 2) * 512 + 65]
                            kts = [e for e in sp if e[3] <= mi * 128 < e[3] + e[4]]
                            def omm(t, kts=kts, mi=mi, oslot=oslot, ab_=ab_, h=h, offs=dict(offs)):
                                last = None
                                for j, (kt, rw, ts_, tq0, span, mk, mko) in enumerate(kts):
                                    last = t.matmul(oslot,
                                                    ab_[:, offs[kt] + mi * 128 - tq0: offs[kt] + mi * 128 - tq0 + 128],
                                                    vtl[rw].rearrange("p (s h2 e) -> p s h2 e", s=4, h2=8)[:, ts_, h, :],
                                                    start=(j == 0), stop=(j == len(kts) - 1))
                                return last
                            p.emit("tensor", omm, waits=[(sv, mask_done[h])] + self.guard(("oslot", mi % 2)),
                                   incs=[(sm, 1)])
                            p.emit("vector", lambda v, oslot=oslot, mi=mi: v.reciprocal(
                                rcp4[:, mi:mi + 1], oslot[:, 64:65]),
                                waits=[(sm, None)], incs=[(sv, 1)])
                            p.emit("vector", lambda v, oslot=oslot, mi=mi, h=h: v.tensor_single_scalar(
                                otm[:, mi * 512 + h * 64: mi * 512 + (h + 1) * 64],
                                oslot[:, 0:64], rcp4[:, mi:mi + 1], ALU.mult),
                                waits=[(sv, None)], incs=[(sv, 1)])
                            self.set_guard(("oslot", mi % 2), sv)
                        self.set_guard(("attn2", h % 2), sm)

                    for h in range(8):
                        emit_sblock(h)
                        if h > 0:
                            emit_oblock(h - 1)
                    emit_oblock(7)
                    o_done = sv.n
                    for mi in range(4):
                        po = 0
                        def tro(t, mi=mi, po=po):
                            last = None
                            for ic in range(4):
                                last = t.transpose(ps3b[:, po + ic * 128: po + (ic + 1) * 128],
                                                   otm[:, mi * 512 + ic * 128: mi * 512 + (ic + 1) * 128],
                                                   self.identb)
                            return last
                        p.emit("tensor", tro, waits=[(sv, o_done)] + self.guard(("ps3b", 0)),
                               incs=[(sm, 1)])
                        p.emit("vector", lambda v, mi=mi, po=po: v.tensor_copy(
                            ofm.rearrange("p (ic t2) -> p ic t2", ic=4)[:, :, mi * 128:(mi + 1) * 128],
                            ps3b[:, po:po + 512].rearrange("p (ic t2) -> p ic t2", ic=4)),
                            waits=[(sm, None)], incs=[(sv, 1)])
                        self.set_guard(("ps3b", 0), sv)
                    ofm_done = sv.n
                    for ctO in range(8):
                        wslice = self.pw[1].ap()[:, (ctO % 2) * 512:(ctO % 2) * 512 + 512]
                        def womm(t, ctO=ctO, wslice=wslice):
                            last = None
                            for icg in range(4):
                                last = t.matmul(wslice,
                                                wo_sb2[:, icg * C + ctO * 128: icg * C + ctO * 128 + 128],
                                                ofm[:, icg * 512:(icg + 1) * 512],
                                                start=(icg == 0), stop=(icg == 3))
                            return last
                        p.emit("tensor", womm,
                               waits=[(sv, ofm_done), (sgw, gw_t)] + self.guard(("sslot", ctO % 2)),
                               incs=[(sm, 1)])
                        p.emit("vector", lambda v, ctO=ctO, w=w, wslice=wslice: v.tensor_tensor(
                            xf[:, ctO * (T + 1) + 1 + w * 512: ctO * (T + 1) + 1 + (w + 1) * 512],
                            xf[:, ctO * (T + 1) + 1 + w * 512: ctO * (T + 1) + 1 + (w + 1) * 512],
                            wslice, ALU.add),
                            waits=[(sm, None)], incs=[(sv, 1)])
                        self.set_guard(("sslot", ctO % 2), sv)
                    self.set_guard(("vtlrd", (w + 2) % 3), sm)
                    self.set_guard(("b_sq",), sm)

            state = {}
            state[0] = front(0)
            chains(0)
            for w in range(NW):
                if w + 1 < NW:
                    state[w + 1] = front(w + 1)
                attn(w, *state[w])
                if w + 1 < NW:
                    chains(w + 1)

        def geglu_stage(l):
            self.ab.reset(m4a)
            ge_m = sm.n; ge_v = sv.n
            z1 = self.ab.alloc(8 * 512 * (T // 512))
            zbuf = self.ab.alloc(8 * 512)
            zqb = self.ab.alloc(8 * 512)
            ggb = [self.ab.alloc(512) for _ in range(2)]
            h4b = [self.ab.alloc(512) for _ in range(2)]
            w1q = [self.ab.alloc(8 * 1536) for _ in range(2)]
            sh4st = [self.sem(f"p4_h4st{l}{i}") for i in range(2)]
            NT4 = T // 512
            z1_t = [None] * NT4
            pw0row = self.pw[0].ap()[0:1, 512:1024]
            pw1row = self.pw[1].ap()[0:1, 512:1024]
            pw0bc = self.pw[0].ap()[:, 0:512]
            pw1bc = self.pw[1].ap()[:, 0:512]
            def emit_ln(tt4):
                for ct in range(8):
                    sh = 1 if ct >= 4 else 0
                    w_ = [(sv, None)] + self.guard(("zbuf",))
                    if tt4 == 0 and ct == 0:
                        w_.append((sm, ge_m))
                    p.emit("vector", lambda v, ct=ct, tt4=tt4, sh=sh: v.tensor_copy(
                        zbuf[:, ct * 512:(ct + 1) * 512],
                        xf[:, ct * (T + 1) + 1 + tt4 * 512 - sh: ct * (T + 1) + 1 + (tt4 + 1) * 512 - sh]),
                        waits=w_, incs=[(sv, 1)])
                    p.emit("scalar", lambda sc, ct=ct: sc.activation(
                        zqb[:, ct * 512:(ct + 1) * 512], zbuf[:, ct * 512:(ct + 1) * 512], AF.Square),
                        waits=[(sv, None)] + self.guard(("zqb",)), incs=[(sa, 1)])
                zd = sv.n; zqd = sa.n
                def statmm(t, out, buf):
                    last = None
                    for k in range(8):
                        last = t.matmul(out, self.onescol[:, 0:1],
                                        buf[:, k * 512:(k + 1) * 512], start=(k == 0), stop=(k == 7))
                    return last
                p.emit("tensor", lambda t: statmm(t, pw0row, zbuf),
                       waits=[(sv, zd)] + self.guard(("geps", 0)), incs=[(sm, 1)])
                p.emit("tensor", lambda t: statmm(t, pw1row, zqb),
                       waits=[(sa, zqd)] + self.guard(("geps", 1)), incs=[(sm, 1)])
                mm_t = sm.n
                self.set_guard(("zbuf",), sm)
                self.set_guard(("zqb",), sm)
                p.emit("vector", lambda v: v.tensor_scalar_mul(r_mu[0:1, :], pw0row, 1.0 / C),
                       waits=[(sm, mm_t), (sv, None)], incs=[(sv, 1)])
                p.emit("vector", lambda v: v.tensor_scalar_mul(r_e2[0:1, :], pw1row, 1.0 / C),
                       waits=[(sv, None)], incs=[(sv, 1)])
                self.set_guard(("geps", 0), sv)
                self.set_guard(("geps", 1), sv)
                p.emit("vector", lambda v: v.tensor_tensor(r_ms[0:1, :], r_mu[0:1, :], r_mu[0:1, :], ALU.mult),
                       waits=[(sv, None)], incs=[(sv, 1)])
                p.emit("vector", lambda v: v.tensor_tensor(r_va[0:1, :], r_e2[0:1, :], r_ms[0:1, :], ALU.subtract),
                       waits=[(sv, None)], incs=[(sv, 1)])
                p.emit("scalar", lambda sc: sc.activation(r_va[0:1, :], r_va[0:1, :], AF.Ln,
                                                          bias=self.cc[LN_EPS][0:1, :]),
                       waits=[(sv, None)], incs=[(sa, 1)])
                p.emit("scalar", lambda sc: sc.activation(r_rs[0:1, :], r_va[0:1, :], AF.Exp, scale=-0.5),
                       waits=[(sa, None)], incs=[(sa, 1)])
                p.emit("vector", lambda v: v.tensor_tensor(r_msb[0:1, :], r_mu[0:1, :], r_rs[0:1, :], ALU.mult),
                       waits=[(sa, None), (sv, None)], incs=[(sv, 1)])
                rows_done_v = sv.n; rows_done_a = sa.n
                p.emit("tensor", lambda t: t.matmul(pw0bc, onesrow[0:1, :], r_rs[0:1, :],
                                                    start=True, stop=True),
                       waits=[(sa, rows_done_a), (sv, rows_done_v)] + self.guard(("geps", 2)),
                       incs=[(sm, 1)])
                p.emit("tensor", lambda t: t.matmul(pw1bc, onesrow[0:1, :], r_msb[0:1, :],
                                                    start=True, stop=True),
                       waits=[(sv, rows_done_v)] + self.guard(("geps", 3)), incs=[(sm, 1)])
                bc_t = sm.n
                for ct in range(8):
                    sh = 1 if ct >= 4 else 0
                    p.emit("vector", lambda v, ct=ct, tt4=tt4, sh=sh: v.tensor_tensor(
                        scr1, xf[:, ct * (T + 1) + 1 + tt4 * 512 - sh: ct * (T + 1) + 1 + (tt4 + 1) * 512 - sh],
                        pw0bc, ALU.mult),
                        waits=[(sm, bc_t), (sv, None)], incs=[(sv, 1)])
                    p.emit("vector", lambda v: v.tensor_tensor(scr2, scr1, pw1bc, ALU.subtract),
                           waits=[(sv, None)], incs=[(sv, 1)])
                    p.emit("vector", lambda v, ct=ct, tt4=tt4: v.tensor_scalar(
                        z1[:, ct * T + tt4 * 512: ct * T + (tt4 + 1) * 512], scr2,
                        lngv[:, ct:ct + 1], lnbv[:, ct:ct + 1], ALU.mult, ALU.add),
                        waits=[(sv, None), (sld, None)], incs=[(sv, 1)])
                self.set_guard(("geps", 2), sv)
                self.set_guard(("geps", 3), sv)
                z1_t[tt4] = sv.n
            quarters = [(0, 6), (6, 5), (11, 6), (17, 5)]
            q_cons = [None] * 6
            stc = [0]

            def emit_q_load(qi):
                p0, np_ = quarters[qi]
                wb = w1q[qi % 2]
                qw = np_ * 256
                ldw = [(sm, ge_m), (sv, ge_v)] if qi < 2 else [(sm, q_cons[qi - 2])]
                p.emit("gpsimd", lambda gp, l=l, p0=p0, qw=qw, wb=wb: gp.dma_start(
                    out=wb.rearrange("p (g n) -> p g n", g=8)[:, :, 0:qw],
                    in_=d["w1p"][l][:, p0 * 256: p0 * 256 + qw].rearrange(
                        "(g p) n -> p g n", p=128)),
                    waits=ldw, incs=[(sg1[qi % 2], 16)])

            def emit_gemm_tile(qi, tt4):
                p0, np_ = quarters[qi]
                wb = w1q[qi % 2]
                for pi in range(np_):
                    step = stc[0]
                    i_glob = p0 + pi
                    pa = self.ps[2 * (step % 2)]
                    pg = self.ps[2 * (step % 2) + 1]
                    for agi, pb in ((0, pa), (1, pg)):
                        def w1mm(t, pi=pi, agi=agi, tt4=tt4, pb=pb, wb=wb):
                            last = None
                            for k in range(8):
                                last = t.matmul(pb.ap()[:],
                                                wb[:, k * 1536 + (2 * pi + agi) * 128: k * 1536 + (2 * pi + agi) * 128 + 128],
                                                z1[:, k * T + tt4 * 512: k * T + (tt4 + 1) * 512],
                                                start=(k == 0), stop=(k == 7))
                            return last
                        p.emit("tensor", w1mm,
                               waits=[(sv, z1_t[tt4]), (sg1[qi % 2], sg1[qi % 2].n)] +
                                     self.guard(("gep", 2 * (step % 2) + agi)),
                               incs=[(sm, 1)])
                    w1_t = sm.n
                    p.emit("scalar", lambda sc, i_glob=i_glob, pg=pg, gb=ggb[step % 2]: sc.activation(
                        gb, pg.ap()[:], AF.Gelu, bias=b1v[:, 2 * i_glob + 1:2 * i_glob + 2]),
                        waits=[(sm, w1_t)] + self.guard(("ggb", step % 2)), incs=[(sa, 1)])
                    self.set_guard(("gep", 2 * (step % 2) + 1), sa)
                    hb = h4b[step % 2]
                    p.emit("vector", lambda v, hb=hb, i_glob=i_glob, pa=pa, gb=ggb[step % 2]: v.scalar_tensor_tensor(
                        hb, pa.ap()[:],
                        b1v[:, 2 * i_glob:2 * i_glob + 1], gb,
                        ALU.add, ALU.mult),
                        waits=[(sa, None), (sh4st[step % 2], sh4st[step % 2].n)], incs=[(sv, 1)])
                    self.set_guard(("gep", 2 * (step % 2)), sv)
                    self.set_guard(("ggb", step % 2), sv)
                    p.emit("sync", lambda sy, i_glob=i_glob, tt4=tt4, hb=hb: sy.dma_start(
                        out=d["h4_d"][i_glob * 128:(i_glob + 1) * 128, tt4 * 512:(tt4 + 1) * 512],
                        in_=hb), waits=[(sv, None)], incs=[(sh4st[step % 2], 16)])
                    stc[0] += 1

            emit_q_load(0)
            emit_q_load(1)
            emit_ln(0)
            emit_ln(1)
            emit_gemm_tile(0, 0)
            emit_ln(2)
            emit_gemm_tile(0, 1)
            emit_ln(3)
            emit_gemm_tile(0, 2)
            emit_gemm_tile(0, 3)
            q_cons[0] = sm.n
            for qi in range(1, 4):
                if qi >= 2:
                    emit_q_load(qi)
                for tt4 in range(NT4):
                    emit_gemm_tile(qi, tt4)
                q_cons[qi] = sm.n
            h4st_t = [sh4st[0].n, sh4st[1].n]
            if l == 0:
                p.emit("gpsimd", lambda gp: gp.dma_start(
                    out=self.wq_lo.rearrange("p (g n) -> p g n", g=3),
                    in_=d["wqkv"][1].rearrange("(g p) n -> p g n", p=128)[:, 5:8, :]),
                    waits=[(sm, None), (sv, None)], incs=[(sgw, 16)])
                p.emit("gpsimd", lambda gp: gp.dma_start(
                    out=self.wo2.rearrange("p (g n) -> p g n", g=4),
                    in_=d["wo"][1].rearrange("(g p) n -> p g n", p=128)),
                    waits=[], incs=[(sgw, 16)])
            self.ab.reset(m4a)
            w2h = [self.ab.alloc(11 * C) for _ in range(2)]
            h4t = [self.ab.alloc(11 * 512) for _ in range(2)]
            for half in range(2):
                wb2 = w2h[half % 2]
                p.emit("gpsimd", lambda gp, l=l, half=half, wb2=wb2: gp.dma_start(
                    out=wb2.rearrange("p (g n) -> p g n", g=11),
                    in_=d["w2p"][l][half * 1408:(half + 1) * 1408, :].rearrange(
                        "(g p) n -> p g n", p=128)),
                    waits=[(sm, None), (sv, None)],
                    incs=[(sg2[half], 16)])
                for tt4 in range(T // 512):
                    hs = h4t[tt4 % 2]
                    hld = sh4[tt4 % 2]
                    p.emit("sync", lambda sy, half=half, tt4=tt4, hs=hs: sy.dma_start(
                        out=hs.rearrange("p (g n) -> p g n", g=11),
                        in_=d["h4_d"][half * 1408:(half + 1) * 1408, tt4 * 512:(tt4 + 1) * 512].rearrange(
                            "(g p) n -> p g n", p=128)),
                        waits=[(sh4st[0], h4st_t[0]), (sh4st[1], h4st_t[1])] + self.guard(("h4t", tt4 % 2)),
                        incs=[(hld, 16)])
                    for ctO in range(8):
                        pb = self.ps[2 + ctO % 2]
                        def w2mm(t, hs=hs, ctO=ctO, pb=pb, wb2=wb2):
                            last = None
                            for k in range(11):
                                last = t.matmul(pb.ap()[:],
                                                wb2[:, k * C + ctO * 128: k * C + ctO * 128 + 128],
                                                hs[:, k * 512:(k + 1) * 512],
                                                start=(k == 0), stop=(k == 10))
                            return last
                        p.emit("tensor", w2mm, waits=[(hld, hld.n), (sg2[half], sg2[half].n)] +
                               self.guard(("gep", 2 + ctO % 2)), incs=[(sm, 1)])
                        if half == 0:
                            p.emit("vector", lambda v, ctO=ctO, tt4=tt4, pb=pb: v.tensor_tensor(
                                xf[:, ctO * (T + 1) + 1 + tt4 * 512: ctO * (T + 1) + 1 + (tt4 + 1) * 512],
                                xf[:, ctO * (T + 1) + 1 + tt4 * 512: ctO * (T + 1) + 1 + (tt4 + 1) * 512],
                                pb.ap()[:], ALU.add),
                                waits=[(sm, None)], incs=[(sv, 1)])
                        else:
                            p.emit("vector", lambda v, ctO=ctO, tt4=tt4, pb=pb: v.scalar_tensor_tensor(
                                xf[:, ctO * (T + 1) + 1 + tt4 * 512: ctO * (T + 1) + 1 + (tt4 + 1) * 512],
                                pb.ap()[:], b2v[:, ctO:ctO + 1],
                                xf[:, ctO * (T + 1) + 1 + tt4 * 512: ctO * (T + 1) + 1 + (tt4 + 1) * 512],
                                ALU.add, ALU.add),
                                waits=[(sm, None)], incs=[(sv, 1)])
                        self.set_guard(("gep", 2 + ctO % 2), sv)
                    self.set_guard(("h4t", tt4 % 2), sm)
            self.set_guard(("smallvec",), sa, sv)

        for l in range(2):
            attn_stage(l)
            geglu_stage(l)
        self.p4_done = [(sv, sv.n), (sm, sm.n), (sa, sa.n), (sl, sl.n), (sst4, sst4.n)]
        self.af.reset(mf0 + 8 * (T + 1))

    def phase5(self):
        p, T, d = self.p, self.T, self.dram
        sm = self.sem("p5_mm"); sv = self.sem("p5_dv"); sa = self.sem("p5_act")
        s_ld = [self.sem(f"p5_ld{i}") for i in range(2)]
        s_st = [self.sem(f"p5_st{i}") for i in range(2)]
        mb = self.ab.mark()
        gfb = self.ab.alloc(8 * T)
        gtt = self.ab.alloc(C)
        mf = self.af.mark()
        thb = self.af.alloc(C)
        xs_t = [self.af.alloc(C) for _ in range(2)]
        ss_t = [self.af.alloc(C) for _ in range(2)]
        tsc = thb
        thb2 = [thb, ss_t[0]]
        g_half = [0, 0]
        for hf in range(T // C):
            for ct in range(8):
                th = thb2[ct % 2]
                p.emit("scalar", lambda sc, ct=ct, hf=hf, th=th: sc.activation(
                    th, self.xf[:, ct * (T + 1) + 1 + hf * C: ct * (T + 1) + 1 + (hf + 1) * C],
                    AF.Tanh, scale=0.5),
                    waits=(list(self.p4_done) + [(sa, None), (sv, None)]) if (ct == 0 and hf == 0)
                    else ([(sv, sv.n - 1)] if sv.n else []),
                    incs=[(sa, 1)])
                p.emit("vector", lambda v, ct=ct, hf=hf, th=th: v.scalar_tensor_tensor(
                    gfb[:, ct * T + hf * C:(ct) * T + (hf + 1) * C], th, 1.0,
                    self.xf[:, ct * (T + 1) + 1 + hf * C: ct * (T + 1) + 1 + (hf + 1) * C],
                    ALU.add, ALU.mult),
                    waits=[(sa, None)], incs=[(sv, 1)])
            g_half[hf] = sv.n
        g_done = sv.n
        psb = [self.ps[2].ap().bitcast(BF16), self.ps[3].ap().bitcast(BF16)]
        gtt2 = [gtt, self.ab.alloc(C)]
        for tt in range(self.NTT):
            pb = psb[tt % 2]
            def trg(t, tt=tt, pb=pb):
                last = None
                for ct in range(8):
                    last = t.transpose(pb[:, ct * 128:(ct + 1) * 128],
                                       gfb[:, ct * T + tt * 128: ct * T + (tt + 1) * 128],
                                       self.identb)
                return last
            w_ = [(sv, g_half[tt // 8])] + self.guard(("p5ps", tt % 2))
            p.emit("tensor", trg, waits=w_, incs=[(sm, 1)])
            gt = gtt2[tt % 2]
            p.emit("vector", lambda v, pb=pb, gt=gt: v.tensor_copy(gt, pb[:, 0:1024]),
                   waits=[(sm, None)] + self.guard(("p5gt", tt % 2)), incs=[(sv, 1)])
            self.set_guard(("p5ps", tt % 2), sv)
            gt_done = sv.n
            xsl = xs_t[tt % 2]; ssl = ss_t[tt % 2]; osl = xs_t[tt % 2]
            ldw = [(s_st[tt % 2], s_st[tt % 2].n)]
            p.emit("sync", lambda sy, tt=tt, xsl=xsl: sy.dma_start(
                out=xsl, in_=d["x"][tt * 128:(tt + 1) * 128, :]),
                waits=ldw + ([(sv, g_done)] if tt < 2 else []), incs=[(s_ld[tt % 2], 16)])
            p.emit("sync", lambda sy, tt=tt, ssl=ssl: sy.dma_start(
                out=ssl, in_=d["s_d"][tt * 128:(tt + 1) * 128, :]),
                waits=[(sv, g_done)] if tt < 2 else [], incs=[(s_ld[tt % 2], 16)])
            p.emit("vector", lambda v, ssl=ssl, gt=gt: v.tensor_tensor(tsc, ssl, gt, ALU.mult),
                   waits=[(s_ld[tt % 2], s_ld[tt % 2].n), (sv, gt_done), (sv, None)], incs=[(sv, 1)])
            p.emit("vector", lambda v, xsl=xsl, osl=osl: v.scalar_tensor_tensor(
                osl, tsc, 0.5, xsl, ALU.mult, ALU.add),
                waits=[(sv, None)], incs=[(sv, 1)])
            self.set_guard(("p5gt", tt % 2), sv)
            p.emit("sync", lambda sy, tt=tt, osl=osl: sy.dma_start(
                out=d["out"][tt * 128:(tt + 1) * 128, :], in_=osl),
                waits=[(sv, None)], incs=[(s_st[tt % 2], 16)])
        self.af.reset(mf); self.ab.reset(mb)

    def finalize(self):
        nc = self.nc
        p = self.p
        waits = [(s, s.n) for s in self.sems.values() if s.n > 0]
        for i in range(0, len(waits), 8):
            p.emit("sync", lambda sy: None, waits=waits[i:i + 8])
        with nc.Block() as block:
            @block.tensor
            def _(t):
                p.replay("tensor", t)

            @block.vector
            def _(v):
                p.replay("vector", v)

            @block.scalar
            def _(sc):
                p.replay("scalar", sc)

            @block.gpsimd
            def _(gp):
                p.replay("gpsimd", gp)

            @block.sync
            def _(sy):
                p.replay("sync", sy)
        self.es.close()



def host_consts(T):
    out = {}
    out["ident"] = np.eye(128, dtype=np.float32)
    m = (np.arange(128)[:, None] < np.arange(128)[None, :]).astype(np.float32)
    out["maskA"] = np.tile(m, (1, 4))
    r = np.arange(128)
    out["mask_low"] = (r[None, :] <= r[:, None]).astype(np.float32)
    out["mask_up"] = (r[None, :] >= r[:, None]).astype(np.float32)
    d2 = np.arange(0, FDH, 2, dtype=np.float64)
    inv_freq = 1.0 / 10000.0 ** (d2 / FDH)
    t = np.arange(T, dtype=np.float64)
    freqs = t[:, None] * inv_freq[None, :]
    freqs = np.concatenate([freqs, freqs], axis=-1)
    base = (d2 + 0.4 * FDH) / (1.4 * FDH)
    power = (t - T // 2) / (WIN // 2)
    scale = base[None, :] ** power[:, None]
    scale = np.concatenate([scale, scale], axis=-1)
    cos, sin = np.cos(freqs), np.sin(freqs)
    sgn = np.concatenate([-np.ones(FDH // 2), np.ones(FDH // 2)])
    out["ct_q"] = (cos * scale).astype(np.float32)
    out["st_q"] = (sin * scale * sgn[None, :]).astype(np.float32)
    out["ct_k"] = (cos / scale).astype(np.float32)
    out["st_k"] = (sin / scale * sgn[None, :]).astype(np.float32)
    return out


def vec_fm(v):
    return np.ascontiguousarray(v.reshape(-1).reshape(8, 128).T).astype(np.float32)


def prep_core_inputs(inputs, b, T):
    g = lambda k: np.asarray(inputs[k])
    m = {}
    m["x"] = np.ascontiguousarray(g("L0_attn_x")[b, :T, :]).astype(np.float32, copy=False)
    m["tmx"] = vec_fm(g("L0_attn_time_maa_x"))
    m["tmw"] = vec_fm(g("L0_attn_time_maa_w"))
    m["tmk"] = vec_fm(g("L0_attn_time_maa_k"))
    m["tmv"] = vec_fm(g("L0_attn_time_maa_v"))
    m["tmr"] = vec_fm(g("L0_attn_time_maa_r"))
    m["tmg"] = vec_fm(g("L0_attn_time_maa_g"))
    m["tdecay"] = vec_fm(g("L0_attn_time_decay"))
    m["u_"] = vec_fm(g("L0_attn_time_faaaa"))
    m["lnw_"] = vec_fm(g("L0_attn_ln_w"))
    m["lnb_"] = vec_fm(g("L0_attn_ln_b"))
    w1 = g("L0_attn_time_maa_w1")
    m["maa_w1"] = np.ascontiguousarray(w1)
    w2 = g("L0_attn_time_maa_w2")
    m["maa_w2"] = np.ascontiguousarray(w2.transpose(1, 0, 2).reshape(32, 5 * C))
    m["td_w1"] = np.ascontiguousarray(g("L0_attn_time_decay_w1"))
    m["td_w2"] = np.ascontiguousarray(g("L0_attn_time_decay_w2"))
    m["w_r"] = np.ascontiguousarray(g("L0_attn_W_r"))
    m["w_k"] = np.ascontiguousarray(g("L0_attn_W_k"))
    m["w_v"] = np.ascontiguousarray(g("L0_attn_W_v"))
    m["w_g"] = np.ascontiguousarray(g("L0_attn_W_g"))
    m["w_o"] = np.ascontiguousarray(g("L0_attn_W_o"))
    m["lnw_rep"] = np.ascontiguousarray(np.broadcast_to(g("L0_attn_ln_w")[None, :], (128, C))).astype(np.float32)
    m["lnb_rep"] = np.ascontiguousarray(np.broadcast_to(g("L0_attn_ln_b")[None, :], (128, C))).astype(np.float32)
    m["wqkv"] = np.ascontiguousarray(g("L0_ffn_wqkv"))
    m["qscale_rep"] = np.ascontiguousarray(np.broadcast_to(g("L0_ffn_q_scale")[:, None, :], (2, 128, FDH))).astype(np.float32)
    m["kscale_rep"] = np.ascontiguousarray(np.broadcast_to(g("L0_ffn_k_scale")[:, None, :], (2, 128, FDH))).astype(np.float32)
    m["wo"] = np.ascontiguousarray(g("L0_ffn_wo"))
    m["lng"] = np.stack([vec_fm(g("L0_ffn_ln_g")[l]) for l in range(2)])
    m["lnb2"] = np.stack([vec_fm(g("L0_ffn_ln_b")[l]) for l in range(2)])
    m["b2"] = np.stack([vec_fm(g("L0_ffn_b2")[l]) for l in range(2)])
    w1f = g("L0_ffn_w1")
    b1f = g("L0_ffn_b1")
    w1p = np.zeros((2, C, 2 * HIDP), dtype=np.float32)
    b1p = np.zeros((2, 128, 44), dtype=np.float32)
    for l in range(2):
        a = w1f[l][:, :HID]; gg = w1f[l][:, HID:]
        ba = b1f[l][:HID]; bg = b1f[l][HID:]
        for i in range(22):
            lo, hi = i * 128, min((i + 1) * 128, HID)
            n = hi - lo
            if n > 0:
                w1p[l][:, (2 * i) * 128:(2 * i) * 128 + n] = a[:, lo:hi]
                w1p[l][:, (2 * i + 1) * 128:(2 * i + 1) * 128 + n] = gg[:, lo:hi]
                b1p[l][:n, 2 * i] = ba[lo:hi]
                b1p[l][:n, 2 * i + 1] = bg[lo:hi]
    m["w1p"] = w1p
    m["b1p"] = b1p
    w2f = g("L0_ffn_w2")
    w2p = np.zeros((2, HIDP, C), dtype=np.float32)
    w2p[:, :HID, :] = w2f
    m["w2p"] = w2p
    m.update(host_consts(T))
    return m


def build_nc(T=2048, dbg=(), phases=6):
    nc = bass.Bass("TRN2", target_bir_lowering=False, debug=False)
    b = Builder(nc, T, dbg)
    b.setup()
    b.load_consts()
    b.load_rwkv_weights()
    b.phase0()
    if phases > 1:
        b.phase1()
    if phases > 4:
        b.prefetch_ffn0()
    if phases > 2:
        b.phase2(parts=globals().get('_P2_PARTS', 31))
    if phases > 3:
        b.phase3()
    if phases > 4:
        b.phase4()
    if phases > 5:
        b.phase5()
    b.finalize()
    return b, nc


def kernel(**inputs):
    T = 2048
    bld, nc = build_nc(T)
    in_maps = [prep_core_inputs(inputs, b, T) for b in range(8)]
    res = run_bass_kernel_spmd(nc, in_maps, core_ids=list(range(8)))
    out = np.stack([res.results[i]["out"] for i in range(8)], axis=0)
    return out


# version 17
# speedup vs baseline: 1.0296x; 1.0011x over previous
import numpy as np
from contextlib import ExitStack

import concourse.bass as bass
import concourse.mybir as mybir
from concourse.bass_utils import run_bass_kernel_spmd

F32 = mybir.dt.float32
BF16 = mybir.dt.bfloat16
AF = mybir.ActivationFunctionType
ALU = mybir.AluOpType
AX = mybir.AxisListType

ENGINES = ("tensor", "vector", "scalar", "gpsimd", "sync")

C = 1024
H = 16
DH = 64
L = 128
FH = 8
FDH = 64
WIN = 512
INNER = 512
HID = 2730
HIDP = 2816
GN_EPS = 1e-5 * 64.0
LN_EPS = 1e-5


class Sem:
    def __init__(self, handle, name):
        self.h = handle
        self.name = name
        self.n = 0


class Prog:

    def __init__(self):
        self.ops = {e: [] for e in ENGINES}

    def emit(self, engine, fn, waits=(), incs=()):
        waits_f = [(s.h, (s.n if v is None else v)) for (s, v) in waits]
        incs_f = [(s.h, k) for (s, k) in incs]
        for s, k in incs:
            s.n += k
        self.ops[engine].append((waits_f, fn, incs_f))

    def replay(self, engine_name, eng):
        waited = {}
        for waits_f, fn, incs_f in self.ops[engine_name]:
            for h, v in waits_f:
                if v > 0 and waited.get(id(h), -1) < v:
                    eng.wait_ge(h, v)
                    waited[id(h)] = v
            inst = fn(eng)
            for h, k in incs_f:
                assert inst is not None, "op with incs must return an instruction"
                inst.then_inc(h, k)


class Arena:

    def __init__(self, t, cols, dtype):
        self.t = t
        self.cols = cols
        self.dtype = dtype
        self.off = 0
        self.align = 8 if dtype == BF16 else 4
        self.peak = 0

    def alloc(self, w):
        o = self.off
        w_al = -(-w // self.align) * self.align
        assert o + w_al <= self.cols, f"arena overflow {o}+{w} > {self.cols}"
        self.off = o + w_al
        self.peak = max(self.peak, self.off)
        return self.t.ap()[:, o:o + w]

    def mark(self):
        return self.off

    def reset(self, m):
        self.off = m


def ceil_div(a, b):
    return -(-a // b)


class Builder:
    def __init__(self, nc, T, dbg=()):
        self.nc = nc
        self.T = T
        self.NTT = T // 128
        self.NCHK = T // L
        self.NW = T // WIN
        self.dbg = set(dbg)
        self.p = Prog()
        self.es = ExitStack()
        self.sems = {}
        self.dram = {}
        self.dbg_outs = {}

    def set_guard(self, key, *sems):
        if not hasattr(self, "guards"):
            self.guards = {}
        self.guards[key] = [(s, s.n) for s in sems]

    def guard(self, key):
        if not hasattr(self, "guards"):
            self.guards = {}
        return list(self.guards.get(key, []))

    def sem(self, name):
        if name not in self.sems:
            self.sems[name] = Sem(self.es.enter_context(self.nc.semaphore(name)), name)
        return self.sems[name]

    def setup(self):
        nc, T = self.nc, self.T
        def din(name, shape, dt=F32):
            self.dram[name] = nc.dram_tensor(name, list(shape), dt, kind="ExternalInput").ap()
        din("x", (T, C))
        for nm in ("tmx", "tmw", "tmk", "tmv", "tmr", "tmg", "tdecay", "u_", "lnw_", "lnb_"):
            din(nm, (128, 8))
        din("maa_w1", (C, 160))
        din("maa_w2", (32, 5 * C))
        din("td_w1", (C, 64))
        din("td_w2", (64, C))
        din("w_r", (C, C)); din("w_k", (C, C)); din("w_v", (C, C)); din("w_g", (C, C)); din("w_o", (C, C))
        din("lnw_rep", (128, C)); din("lnb_rep", (128, C))
        din("wqkv", (2, C, 3 * INNER))
        din("qscale_rep", (2, 128, FDH)); din("kscale_rep", (2, 128, FDH))
        din("wo", (2, INNER, C))
        din("lng", (2, 128, 8)); din("lnb2", (2, 128, 8)); din("b2", (2, 128, 8))
        din("w1p", (2, C, 2 * HIDP))
        din("b1p", (2, 128, 44))
        din("w2p", (2, HIDP, C))
        din("ident", (128, 128))
        din("maskA", (128, 512))
        din("mask_low", (128, 128))
        din("mask_up", (128, 128))
        din("ct_q", (T, 64)); din("st_q", (T, 64)); din("ct_k", (T, 64)); din("st_k", (T, 64))
        def dscr(name, shape, dt):
            self.dram[name] = nc.dram_tensor(name, list(shape), dt, kind="Internal").ap()
        dscr("rq_d", (2 * C, T), BF16)
        dscr("kk_d", (C, T), BF16)
        dscr("v_d", (T, C), BF16)
        dscr("g2x_d", (T, C), BF16)
        dscr("s_d", (T, C), F32)
        dscr("h4_d", (HIDP, T), BF16)
        self.dram["out"] = nc.dram_tensor("out", [T, C], F32, kind="ExternalOutput").ap()

        ncc = self.nc
        self.af = Arena(self.es.enter_context(ncc.sbuf_tensor("arena_f", [128, 23000], F32)), 23000, F32)
        self.ab = Arena(self.es.enter_context(ncc.sbuf_tensor("arena_b", [128, 56000], BF16)), 56000, BF16)
        self.ps = [self.es.enter_context(ncc.psum_tensor(f"ps{i}", [128, 512], F32)) for i in range(4)]
        self.pw = [self.es.enter_context(ncc.psum_tensor(f"pw{i}", [128, 1024], F32)) for i in range(2)]
        self.wq_lo = self.ab.t.ap()[:, 46900:46900 + 3 * 1536]
        self.wo2 = self.ab.t.ap()[:, 51520:51520 + 4 * C]

        self.consts_small()

    def consts_small(self):
        pass

    def xn_view(self, ct, t0, n, shift=0):
        base = ct * (self.T + 1) + 1 + t0 - shift
        return self.xn[:, base:base + n]

    def dbg_out(self, name, shape, dt=F32):
        if name in self.dbg:
            ap = self.nc.dram_tensor("dbg_" + name, list(shape), dt, kind="ExternalOutput").ap()
            self.dbg_outs[name] = ap
            return ap
        return None

    def phase0(self):
        p, T = self.p, self.T
        d = self.dram
        s_xs = [self.sem(f"p0_x{i}") for i in range(4)]
        s_sta = self.sem("p0_sta")
        s_stv = self.sem("p0_stv")
        s_vch = self.sem("p0_vch")
        s_nv = self.sem("p0_norm")
        s_tr = self.sem("p0_tr")
        s_ev = self.sem("p0_ev")
        m = self.af.mark()
        mb = self.ab.mark()
        xtm = [self.af.alloc(C) for _ in range(4)]
        sq = [self.af.alloc(C) for _ in range(4)]
        stats = self.af.alloc(32)
        xnt = [self.ab.alloc(C) for _ in range(2)]
        p.emit("vector", lambda v: v.memset(
            self.xn.rearrange("p (c t) -> p c t", c=8)[:, :, 0:1], 0.0))
        for tt in range(self.NTT):
            xb = xtm[tt % 4]
            s_x = s_xs[tt % 4]
            waits = [(s_nv, tt - 4 + 1)] if tt >= 4 else []
            p.emit("sync", lambda sy, xb=xb, tt=tt: sy.dma_start(
                out=xb, in_=d["x"][tt * 128:(tt + 1) * 128, :]), waits=waits, incs=[(s_x, 16)])
            sqb = sq[tt % 4]
            col = lambda j, tt=tt: stats[:, (tt % 4) * 8 + j:(tt % 4) * 8 + j + 1]
            ssq, sxs, muv, tmp, var, lnv, rst = (col(j) for j in range(7))
            guards = []
            if tt >= 4:
                guards = [(s_nv, tt - 4 + 1), (s_stv, tt - 4 + 1)]
            p.emit("scalar", lambda sc, xb=xb, sqb=sqb, ssq=ssq: sc.activation(
                sqb, xb, AF.Square, accum_out=ssq),
                waits=[(s_x, None)] + guards + ([(s_sta, 2 * (tt - 4) + 1)] if tt >= 4 else []),
                incs=[(s_sta, 1)])
            p.emit("vector", lambda v, xb=xb, sxs=sxs: v.tensor_reduce(sxs, xb, AX.X, ALU.add),
                   waits=[(s_x, None)] + guards, incs=[(s_vch, 1)])
            p.emit("vector", lambda v, sxs=sxs, muv=muv: v.tensor_scalar_mul(muv, sxs, 1.0 / C),
                   waits=[(s_vch, 3 * tt + 1)], incs=[(s_vch, 1)])
            p.emit("vector", lambda v, muv=muv, tmp=tmp: v.tensor_tensor(tmp, muv, muv, ALU.mult),
                   waits=[(s_vch, 3 * tt + 2)], incs=[(s_vch, 1)])
            p.emit("vector", lambda v, ssq=ssq, tmp=tmp, var=var: v.scalar_tensor_tensor(
                var, ssq, 1.0 / C, tmp, ALU.mult, ALU.subtract),
                waits=[(s_vch, 3 * tt + 3), (s_sta, 2 * tt + 1)], incs=[(s_stv, 1)])
            p.emit("scalar", lambda sc, var=var, lnv=lnv: sc.activation(
                lnv, var, AF.Ln, bias=self.cc[LN_EPS]),
                waits=[(s_stv, tt + 1)], incs=[(self.sem("p0_lnch"), 1)])
            p.emit("scalar", lambda sc, lnv=lnv, rst=rst: sc.activation(rst, lnv, AF.Exp, scale=-0.5),
                   waits=[(self.sem("p0_lnch"), tt + 1)], incs=[(s_sta, 1)])
            xnb = xnt[tt % 2]
            waits = [(s_sta, 2 * tt + 2), (s_vch, 3 * tt + 2)]
            if tt >= 2:
                waits.append((s_ev, (tt - 2 + 1) * 2))
            p.emit("vector", lambda v, xb=xb, xnb=xnb, muv=muv, rst=rst: v.tensor_scalar(
                xnb, xb, muv, rst, ALU.subtract, ALU.mult),
                waits=waits, incs=[(s_nv, 1)])
            for g in range(2):
                bank = self.ps[(tt * 2 + g) % 4].ap().bitcast(BF16)
                def tr_f(t, xnb=xnb, g=g, bank=bank):
                    last = None
                    for j in range(4):
                        ct = g * 4 + j
                        last = t.transpose(bank[:, j * 128:(j + 1) * 128],
                                           xnb[:, ct * 128:(ct + 1) * 128], self.identb)
                    return last
                waits = [(s_nv, tt + 1)]
                if tt * 2 + g >= 4:
                    waits.append((s_ev, (tt * 2 + g) - 4 + 1))
                if tt == 0 and g == 0:
                    waits.append((self.sem("consts_g"), self.cg_sem_target))
                p.emit("tensor", tr_f, waits=waits, incs=[(s_tr, 1)])
                xn4 = self.xn.rearrange("p (c t) -> p c t", c=8)[:, g * 4:(g + 1) * 4,
                                                                1 + tt * 128:1 + (tt + 1) * 128]
                p.emit("vector", lambda v, bank=bank, xn4=xn4: v.tensor_copy(
                    xn4, bank[:, 0:512].rearrange("p (c t) -> p c t", c=4)),
                    waits=[(s_tr, tt * 2 + g + 1)], incs=[(s_ev, 1)])
        self.p0_done = [(s_ev, s_ev.n)]
        dbg = self.dbg_out("xn", (128, 8 * (T + 1)), BF16)
        if dbg is not None:
            p.emit("sync", lambda sy: sy.dma_start(out=dbg, in_=self.xn),
                   waits=[(s_ev, None)], incs=[(self.sem("dbg"), 16)])
        self.af.reset(m)
        self.ab.reset(mb)

    def load_consts(self):
        p, d = self.p, self.dram
        s_c = self.sem("consts")
        self.identb = self.ab.alloc(128)
        self.identf = self.af.alloc(128)
        s_cg = self.sem("consts_g")
        p.emit("gpsimd", lambda gp: gp.dma_start(out=self.identb, in_=d["ident"]), incs=[(s_cg, 16)])
        p.emit("sync", lambda sy: sy.dma_start(out=self.identf, in_=d["ident"]), incs=[(s_c, 16)])
        self.maskA = self.af.alloc(512)
        p.emit("sync", lambda sy: sy.dma_start(out=self.maskA, in_=d["maskA"]), incs=[(s_c, 16)])
        self.mask_low = self.af.alloc(128)
        self.mask_up = self.af.alloc(128)
        p.emit("sync", lambda sy: sy.dma_start(out=self.mask_low, in_=d["mask_low"]), incs=[(s_c, 16)])
        p.emit("sync", lambda sy: sy.dma_start(out=self.mask_up, in_=d["mask_up"]), incs=[(s_c, 16)])
        self.vecs = {}
        for nm in ("tmx", "tmw", "tmk", "tmv", "tmr", "tmg", "tdecay", "u_", "lnw_", "lnb_"):
            t = self.af.alloc(8)
            self.vecs[nm] = t
            p.emit("sync", lambda sy, t=t, nm=nm: sy.dma_start(out=t, in_=d[nm]), incs=[(s_c, 16)])
        self.blockones = self.ab.alloc(2)
        def bo_f(v):
            v.memset(self.blockones[0:64, 0:1], 1.0)
            v.memset(self.blockones[64:128, 0:1], 0.0)
            v.memset(self.blockones[0:64, 1:2], 0.0)
            return v.memset(self.blockones[64:128, 1:2], 1.0)
        s_cv = self.sem("consts_v")
        p.emit("vector", bo_f, incs=[(s_cv, 1)])
        self.onescol = self.ab.alloc(8)
        p.emit("vector", lambda v: v.memset(self.onescol, 1.0), incs=[(s_cv, 1)])
        self.cc = {}
        for val in (LN_EPS, GN_EPS, 1e-30, 0.0):
            t = self.af.alloc(1)
            self.cc[val] = t
            p.emit("vector", lambda v, t=t, val=val: v.memset(t, val), incs=[(s_cv, 1)])
        self.c_sem_target = s_c.n
        self.cv_sem_target = s_cv.n
        self.cg_sem_target = s_cg.n
        self.xn_mark = self.ab.mark()
        self.xn = self.ab.alloc(8 * (self.T + 1))

    def load_rwkv_weights(self):
        p, d = self.p, self.dram
        self.wmark_b = self.ab.mark()
        s_w = self.sem("rwkv_w")
        self.Wslot = [self.ab.alloc(8 * C), self.ab.alloc(8 * C)]
        for slot, nm in ((0, "w_k"), (1, "w_r")):
            t = self.Wslot[slot]
            p.emit("gpsimd", lambda gp, t=t, nm=nm: gp.dma_start(
                out=t.rearrange("p (g n) -> p g n", g=8),
                in_=d[nm].rearrange("(g p) n -> p g n", p=128)), incs=[(s_w, 16)])
        self.maa_w1 = self.ab.alloc(8 * 160)
        p.emit("gpsimd", lambda gp: gp.dma_start(
            out=self.maa_w1.rearrange("p (g n) -> p g n", g=8),
            in_=d["maa_w1"].rearrange("(g p) n -> p g n", p=128)), incs=[(s_w, 16)])
        self.maa_w2 = self.ab.alloc(5 * C)
        p.emit("gpsimd", lambda gp: gp.dma_start(
            out=self.maa_w2[0:32, :], in_=d["maa_w2"]), incs=[(s_w, 16)])
        self.td_w2 = self.ab.alloc(C)[0:64, :]
        p.emit("gpsimd", lambda gp: gp.dma_start(
            out=self.td_w2, in_=d["td_w2"]), incs=[(s_w, 16)])
        self.td_w1 = self.ab.alloc(8 * 64)
        p.emit("gpsimd", lambda gp: gp.dma_start(
            out=self.td_w1.rearrange("p (g n) -> p g n", g=8),
            in_=d["td_w1"].rearrange("(g p) n -> p g n", p=128)), incs=[(s_w, 16)])
        self.rwkv_w = s_w

    def phase1(self):
        p, T, d = self.p, self.T, self.dram
        TS = 256
        NTIL = T // TS
        self.dvec = self.af.alloc(self.NTT * 16)
        self.e3 = self.af.alloc(8 * self.NCHK)
        m = self.af.mark(); mb = self.ab.mark()
        xx2 = [self.af.alloc(8 * TS) for _ in range(2)]
        kf = self.af.alloc(8 * TS)
        ld = self.af.alloc(8 * TS)
        q = self.af.alloc(8 * TS)
        teb = [self.af.alloc(TS) for _ in range(2)]
        e1b = [self.af.alloc(TS) for _ in range(2)]
        e2b = [self.af.alloc(TS) for _ in range(2)]
        zbb = [self.ab.alloc(TS) for _ in range(2)]
        qlb = self.af.alloc(2)
        xxx1 = self.ab.alloc(8 * TS)
        oper = [self.ab.alloc(8 * TS) for _ in range(3)]
        t5f1 = [self.ab.alloc(TS) for _ in range(5)]
        tw = self.ab.alloc(TS)
        rkub = self.ab.alloc(TS)
        fRQ = self.ab.alloc(8 * TS)
        fKK = self.ab.alloc(8 * TS)
        fV = fRQ
        fG = fKK
        msb2 = [self.af.alloc(TS) for _ in range(2)]
        rrb2 = [self.af.alloc(TS) for _ in range(2)]
        sv = self.sem("p1_v")
        sa = self.sem("p1_a")
        sm = self.sem("p1_m")
        sl = self.sem("p1_pool")
        sst = self.sem("p1_st")
        sw2a = self.sem("p1_w2a")
        sw2b = self.sem("p1_w2b")

        s_zro = self.sem("p1_zro")
        p.emit("vector", lambda v: v.memset(fKK, 0.0), waits=list(self.p0_done), incs=[(sv, 1)])
        vv0 = d["rq_d"].rearrange("(c p) t -> p c t", c=8)
        for cz in range(8):
            p.emit("sync", lambda sy, cz=cz: sy.dma_start(
                out=vv0[64:192, cz, :], in_=fKK[:, 0:T]),
                waits=[(sv, None)], incs=[(s_zro, 16)])
        zro_t = s_zro.n

        xxx_done_t = {}

        def emit_xxx(it, pass2):
            t0 = it * TS
            G = self.guard
            par = it % 2
            xx = xx2[par]
            for ct in range(8):
                w0 = G(("xx", par, ct)) + (self.p0_done + [
                    (self.sem("consts"), self.c_sem_target),
                    (self.sem("consts_v"), self.cv_sem_target)]
                    if (it == 0 and not pass2 and ct == 0) else [])
                p.emit("vector", lambda v, ct=ct, t0=t0, xx=xx: v.tensor_tensor(
                    xx[:, ct * TS:(ct + 1) * TS], self.xn_view(ct, t0, TS, shift=1),
                    self.xn_view(ct, t0, TS), ALU.subtract),
                    waits=w0, incs=[(sv, 1)])
                p.emit("vector", lambda v, ct=ct, t0=t0, xx=xx: v.scalar_tensor_tensor(
                    xxx1[:, ct * TS:(ct + 1) * TS], xx[:, ct * TS:(ct + 1) * TS],
                    self.vecs["tmx"][:, ct:ct + 1], self.xn_view(ct, t0, TS),
                    ALU.mult, ALU.add), waits=G(("xxxbuf",)) + [(sv, None)], incs=[(sv, 1)])
            xxx_done_t[it] = sv.n

        def tile_pass(it, pass2):
            t0 = it * TS
            c0g = (it * TS) // L
            G = self.guard; SG = self.set_guard
            par = it % 2
            xx = xx2[par]
            xxx = xxx1
            t5f = t5f1
            xxx_done = xxx_done_t[it]
            for f5 in range(5):
                pb = self.ps[f5 % 2]
                po = 0
                def t5mm(t, f5=f5, pb=pb, po=po):
                    last = None
                    for k in range(8):
                        last = t.matmul(pb.ap()[0:32, po:po + TS],
                                        self.maa_w1[:, k * 160 + f5 * 32:k * 160 + f5 * 32 + 32],
                                        xxx[:, k * TS:(k + 1) * TS],
                                        start=(k == 0), stop=(k == 7))
                    return last
                p.emit("tensor", t5mm, waits=[(sv, xxx_done), (self.rwkv_w, None)] +
                       G(("t5ps", f5 % 2)), incs=[(sm, 1)])
                p.emit("scalar", lambda sc, f5=f5, pb=pb, po=po: sc.activation(
                    t5f[f5][0:32, :], pb.ap()[0:32, po:po + TS], AF.Tanh),
                    waits=[(sm, None)], incs=[(sa, 1)])
                SG(("t5ps", f5 % 2), sa)
            SG(("xxxbuf",), sm)
            fmap = {0: "tmw", 1: "tmk", 2: "tmv", 3: "tmr", 4: "tmg"}
            fs = [2, 4] if pass2 else [0, 1, 3]
            t5_a = {f5: None for f5 in range(5)}
            for fi, f in enumerate(fs):
                nmv = fmap[f]
                ob = oper[fi] if not pass2 else oper[fi % 2]
                t5a_cnt = sa.n
                for ct in range(8):
                    pb = self.ps[2 + ct % 2]
                    p.emit("tensor", lambda t, f=f, ct=ct, pb=pb: t.matmul(
                        pb.ap()[:, 0:TS], self.maa_w2[0:32, f * C + ct * 128:f * C + ct * 128 + 128],
                        t5f[f][0:32, :], start=True, stop=True),
                        waits=[(sa, t5a_cnt)] + G(("pk", ct % 2)), incs=[(sm, 1)])
                    msb = msb2[ct % 2]
                    p.emit("scalar", lambda sc, pb=pb, msb=msb: sc.activation(
                        msb, pb.ap()[:, 0:TS], AF.Copy),
                        waits=[(sm, None)] + G(("msb", ct % 2)), incs=[(sa, 1)])
                    SG(("pk", ct % 2), sa)
                    zb = zbb[ct % 2]
                    p.emit("vector", lambda v, ct=ct, zb=zb, msb=msb, nmv=nmv, xx=xx: v.scalar_tensor_tensor(
                        zb, msb, self.vecs[nmv][:, ct:ct + 1],
                        xx[:, ct * TS:(ct + 1) * TS], ALU.add, ALU.mult),
                        waits=[(sa, None)] + G(("zbb", ct % 2)), incs=[(sv, 1)])
                    SG(("msb", ct % 2), sv)
                    p.emit("vector", lambda v, ct=ct, zb=zb, ob=ob, t0=t0: v.tensor_tensor(
                        ob[:, ct * TS:(ct + 1) * TS], zb, self.xn_view(ct, t0, TS), ALU.add),
                        waits=[(sv, None)] + G(("oper", ob.offset, ct)), incs=[(sl, 1)])
                    SG(("zbb", ct % 2), sl)
            oper_done = sl.n
            if not pass2:
                ob = oper[0]
                def twmm(t):
                    last = None
                    for k in range(8):
                        last = t.matmul(self.pw[0].ap()[0:64, 0:TS],
                                        self.td_w1[:, k * 64:(k + 1) * 64],
                                        ob[:, k * TS:(k + 1) * TS], start=(k == 0), stop=(k == 7))
                    return last
                p.emit("tensor", twmm, waits=[(sl, oper_done - 16)] + G(("pw0",)), incs=[(sm, 1)])
                p.emit("scalar", lambda sc: sc.activation(tw[0:64, :], self.pw[0].ap()[0:64, 0:TS],
                                                          AF.Tanh), waits=[(sm, None)], incs=[(sa, 1)])
                SG(("pw0",), sa)
                tw_done = sa.n
                for ct in range(8):
                    pb = self.ps[2 + ct % 2]
                    p.emit("tensor", lambda t, ct=ct, pb=pb: t.matmul(
                        pb.ap()[:, 0:TS], self.td_w2[:, ct * 128:(ct + 1) * 128], tw[0:64, :],
                        start=True, stop=True),
                        waits=[(sa, tw_done)] + G(("pk", ct % 2)), incs=[(sm, 1)])
                    p.emit("scalar", lambda sc, ct=ct, pb=pb: sc.activation(
                        ld[:, ct * TS:(ct + 1) * TS], pb.ap()[:, 0:TS], AF.Exp,
                        bias=self.vecs["tdecay"][:, ct:ct + 1]),
                        waits=[(sm, None)] + G(("ld", ct)), incs=[(sa, 1)])
                    SG(("pk", ct % 2), sa)
                    for cc_ in range(TS // L):
                        p.emit("vector", lambda v, ct=ct, cc_=cc_: v.tensor_tensor_scan(
                            q[:, ct * TS + cc_ * L: ct * TS + (cc_ + 1) * L],
                            ld[:, ct * TS + cc_ * L: ct * TS + (cc_ + 1) * L],
                            self.cc[0.0].broadcast_to([128, L]), 0.0, ALU.add, ALU.add),
                            waits=[(sa, None)] + G(("q", ct, cc_)), incs=[(sv, 1)])
                    p.emit("vector", lambda v, ct=ct: v.tensor_copy(
                        qlb, q[:, ct * TS + L - 1: (ct + 1) * TS: L]),
                        waits=[(sv, None)] + G(("qlb",)), incs=[(sv, 1)])
                    p.emit("scalar", lambda sc, ct=ct, c0g=c0g: sc.activation(
                        self.e3[:, ct * self.NCHK + c0g: ct * self.NCHK + c0g + TS // L],
                        qlb, AF.Exp, scale=-1.0),
                        waits=[(sv, None)], incs=[(sa, 1)])
                    SG(("qlb",), sa)
                if it + 1 < NTIL:
                    emit_xxx(it + 1, pass2)
                for ct in range(8):
                    pb = self.ps[2 + ct % 2]
                    ob1 = oper[1]
                    def kmm(t, ct=ct, pb=pb, ob1=ob1):
                        last = None
                        for k in range(8):
                            last = t.matmul(pb.ap()[:, 0:TS],
                                            self.Wslot[0][:, k * C + ct * 128: k * C + ct * 128 + 128],
                                            ob1[:, k * TS:(k + 1) * TS], start=(k == 0), stop=(k == 7))
                        return last
                    p.emit("tensor", kmm, waits=[(sl, oper_done - 8), (s_zro, zro_t)] +
                           G(("pk", ct % 2)), incs=[(sm, 1)])
                    p.emit("scalar", lambda sc, ct=ct, pb=pb: sc.activation(
                        kf[:, ct * TS:(ct + 1) * TS], pb.ap()[:, 0:TS], AF.Copy),
                        waits=[(sm, None)] + G(("kf", ct)), incs=[(sa, 1)])
                    SG(("pk", ct % 2), sa)
                    p.emit("scalar", lambda sc, ct=ct, e2=e2b[ct % 2]: sc.activation(
                        e2, q[:, ct * TS:(ct + 1) * TS], AF.Exp),
                        waits=[(sv, None)] + G(("e2b", ct % 2)), incs=[(sa, 1)])
                    p.emit("vector", lambda v, ct=ct, e2=e2b[ct % 2]: v.tensor_tensor(
                        fKK[:, ct * TS:(ct + 1) * TS], kf[:, ct * TS:(ct + 1) * TS], e2, ALU.mult),
                        waits=[(sa, None)] + G(("fKK", ct)), incs=[(sv, 1)])
                    SG(("e2b", ct % 2), sv)
                    SG(("q", ct, 0), sv); SG(("q", ct, 1), sv)
                for ct in range(8):
                    pb = self.ps[2 + ct % 2]
                    ob2 = oper[2]
                    def rmm(t, ct=ct, pb=pb, ob2=ob2):
                        last = None
                        for k in range(8):
                            last = t.matmul(pb.ap()[:, 0:TS],
                                            self.Wslot[1][:, k * C + ct * 128: k * C + ct * 128 + 128],
                                            ob2[:, k * TS:(k + 1) * TS], start=(k == 0), stop=(k == 7))
                        return last
                    p.emit("tensor", rmm, waits=[(sl, oper_done)] + G(("pk", ct % 2)), incs=[(sm, 1)])
                    p.emit("vector", lambda v, ct=ct, te=teb[ct % 2]: v.tensor_tensor(
                        te, ld[:, ct * TS:(ct + 1) * TS], q[:, ct * TS:(ct + 1) * TS], ALU.subtract),
                        waits=[(sv, None)] + G(("teb", ct % 2)), incs=[(sv, 1)])
                    SG(("ld", ct), sv)
                    p.emit("scalar", lambda sc, ct=ct, te=teb[ct % 2], e1=e1b[ct % 2]: sc.activation(
                        e1, te, AF.Exp), waits=[(sv, None)] + G(("e1b", ct % 2)), incs=[(sa, 1)])
                    SG(("teb", ct % 2), sa)
                    rrb = rrb2[ct % 2]
                    p.emit("scalar", lambda sc, pb=pb, rrb=rrb: sc.activation(
                        rrb, pb.ap()[:, 0:TS], AF.Copy),
                        waits=[(sm, None)] + G(("rrb", ct % 2)), incs=[(sa, 1)])
                    SG(("pk", ct % 2), sa)
                    p.emit("vector", lambda v, ct=ct, rrb=rrb, e1=e1b[ct % 2]: v.tensor_tensor(
                        fRQ[:, ct * TS:(ct + 1) * TS], rrb, e1, ALU.mult),
                        waits=[(sa, None)] + G(("fRQ", ct)), incs=[(sv, 1)])
                    p.emit("vector", lambda v, ct=ct, rrb=rrb: v.scalar_tensor_tensor(
                        rkub, rrb, self.vecs["u_"][:, ct:ct + 1],
                        kf[:, ct * TS:(ct + 1) * TS], ALU.mult, ALU.mult),
                        waits=[(sv, None)] + G(("rkub",)), incs=[(sv, 1)])
                    SG(("rrb", ct % 2), sv)
                    SG(("e1b", ct % 2), sv)
                    SG(("kf", ct), sv)
                    for ts_ in range(2):
                        tt = it * 2 + ts_
                        p.emit("tensor", lambda t, ts_=ts_, ct=ct: t.matmul(
                            self.pw[1].ap()[:, 960:962], rkub[:, ts_ * 128:(ts_ + 1) * 128],
                            self.blockones, start=True, stop=True),
                            waits=[(sv, None)] + G(("pw1d",)), incs=[(sm, 1)])
                        p.emit("vector", lambda v, tt=tt, ct=ct: v.tensor_copy(
                            self.dvec[:, tt * 16 + 2 * ct: tt * 16 + 2 * ct + 2],
                            self.pw[1].ap()[:, 960:962]),
                            waits=[(sm, None)], incs=[(sv, 1)])
                        SG(("pw1d",), sv)
                    SG(("rkub",), sm)
                vv = d["rq_d"].rearrange("(c p) t -> p c t", c=8)
                p.emit("sync", lambda sy, t0=t0, vv=vv: sy.dma_start(
                    out=vv[0:64, :, t0:t0 + TS],
                    in_=fRQ[0:64, :].rearrange("p (g t) -> p g t", g=8)),
                    waits=[(sv, None)], incs=[(sst, 16)])
                p.emit("sync", lambda sy, t0=t0, vv=vv: sy.dma_start(
                    out=vv[192:256, :, t0:t0 + TS],
                    in_=fRQ[64:128, :].rearrange("p (g t) -> p g t", g=8)),
                    waits=[(sv, None)], incs=[(sst, 16)])
                p.emit("sync", lambda sy, it=it: sy.dma_start(
                    out=d["kk_d"].rearrange("(g p) t -> p g t", p=128)[:, :, t0:t0 + TS],
                    in_=fKK.rearrange("p (g t) -> p g t", g=8)),
                    waits=[(sv, None)], incs=[(sst, 16)])
                sst_t = sst.n
                for ct in range(8):
                    SG(("fRQ", ct), sst); SG(("fKK", ct), sst)
                    SG(("xx", par, ct), sl)
            else:
                if it + 1 < NTIL:
                    emit_xxx(it + 1, pass2)
                for fi, (f, dst, silu) in enumerate(((2, fV, False), (4, fG, True))):
                    ob = oper[fi % 2]
                    wslot = self.Wslot[0] if f == 2 else self.Wslot[1]
                    for ts_ in range(2):
                        pb = self.pw[ts_ % 2]
                        def vmm(t, ob=ob, wslot=wslot, ts_=ts_, pb=pb):
                            last = None
                            for k in range(8):
                                for nh in range(2):
                                    last = t.matmul(
                                        pb.ap()[:, nh * 512:(nh + 1) * 512],
                                        ob[:, k * TS + ts_ * 128: k * TS + ts_ * 128 + 128],
                                        wslot[:, k * C + nh * 512: k * C + (nh + 1) * 512],
                                        start=(k == 0), stop=(k == 7))
                            return last
                        p.emit("tensor", vmm,
                               waits=[(sl, oper_done - 16 + (fi + 1) * 8),
                                      ((sw2a if f == 2 else sw2b), 16)] + G(("pwv", ts_ % 2)),
                               incs=[(sm, 1)])
                        if not silu:
                            p.emit("scalar", lambda sc, ts_=ts_, pb=pb, dst=dst: sc.activation(
                                dst[:, ts_ * C:(ts_ + 1) * C], pb.ap()[:, 0:C], AF.Copy),
                                waits=[(sm, None)] + G(("fV", ts_)), incs=[(sa, 1)])
                            SG(("pwv", ts_ % 2), sa)
                        else:
                            th = kf[:, ts_ * C:(ts_ + 1) * C]
                            p.emit("scalar", lambda sc, ts_=ts_, pb=pb, th=th: sc.activation(
                                th, pb.ap()[:, 0:C], AF.Tanh, scale=0.5),
                                waits=[(sm, None)] + G(("thv", ts_)), incs=[(sa, 1)])
                            p.emit("vector", lambda v, ts_=ts_, pb=pb, th=th, dst=dst: v.scalar_tensor_tensor(
                                dst[:, ts_ * C:(ts_ + 1) * C], th, 1.0, pb.ap()[:, 0:C],
                                ALU.add, ALU.mult),
                                waits=[(sa, None)] + G(("fG", ts_)), incs=[(sv, 1)])
                            SG(("pwv", ts_ % 2), sv)
                            SG(("thv", ts_), sv)
                p.emit("sync", lambda sy, it=it: sy.dma_start(
                    out=d["v_d"][t0:t0 + TS, :].rearrange("(g p) c -> p g c", p=128),
                    in_=fV.rearrange("p (g c) -> p g c", g=2)),
                    waits=[(sa, None)], incs=[(sst, 16)])
                p.emit("sync", lambda sy, it=it: sy.dma_start(
                    out=d["g2x_d"][t0:t0 + TS, :].rearrange("(g p) c -> p g c", p=128),
                    in_=fG.rearrange("p (g c) -> p g c", g=2)),
                    waits=[(sv, None)], incs=[(sst, 16)])
                for ts_ in range(2):
                    SG(("fV", ts_), sst); SG(("fG", ts_), sst)
                for ct in range(8):
                    SG(("xx", par, ct), sl)

        emit_xxx(0, False)
        for it in range(NTIL):
            tile_pass(it, pass2=False)
        for ts_ in range(2):
            self.set_guard(("fV", ts_), sst)
            self.set_guard(("fG", ts_), sst)
            self.set_guard(("thv", ts_), sv)
            self.set_guard(("pwv", ts_), sv)
        p.emit("gpsimd", lambda gp: gp.dma_start(
            out=self.Wslot[0].rearrange("p (g n) -> p g n", g=8),
            in_=d["w_v"].rearrange("(g p) n -> p g n", p=128)),
            waits=[(sm, None)], incs=[(sw2a, 16)])
        p.emit("gpsimd", lambda gp: gp.dma_start(
            out=self.Wslot[1].rearrange("p (g n) -> p g n", g=8),
            in_=d["w_g"].rearrange("(g p) n -> p g n", p=128)),
            waits=[(sm, None)], incs=[(sw2b, 16)])
        emit_xxx(0, True)
        for it in range(NTIL):
            tile_pass(it, pass2=True)
        self.p1_done = [(sst, sst.n), (sv, sv.n), (sa, sa.n), (sm, sm.n), (sl, sl.n)]
        for nm, buf in (("dvec", self.dvec), ("e3", self.e3)):
            dbg = self.dbg_out(nm, (128, buf.shape[1]))
            if dbg is not None:
                p.emit("sync", lambda sy, dbg=dbg, buf=buf: sy.dma_start(out=dbg, in_=buf),
                       waits=self.p1_done, incs=[(self.sem("dbg"), 16)])
        self.af.reset(m); self.ab.reset(mb)

    def prefetch_ffn0(self):
        p, d = self.p, self.dram
        s = self.sem("p4_gw")
        p.emit("gpsimd", lambda gp: gp.dma_start(
            out=self.wq_lo.rearrange("p (g n) -> p g n", g=3),
            in_=d["wqkv"][0].rearrange("(g p) n -> p g n", p=128)[:, 5:8, :]),
            waits=list(self.p1_done), incs=[(s, 16)])
        p.emit("gpsimd", lambda gp: gp.dma_start(
            out=self.wo2.rearrange("p (g n) -> p g n", g=4),
            in_=d["wo"][0].rearrange("(g p) n -> p g n", p=128)),
            waits=[], incs=[(s, 16)])

    def phase2(self, parts=31):
        p, T, d = self.p, self.T, self.dram
        NC = self.NCHK
        self.ab.reset(self.wmark_b)
        m = self.af.mark(); mb = self.ab.mark()
        self.yng = self.ab.alloc(self.NTT * C)
        self.p2_streams_mark = self.ab.mark()
        rqc = [self.ab.alloc(2 * C) for _ in range(2)]
        kkc = [self.ab.alloc(C) for _ in range(2)]
        vc = [self.ab.alloc(C) for _ in range(2)]
        gc = [self.ab.alloc(C) for _ in range(2)]
        s_rq = [self.sem(f"p2_rq{i}") for i in range(2)]
        s_kk = [self.sem(f"p2_kk{i}") for i in range(2)]
        s_v = [self.sem(f"p2_v{i}") for i in range(2)]
        s_g = [self.sem(f"p2_g{i}") for i in range(2)]
        kkt = self.ab.alloc(C)
        absb = self.ab.alloc(2048)
        Sst = self.af.alloc(8 * DH)
        stp = self.af.alloc(8 * DH)
        Sbf = self.ab.alloc(8 * DH)
        ysb = self.af.alloc(C)
        sqb = self.af.alloc(C)
        self.lnw_rep = self.af.alloc(C)
        self.lnb_rep = self.af.alloc(C)
        s_lnr = self.sem("p2_lnr")
        p.emit("sync", lambda sy: sy.dma_start(out=self.lnw_rep, in_=d["lnw_rep"]),
               waits=list(self.p1_done), incs=[(s_lnr, 16)])
        p.emit("sync", lambda sy: sy.dma_start(out=self.lnb_rep, in_=d["lnb_rep"]),
               waits=[], incs=[(s_lnr, 16)])
        st16 = self.af.alloc(7 * 16)
        col16 = lambda j: st16[:, j * 16:(j + 1) * 16]
        sm = self.sem("p2_mm"); sv = self.sem("p2_dv"); sa = self.sem("p2_act"); sl = self.sem("p2_pl")
        p.emit("vector", lambda v: v.memset(Sst, 0.0), waits=self.p1_done, incs=[(sv, 1)])
        wave_consumed = [None] * (NC + 2)

        def emit_loads(c, ldw):
            sl_ = c % 2
            p.emit("sync", lambda sy, c=c, b=rqc[sl_]: sy.dma_start(
                out=b.rearrange("p (g t) -> p g t", g=16),
                in_=d["rq_d"].rearrange("(g p) t -> p g t", p=128)[:, :, c * L:(c + 1) * L]),
                waits=ldw, incs=[(s_rq[sl_], 16)])
            p.emit("sync", lambda sy, c=c, b=kkc[sl_]: sy.dma_start(
                out=b.rearrange("p (g t) -> p g t", g=8),
                in_=d["kk_d"].rearrange("(g p) t -> p g t", p=128)[:, :, c * L:(c + 1) * L]),
                waits=ldw, incs=[(s_kk[sl_], 16)])
            p.emit("sync", lambda sy, c=c, b=vc[sl_]: sy.dma_start(
                out=b, in_=d["v_d"][c * L:(c + 1) * L, :]),
                waits=ldw, incs=[(s_v[sl_], 16)])
            p.emit("sync", lambda sy, c=c, b=gc[sl_]: sy.dma_start(
                out=b, in_=d["g2x_d"][c * L:(c + 1) * L, :]),
                waits=ldw, incs=[(s_g[sl_], 16)])

        emit_loads(0, list(self.p1_done))
        if NC > 1:
            emit_loads(1, list(self.p1_done))
        for c in range(NC):
            sl_ = c % 2
            RQ, KK, V, G = rqc[sl_], kkc[sl_], vc[sl_], gc[sl_]
            pw0b = self.pw[0].ap().bitcast(BF16)
            def trf(t, KK=KK):
                last = None
                for j in range(8):
                    last = t.transpose(pw0b[:, j * 128:(j + 1) * 128],
                                       KK[:, j * 128:(j + 1) * 128], self.identb)
                return last
            if not (parts & 1):
                continue
            p.emit("tensor", trf, waits=[(s_kk[sl_], s_kk[sl_].n)] + self.guard(("p2pw0",)),
                   incs=[(sm, 1)])
            p.emit("scalar", lambda sc: sc.activation(kkt, pw0b[:, 0:1024], AF.Copy),
                   waits=[(sm, None)], incs=[(sa, 1)])
            kkt_done = sa.n
            p.emit("scalar", lambda sc: sc.activation(Sbf, Sst, AF.Copy),
                   waits=[(sv, None), (sm, None)], incs=[(sa, 1)])
            sbf_done = sa.n
            bank_heads = [(self.ps[0], (0, 2, 4, 6)), (self.ps[1], (8, 10, 12, 14)),
                          (self.ps[2], (1, 3, 5, 7)), (self.ps[3], (9, 11, 13, 15))]
            for bank, heads in (bank_heads if parts & 2 else []):
                def amm(t, bank=bank, heads=heads, RQ=RQ, KK=KK):
                    last = None
                    for hh, h in enumerate(heads):
                        ct = h // 2; r0 = 64 * (h % 2)
                        last = t.matmul(bank.ap()[:, hh * 128:(hh + 1) * 128],
                                        KK[r0:r0 + 64, ct * 128:(ct + 1) * 128],
                                        RQ[r0:r0 + 64, h * 128:(h + 1) * 128],
                                        start=True, stop=True)
                    return last
                p.emit("tensor", amm, waits=[(s_rq[sl_], s_rq[sl_].n), (s_kk[sl_], s_kk[sl_].n)] +
                       self.guard(("Aps", bank_heads.index((bank, heads)) if False else id(bank))),
                       incs=[(sm, 1)])
                p.emit("scalar", lambda sc, bank=bank, heads=heads: sc.activation(
                    absb.rearrange("p (h t) -> p h t", h=16)[:, heads[0]:heads[0] + 7:2, :],
                    bank.ap()[:].rearrange("p (h t) -> p h t", h=4), AF.Copy),
                    waits=[(sm, None)], incs=[(sa, 1)])
                p.emit("vector", lambda v, heads=heads: v.tensor_tensor(
                    absb.rearrange("p (h t) -> p h t", h=16)[:, heads[0]:heads[0] + 7:2, :],
                    absb.rearrange("p (h t) -> p h t", h=16)[:, heads[0]:heads[0] + 7:2, :],
                    self.maskA.rearrange("p (h t) -> p h t", h=4), ALU.mult),
                    waits=[(sa, None)], incs=[(sv, 1)])
                self.set_guard(("Aps", id(bank)), sv)
            aev_done = sv.n
            for h in (range(16) if parts & 4 else []):
                ct = h // 2; r0 = 64 * (h % 2)
                def ymm(t, h=h, ct=ct, r0=r0, RQ=RQ, V=V):
                    t.matmul(self.pw[0].ap()[:, h * 64:(h + 1) * 64],
                             absb[:, h * 128:(h + 1) * 128],
                             V[:, h * 64:(h + 1) * 64], start=True, stop=False)
                    return t.matmul(self.pw[0].ap()[:, h * 64:(h + 1) * 64],
                                    RQ[:, h * 128:(h + 1) * 128],
                                    Sbf[:, ct * 64:(ct + 1) * 64],
                                    start=False, stop=True)
                p.emit("tensor", ymm, waits=[(sv, aev_done), (sa, sbf_done), (s_v[sl_], s_v[sl_].n)] +
                       self.guard(("p2pw0",)), incs=[(sm, 1)])
            y_done = sm.n
            for h in (range(16) if parts & 4 else []):
                ct = h // 2; r0 = 64 * (h % 2)
                p.emit("tensor", lambda t, h=h, ct=ct, r0=r0, V=V: t.matmul(
                    self.pw[1].ap()[0:64, h * 64:(h + 1) * 64],
                    kkt[:, ct * 128 + r0: ct * 128 + r0 + 64],
                    V[:, h * 64:(h + 1) * 64], start=True, stop=True),
                    waits=[(sa, kkt_done)], incs=[(sm, 1)])
            u_done = sm.n
            for h in (range(16) if parts & 8 else []):
                p.emit("vector", lambda v, h=h, c=c, V=V: v.scalar_tensor_tensor(
                    ysb[:, h * 64:(h + 1) * 64], V[:, h * 64:(h + 1) * 64],
                    self.dvec[:, c * 16 + h: c * 16 + h + 1],
                    self.pw[0].ap()[:, h * 64:(h + 1) * 64], ALU.mult, ALU.add),
                    waits=[(sm, y_done), (sv, None), (sl, None)], incs=[(sv, 1)])
            yev_done = sv.n
            self.set_guard(("p2pw0",), sv)
            if parts & 8:
                uview = self.pw[1].ap()[0:64, :].rearrange("p (h x e) -> p h x e", h=8, x=2)
                p.emit("vector", lambda v, uview=uview: v.tensor_tensor(
                    stp[0:64, :].rearrange("p (h e) -> p h e", h=8),
                    Sst[0:64, :].rearrange("p (h e) -> p h e", h=8),
                    uview[:, :, 0, :], ALU.add),
                    waits=[(sm, u_done), (sa, sbf_done)], incs=[(sv, 1)])
                p.emit("vector", lambda v, uview=uview: v.tensor_tensor(
                    stp[64:128, :].rearrange("p (h e) -> p h e", h=8),
                    Sst[64:128, :].rearrange("p (h e) -> p h e", h=8),
                    uview[:, :, 1, :], ALU.add),
                    waits=[(sm, u_done), (sv, None)], incs=[(sv, 1)])
                e3v = self.e3.rearrange("p (ct n) -> p ct n", ct=8)[:, :, c:c + 1]
                p.emit("vector", lambda v, e3v=e3v: v.tensor_tensor(
                    Sst.rearrange("p (ct e) -> p ct e", ct=8), stp.rearrange("p (ct e) -> p ct e", ct=8),
                    e3v.broadcast_to([128, 8, DH]), ALU.mult),
                    waits=[(sv, None)], incs=[(sv, 1)])
            if not (parts & 16):
                wave_consumed[c] = (sv.n, sm.n)
                if c + 2 < NC:
                    emit_loads(c + 2, [(sv, sv.n), (sm, sm.n)])
                continue
            p.emit("gpsimd", lambda g_: g_.tensor_tensor(sqb, ysb, ysb, ALU.mult),
                   waits=[(sv, yev_done), (sv, None)], incs=[(sl, 1)])
            ysum, sqs, mu, msq, var, lnv, rstd = (col16(j) for j in range(7))
            p.emit("vector", lambda v: v.tensor_reduce(
                ysum, ysb.rearrange("p (h j) -> p h j", h=16), AX.X, ALU.add),
                waits=[(sv, yev_done)], incs=[(sv, 1)])
            p.emit("vector", lambda v: v.tensor_reduce(
                sqs, sqb.rearrange("p (h j) -> p h j", h=16), AX.X, ALU.add),
                waits=[(sl, None)], incs=[(sv, 1)])
            p.emit("vector", lambda v: v.tensor_scalar_mul(mu, ysum, 1.0 / DH),
                   waits=[(sv, None)], incs=[(sv, 1)])
            p.emit("vector", lambda v: v.tensor_tensor(msq, mu, mu, ALU.mult),
                   waits=[(sv, None)], incs=[(sv, 1)])
            p.emit("vector", lambda v: v.scalar_tensor_tensor(
                var, sqs, 1.0 / DH, msq, ALU.mult, ALU.subtract),
                waits=[(sv, None)], incs=[(sv, 1)])
            p.emit("scalar", lambda sc: sc.activation(lnv, var, AF.Ln, bias=self.cc[GN_EPS]),
                   waits=[(sv, None)], incs=[(sa, 1)])
            p.emit("scalar", lambda sc: sc.activation(rstd, lnv, AF.Exp, scale=-0.5),
                   waits=[(sa, None)], incs=[(sa, 1)])
            for h in range(16):
                p.emit("vector", lambda v, h=h: v.tensor_scalar(
                    sqb[:, h * 64:(h + 1) * 64], ysb[:, h * 64:(h + 1) * 64],
                    mu[:, h:h + 1], rstd[:, h:h + 1], ALU.subtract, ALU.mult),
                    waits=[(sa, None), (sl, None)], incs=[(sv, 1)])
            p.emit("vector", lambda v: v.tensor_tensor(ysb, sqb, self.lnw_rep, ALU.mult),
                   waits=[(sv, None), (sl, None), (self.sem("p2_lnr"), 32)], incs=[(sv, 1)])
            p.emit("vector", lambda v: v.tensor_tensor(sqb, ysb, self.lnb_rep, ALU.add),
                   waits=[(sv, None)], incs=[(sl, 1)])
            p.emit("vector", lambda v, c=c, G=G: v.scalar_tensor_tensor(
                self.yng[:, c * C:(c + 1) * C], sqb, 0.5, G, ALU.mult, ALU.mult),
                waits=[(sl, None), (s_g[sl_], s_g[sl_].n)], incs=[(sv, 1)])
            wave_consumed[c] = (sv.n, sm.n)
            if c + 2 < NC:
                emit_loads(c + 2, [(sv, sv.n), (sm, sm.n)])
        self.p2_done = [(sv, sv.n), (sm, sm.n), (sa, sa.n), (sl, sl.n)]
        dbg = self.dbg_out("yng", (128, self.NTT * C), BF16)
        if dbg is not None:
            p.emit("sync", lambda sy, dbg=dbg: sy.dma_start(out=dbg, in_=self.yng),
                   waits=self.p2_done, incs=[(self.sem("dbg"), 16)])
        self.af.reset(m)
        self.p2_mb = mb

    def phase3(self):
        p, T, d = self.p, self.T, self.dram
        m = self.af.mark()
        self.ab.reset(self.p2_streams_mark)
        yngf = self.ab.alloc(C)
        self.wo_sb = self.ab.alloc(8 * C)
        s_til = [self.af.alloc(C) for _ in range(2)]
        sm = self.sem("p3_mm"); sv = self.sem("p3_dv"); sa = self.sem("p3_act")
        ssts = [self.sem(f"p3_st{i}") for i in range(2)]
        ps3b = self.ps[3].ap().bitcast(BF16)
        s_wo = self.sem("p2_wo")
        p.emit("gpsimd", lambda gp: gp.dma_start(
            out=self.wo_sb.rearrange("p (g n) -> p g n", g=8),
            in_=d["w_o"].rearrange("(g p) n -> p g n", p=128)),
            waits=list(self.p2_done), incs=[(s_wo, 16)])
        for tt in range(self.NTT):
            def trf(t, tt=tt):
                last = None
                for ct in range(8):
                    last = t.transpose(ps3b[:, ct * 128:(ct + 1) * 128],
                                       self.yng[:, tt * C + ct * 128: tt * C + (ct + 1) * 128],
                                       self.identb)
                return last
            w_ = [(sv, None)] + (self.p2_done if tt == 0 else [])
            p.emit("tensor", trf, waits=w_, incs=[(sm, 1)])
            p.emit("vector", lambda v: v.tensor_copy(yngf, ps3b), waits=[(sm, None)], incs=[(sv, 1)])
            def womm(t, tt=tt):
                last = None
                for ct in range(8):
                    for nh in range(2):
                        last = t.matmul(self.pw[0].ap()[:, nh * 512:(nh + 1) * 512],
                                        yngf[:, ct * 128:(ct + 1) * 128],
                                        self.wo_sb[:, ct * C + nh * 512: ct * C + nh * 512 + 512],
                                        start=(ct == 0), stop=(ct == 7))
                return last
            p.emit("tensor", womm, waits=[(sv, None), (s_wo, 16), (sa, None)], incs=[(sm, 1)])
            stl = s_til[tt % 2]
            sst = ssts[tt % 2]
            w_ = [(sm, None)]
            if tt >= 2:
                w_.append((sst, sst.n))
            p.emit("scalar", lambda sc, stl=stl: sc.activation(stl, self.pw[0].ap()[:], AF.Copy),
                   waits=w_, incs=[(sa, 1)])
            p.emit("sync", lambda sy, tt=tt, stl=stl: sy.dma_start(
                out=d["s_d"][tt * 128:(tt + 1) * 128, :], in_=stl),
                waits=[(sa, None)], incs=[(sst, 16)])
        self.p3_done = [(ssts[0], ssts[0].n), (ssts[1], ssts[1].n), (sm, sm.n), (sv, sv.n), (sa, sa.n)]
        self.af.reset(m)
        self.ab.reset(self.p2_mb)

    def phase4(self):
        p, T, d = self.p, self.T, self.dram
        NW = self.NW
        sm = self.sem("p4_mm"); sv = self.sem("p4_dv"); sa = self.sem("p4_act"); sl = self.sem("p4_pl")
        sld = self.sem("p4_ld")
        sgw = self.sem("p4_gw")
        sg1 = [self.sem(f"p4_g1{i}") for i in range(2)]
        sg2 = [self.sem(f"p4_g2{i}") for i in range(2)]
        sh4 = [self.sem(f"p4_h4{i}") for i in range(2)]
        sst4 = self.sem("p4_st")
        mf0 = self.af.mark()
        xf = self.af.alloc(8 * (T + 1))
        self.xf = xf
        for ct in range(8):
            p.emit("vector", lambda v, ct=ct: v.tensor_copy(
                xf[:, ct * (T + 1):(ct + 1) * (T + 1)], self.xn[:, ct * (T + 1):(ct + 1) * (T + 1)]),
                waits=(list(self.p3_done) + [(sv, None)]) if ct == 0 else [], incs=[(sv, 1)])
        xf_init = sv.n
        self.ab.reset(self.xn_mark)
        qtm = self.af.alloc(512); q2b = self.af.alloc(512)
        rot1 = self.af.alloc(512); rot2 = self.af.alloc(512)
        s8 = self.af.alloc(8); l8 = self.af.alloc(8); r8 = self.af.alloc(8)
        r_mu = qtm[:, 0:512]; r_e2 = q2b[:, 0:512]
        r_ms = rot1[:, 0:512]; r_va = rot2[:, 0:512]
        r_rs = self.af.alloc(512); r_msb = self.af.alloc(512)
        onesrow = self.af.alloc(128)
        p.emit("vector", lambda v: v.memset(onesrow[0:1, :], 1.0), incs=[(sv, 1)])
        qsc = self.af.alloc(FDH); ksc = self.af.alloc(FDH)
        lngv = self.af.alloc(8); lnbv = self.af.alloc(8); b2v = self.af.alloc(8)
        b1v = self.af.alloc(44)
        scr1 = self.af.alloc(512); scr2 = self.af.alloc(512)
        rcp4 = self.af.alloc(4)
        m4a = self.ab.mark()

        def attn_stage(l):
            self.ab.reset(m4a)
            wq_hi = self.ab.alloc(5 * 3 * INNER)
            wo_sb2 = self.wo2
            p.emit("gpsimd", lambda gp, l=l: gp.dma_start(
                out=wq_hi.rearrange("p (g n) -> p g n", g=5),
                in_=d["wqkv"][l].rearrange("(g p) n -> p g n", p=128)[:, 0:5, :]),
                waits=[(sm, None), (sv, None)], incs=[(sgw, 16)])
            tabs_raw = {}
            for nm in ("ct_q", "st_q", "ct_k", "st_k"):
                t = self.ab.alloc(self.NTT * FDH)
                tabs_raw[nm] = t
                p.emit("gpsimd", lambda gp, t=t, nm=nm: gp.dma_start(
                    out=t.rearrange("p (g n) -> p g n", g=self.NTT),
                    in_=d[nm].rearrange("(g p) n -> p g n", p=128)),
                    waits=([(sv, xf_init)] + list(self.p3_done)) if l == 0 else
                          [(sm, None), (sv, None)],
                    incs=[(sgw, 16)])
            gw_t = sgw.n
            for nm, t in (("lng", lngv), ("lnb2", lnbv), ("b2", b2v), ("b1p", b1v)):
                p.emit("sync", lambda sy, nm=nm, t=t, l=l: sy.dma_start(out=t, in_=d[nm][l]),
                       waits=self.guard(("smallvec",)), incs=[(sld, 16)])
            ld_t = sld.n
            tabs = tabs_raw
            qscb = self.ab.alloc(FDH)
            kscb = self.ab.alloc(FDH)
            p.emit("gpsimd", lambda gp, l=l: gp.dma_start(out=qscb, in_=d["qscale_rep"][l]),
                   waits=[(sm, None), (sv, None)], incs=[(sgw, 16)])
            p.emit("gpsimd", lambda gp, l=l: gp.dma_start(out=kscb, in_=d["kscale_rep"][l]),
                   waits=[], incs=[(sgw, 16)])
            gw_t = sgw.n
            for nm, scv in (("ct_q", qscb), ("st_q", qscb), ("ct_k", kscb), ("st_k", kscb)):
                p.emit("vector", lambda v, nm=nm, scv=scv: v.tensor_tensor(
                    tabs_raw[nm].rearrange("p (g e) -> p g e", g=self.NTT),
                    tabs_raw[nm].rearrange("p (g e) -> p g e", g=self.NTT),
                    scv.unsqueeze(1).broadcast_to([128, self.NTT, FDH]), ALU.mult),
                    waits=[(sgw, gw_t), (sv, None)], incs=[(sv, 1)])
            zwin = self.ab.alloc(8 * 512)
            kfm = [self.ab.alloc(4 * 512) for _ in range(3)]
            vtl = [self.ab.alloc(4 * 520) for _ in range(3)]
            qfm = self.ab.alloc(4 * 512)
            attn2 = [self.ab.alloc(2560) for _ in range(2)]
            otm = self.ab.alloc(4 * 512)
            b_rotB = otm
            ofm = self.ab.alloc(4 * 512)
            b_sqB = ofm
            q_tm = self.ab.alloc(2048)
            k_tm = self.ab.alloc(2048)
            s32 = self.af.alloc(32)
            xbnd = self.ab.alloc(8 * NW)
            assert self.ab.off <= 46900, f"attn allocs {self.ab.off} hit prefetch region"
            for rwi in range(3):
                p.emit("vector", lambda v, rwi=rwi: v.memset(
                    vtl[rwi].rearrange("p (s h e) -> p s h e", s=4, h=8)[:, :, :, 64:65], 1.0),
                    waits=[(sv, None), (sm, None)], incs=[(sv, 1)])
            p.emit("vector", lambda v: v.tensor_copy(
                xbnd.rearrange("p (c w2) -> p c w2", c=8),
                xf.rearrange("p (c t) -> p c t", c=8, t=T + 1)[:, :, 0:T:512]),
                waits=[(sv, None), (sm, None)], incs=[(sv, 1)])
            xbnd_done = sv.n

            def spans(w):
                out = []
                if w >= 1:
                    for kt in range(4):
                        out.append((kt, (w - 1) % 3, kt, 0, 128 * (kt + 1), "low", 128 * kt))
                for kt in range(4, 8):
                    tq0 = 128 * (kt - 4)
                    out.append((kt, w % 3, kt - 4, tq0, 512 - tq0, "up", 0))
                return out

            ps3b = self.ps[3].ap().bitcast(BF16)

            qk_evac = {}
            qk_done = {}

            def front(w):
                sp = spans(w)
                offs = {}
                o = 0
                for e in sp:
                    offs[e[0]] = o
                    o += e[4]
                for ct in range(8):
                    sh = 1 if ct >= 4 else 0
                    p.emit("vector", lambda v, ct=ct, w=w, sh=sh: v.tensor_copy(
                        zwin[:, ct * 512:(ct + 1) * 512],
                        xf[:, ct * (T + 1) + 1 + w * 512 - sh: ct * (T + 1) + 1 + (w + 1) * 512 - sh]),
                        waits=[(sv, None)] + self.guard(("zwin", ct)), incs=[(sv, 1)])
                for ct in range(4, 8):
                    p.emit("vector", lambda v, ct=ct, w=w: v.tensor_copy(
                        zwin[:, ct * 512:ct * 512 + 1], xbnd[:, ct * NW + w: ct * NW + w + 1]),
                        waits=[(sv, None)], incs=[(sv, 1)])
                zdone = sv.n
                for ts_ in range(4):
                    for wh, pb in ((0, self.ps[0]), (1, self.ps[1]), (2, self.ps[2])):
                        def qkvmm(t, ts_=ts_, wh=wh, pb=pb):
                            last = None
                            for k in range(8):
                                wsrc = (wq_hi[:, k * 1536 + wh * 512: k * 1536 + (wh + 1) * 512]
                                        if k < 5 else
                                        self.wq_lo[:, (k - 5) * 1536 + wh * 512: (k - 5) * 1536 + (wh + 1) * 512])
                                last = t.matmul(pb.ap()[:],
                                                zwin[:, k * 512 + ts_ * 128: k * 512 + ts_ * 128 + 128],
                                                wsrc,
                                                start=(k == 0), stop=(k == 7))
                            return last
                        p.emit("tensor", qkvmm,
                               waits=[(sv, zdone), (sgw, gw_t)] + self.guard(("ps", wh)),
                               incs=[(sm, 1)])
                    p.emit("scalar", lambda sc, ts_=ts_, w=w: sc.activation(
                        vtl[w % 3].rearrange("p (s h e) -> p s h e", s=4, h=8)[:, ts_, :, 0:64],
                        self.ps[2].ap()[:].rearrange("p (h e) -> p h e", h=8), AF.Copy),
                        waits=[(sm, None)] + self.guard(("vtlrd", w % 3)), incs=[(sa, 1)])
                    self.set_guard(("ps", 2), sa)
                    p.emit("scalar", lambda sc, ts_=ts_: sc.activation(
                        q_tm[:, ts_ * 512:(ts_ + 1) * 512], self.ps[0].ap()[:], AF.Copy),
                        waits=[(sm, None)] + self.guard(("q_tm",)), incs=[(sa, 1)])
                    self.set_guard(("ps", 0), sa)
                    p.emit("scalar", lambda sc, ts_=ts_: sc.activation(
                        k_tm[:, ts_ * 512:(ts_ + 1) * 512], self.ps[1].ap()[:], AF.Copy),
                        waits=[(sm, None)] + self.guard(("k_tm",)), incs=[(sa, 1)])
                    self.set_guard(("ps", 1), sa)
                qk_evac[w] = sa.n
                for ct in range(8):
                    self.set_guard(('zwin', ct), sm)
                return sp, offs

            def chains(w):
                    for which, src_tm, ct_t, st_t, dstfm in (
                            (0, q_tm, "ct_q", "st_q", qfm),
                            (1, k_tm, "ct_k", "st_k", kfm[w % 3])):
                        p.emit("scalar", lambda sc, src_tm=src_tm: sc.activation(
                            b_sqB, src_tm, AF.Square),
                            waits=[(sa, qk_evac[w])] + self.guard(("b_sq",)), incs=[(sa, 1)])
                        p.emit("vector", lambda v: v.tensor_reduce(
                            s32, b_sqB.rearrange("p (g e) -> p g e", g=32), AX.X, ALU.add),
                            waits=[(sa, None)] + self.guard(("s32",)), incs=[(sv, 1)])
                        p.emit("scalar", lambda sc: sc.activation(s32, s32, AF.Ln, bias=self.cc[1e-30]),
                               waits=[(sv, None)], incs=[(sa, 1)])
                        p.emit("scalar", lambda sc: sc.activation(s32, s32, AF.Exp, scale=-0.5),
                               waits=[(sa, None)], incs=[(sa, 1)])
                        p.emit("vector", lambda v: v.tensor_scalar_min(s32, s32, 1e12),
                               waits=[(sa, None)], incs=[(sv, 1)])
                        p.emit("vector", lambda v, src_tm=src_tm: v.tensor_tensor(
                            b_sqB.rearrange("p (g e) -> p g e", g=32),
                            src_tm.rearrange("p (g e) -> p g e", g=32),
                            s32.unsqueeze(2).broadcast_to([128, 32, FDH]), ALU.mult),
                            waits=[(sv, None)], incs=[(sv, 1)])
                        ctab = tabs[ct_t][:, w * 256:(w + 1) * 256]
                        stab = tabs[st_t][:, w * 256:(w + 1) * 256]
                        p.emit("vector", lambda v, src_tm=src_tm, ctab=ctab: v.tensor_tensor(
                            src_tm.rearrange("p (s h e) -> p s h e", s=4, h=8),
                            b_sqB.rearrange("p (s h e) -> p s h e", s=4, h=8),
                            ctab.rearrange("p (s e) -> p s e", s=4).unsqueeze(2).broadcast_to(
                                [128, 4, 8, FDH]), ALU.mult),
                            waits=[(sv, None)], incs=[(sv, 1)])
                        for half in range(2):
                            p.emit("vector", lambda v, half=half, stab=stab: v.tensor_tensor(
                                b_rotB.rearrange("p (s h x e) -> p s h x e", s=4, h=8, x=2)[:, :, :, half, :],
                                b_sqB.rearrange("p (s h x e) -> p s h x e", s=4, h=8, x=2)[:, :, :, 1 - half, :],
                                stab.rearrange("p (s x e) -> p s x e", s=4, x=2)[:, :, half, :]
                                .unsqueeze(2).broadcast_to([128, 4, 8, 32]), ALU.mult),
                                waits=[(sv, None)], incs=[(sv, 1)])
                        p.emit("vector", lambda v, src_tm=src_tm: v.tensor_tensor(
                            b_sqB, src_tm, b_rotB, ALU.add),
                            waits=[(sv, None)], incs=[(sv, 1)])
                        fin_v = sv.n
                        for ts_ in range(4):
                            po = which * 512
                            def trq(t, ts_=ts_, po=po):
                                last = None
                                for ic in range(4):
                                    last = t.transpose(ps3b[:, po + ic * 128: po + (ic + 1) * 128],
                                                       b_sqB[:, ts_ * 512 + ic * 128: ts_ * 512 + (ic + 1) * 128],
                                                       self.identb)
                                return last
                            p.emit("tensor", trq, waits=[(sv, fin_v)] + self.guard(("ps3b", which)),
                                   incs=[(sm, 1)])
                            p.emit("vector", lambda v, ts_=ts_, dstfm=dstfm, po=po: v.tensor_copy(
                                dstfm.rearrange("p (ic t2) -> p ic t2", ic=4)[:, :, ts_ * 128:(ts_ + 1) * 128],
                                ps3b[:, po:po + 512].rearrange("p (ic t2) -> p ic t2", ic=4)),
                                waits=[(sm, None)], incs=[(sv, 1)])
                            self.set_guard(("ps3b", which), sv)
                        if which == 0:
                            self.set_guard(("q_tm",), sm)
                        else:
                            self.set_guard(("k_tm",), sm)
                        self.set_guard(("b_sq",), sm)
                        self.set_guard(("s32",), sv)
                    qk_done[w] = sv.n

            def attn(w, sp, offs):
                    sslot = {0: self.pw[1].ap()[:, 0:512], 1: self.pw[1].ap()[:, 512:1024]}
                    mask_done = {}

                    def emit_sblock(h):
                        ab_ = attn2[h % 2]
                        r0 = 64 * (h % 2); ic = h // 2
                        for ei, (kt, rw, ts_, tq0, span, mk, mko) in enumerate(sp):
                            slot = sslot[h % 2][:, 0:span]
                            kf_src = kfm[rw]
                            p.emit("tensor", lambda t, kf_src=kf_src, r0=r0, ic=ic, ts_=ts_, tq0=tq0, span=span, slot=slot: t.matmul(
                                slot, kf_src[r0:r0 + 64, ic * 512 + ts_ * 128: ic * 512 + (ts_ + 1) * 128],
                                qfm[r0:r0 + 64, ic * 512 + tq0: ic * 512 + tq0 + span],
                                start=True, stop=True),
                                waits=[(sv, qk_done[w])] + self.guard(("sslot", h % 2)),
                                incs=[(sm, 1)])
                            oo = offs[kt]
                            p.emit("scalar", lambda sc, slot=slot, ab_=ab_, oo=oo, span=span: sc.activation(
                                ab_[:, oo:oo + span], slot, AF.Exp, scale=8.0),
                                waits=[(sm, None)] + self.guard(("attn2", h % 2)), incs=[(sa, 1)])
                            self.set_guard(("sslot", h % 2), sa)
                            msk = self.mask_low if mk == "low" else self.mask_up
                            p.emit("vector", lambda v, ab_=ab_, oo=oo, mko=mko, msk=msk: v.tensor_tensor(
                                ab_[:, oo + mko: oo + mko + 128],
                                ab_[:, oo + mko: oo + mko + 128], msk, ALU.mult),
                                waits=[(sa, None)], incs=[(sv, 1)])
                        mask_done[h] = sv.n

                    def emit_oblock(h):
                        ab_ = attn2[h % 2]
                        for mi in range(4):
                            oslot = self.pw[0].ap()[:, (mi % 2) * 512:(mi % 2) * 512 + 65]
                            kts = [e for e in sp if e[3] <= mi * 128 < e[3] + e[4]]
                            def omm(t, kts=kts, mi=mi, oslot=oslot, ab_=ab_, h=h, offs=dict(offs)):
                                last = None
                                for j, (kt, rw, ts_, tq0, span, mk, mko) in enumerate(kts):
                                    last = t.matmul(oslot,
                                                    ab_[:, offs[kt] + mi * 128 - tq0: offs[kt] + mi * 128 - tq0 + 128],
                                                    vtl[rw].rearrange("p (s h2 e) -> p s h2 e", s=4, h2=8)[:, ts_, h, :],
                                                    start=(j == 0), stop=(j == len(kts) - 1))
                                return last
                            p.emit("tensor", omm, waits=[(sv, mask_done[h])] + self.guard(("oslot", mi % 2)),
                                   incs=[(sm, 1)])
                            p.emit("vector", lambda v, oslot=oslot, mi=mi: v.reciprocal(
                                rcp4[:, mi:mi + 1], oslot[:, 64:65]),
                                waits=[(sm, None)], incs=[(sv, 1)])
                            p.emit("vector", lambda v, oslot=oslot, mi=mi, h=h: v.tensor_single_scalar(
                                otm[:, mi * 512 + h * 64: mi * 512 + (h + 1) * 64],
                                oslot[:, 0:64], rcp4[:, mi:mi + 1], ALU.mult),
                                waits=[(sv, None)], incs=[(sv, 1)])
                            self.set_guard(("oslot", mi % 2), sv)
                        self.set_guard(("attn2", h % 2), sm)

                    for h in range(8):
                        emit_sblock(h)
                        if h > 0:
                            emit_oblock(h - 1)
                    emit_oblock(7)
                    o_done = sv.n
                    for mi in range(4):
                        po = 0
                        def tro(t, mi=mi, po=po):
                            last = None
                            for ic in range(4):
                                last = t.transpose(ps3b[:, po + ic * 128: po + (ic + 1) * 128],
                                                   otm[:, mi * 512 + ic * 128: mi * 512 + (ic + 1) * 128],
                                                   self.identb)
                            return last
                        p.emit("tensor", tro, waits=[(sv, o_done)] + self.guard(("ps3b", 0)),
                               incs=[(sm, 1)])
                        p.emit("vector", lambda v, mi=mi, po=po: v.tensor_copy(
                            ofm.rearrange("p (ic t2) -> p ic t2", ic=4)[:, :, mi * 128:(mi + 1) * 128],
                            ps3b[:, po:po + 512].rearrange("p (ic t2) -> p ic t2", ic=4)),
                            waits=[(sm, None)], incs=[(sv, 1)])
                        self.set_guard(("ps3b", 0), sv)
                    ofm_done = sv.n
                    for ctO in range(8):
                        wslice = self.pw[1].ap()[:, (ctO % 2) * 512:(ctO % 2) * 512 + 512]
                        def womm(t, ctO=ctO, wslice=wslice):
                            last = None
                            for icg in range(4):
                                last = t.matmul(wslice,
                                                wo_sb2[:, icg * C + ctO * 128: icg * C + ctO * 128 + 128],
                                                ofm[:, icg * 512:(icg + 1) * 512],
                                                start=(icg == 0), stop=(icg == 3))
                            return last
                        p.emit("tensor", womm,
                               waits=[(sv, ofm_done), (sgw, gw_t)] + self.guard(("sslot", ctO % 2)),
                               incs=[(sm, 1)])
                        p.emit("vector", lambda v, ctO=ctO, w=w, wslice=wslice: v.tensor_tensor(
                            xf[:, ctO * (T + 1) + 1 + w * 512: ctO * (T + 1) + 1 + (w + 1) * 512],
                            xf[:, ctO * (T + 1) + 1 + w * 512: ctO * (T + 1) + 1 + (w + 1) * 512],
                            wslice, ALU.add),
                            waits=[(sm, None)], incs=[(sv, 1)])
                        self.set_guard(("sslot", ctO % 2), sv)
                    self.set_guard(("vtlrd", (w + 2) % 3), sm)
                    self.set_guard(("b_sq",), sm)

            state = {}
            state[0] = front(0)
            chains(0)
            for w in range(NW):
                if w + 1 < NW:
                    state[w + 1] = front(w + 1)
                attn(w, *state[w])
                if w + 1 < NW:
                    chains(w + 1)

        def geglu_stage(l):
            self.ab.reset(m4a)
            ge_m = sm.n; ge_v = sv.n
            z1 = self.ab.alloc(8 * 512 * (T // 512))
            zbuf = self.ab.alloc(8 * 512)
            zqb = self.ab.alloc(8 * 512)
            ggb = [self.ab.alloc(512) for _ in range(2)]
            h4b = [self.ab.alloc(512) for _ in range(2)]
            w1q = [self.ab.alloc(8 * 1536) for _ in range(2)]
            sh4st = [self.sem(f"p4_h4st{l}{i}") for i in range(2)]
            NT4 = T // 512
            z1_t = [None] * NT4
            pw0row = self.pw[0].ap()[0:1, 512:1024]
            pw1row = self.pw[1].ap()[0:1, 512:1024]
            pw0bc = self.pw[0].ap()[:, 0:512]
            pw1bc = self.pw[1].ap()[:, 0:512]
            def emit_ln(tt4):
                for ct in range(8):
                    sh = 1 if ct >= 4 else 0
                    w_ = [(sv, None)] + self.guard(("zbuf",))
                    if tt4 == 0 and ct == 0:
                        w_.append((sm, ge_m))
                    p.emit("vector", lambda v, ct=ct, tt4=tt4, sh=sh: v.tensor_copy(
                        zbuf[:, ct * 512:(ct + 1) * 512],
                        xf[:, ct * (T + 1) + 1 + tt4 * 512 - sh: ct * (T + 1) + 1 + (tt4 + 1) * 512 - sh]),
                        waits=w_, incs=[(sv, 1)])
                    p.emit("scalar", lambda sc, ct=ct: sc.activation(
                        zqb[:, ct * 512:(ct + 1) * 512], zbuf[:, ct * 512:(ct + 1) * 512], AF.Square),
                        waits=[(sv, None)] + self.guard(("zqb",)), incs=[(sa, 1)])
                zd = sv.n; zqd = sa.n
                def statmm(t, out, buf):
                    last = None
                    for k in range(8):
                        last = t.matmul(out, self.onescol[:, 0:1],
                                        buf[:, k * 512:(k + 1) * 512], start=(k == 0), stop=(k == 7))
                    return last
                p.emit("tensor", lambda t: statmm(t, pw0row, zbuf),
                       waits=[(sv, zd)] + self.guard(("geps", 0)), incs=[(sm, 1)])
                p.emit("tensor", lambda t: statmm(t, pw1row, zqb),
                       waits=[(sa, zqd)] + self.guard(("geps", 1)), incs=[(sm, 1)])
                mm_t = sm.n
                self.set_guard(("zbuf",), sm)
                self.set_guard(("zqb",), sm)
                p.emit("vector", lambda v: v.tensor_scalar_mul(r_mu[0:1, :], pw0row, 1.0 / C),
                       waits=[(sm, mm_t), (sv, None)], incs=[(sv, 1)])
                p.emit("vector", lambda v: v.tensor_scalar_mul(r_e2[0:1, :], pw1row, 1.0 / C),
                       waits=[(sv, None)], incs=[(sv, 1)])
                self.set_guard(("geps", 0), sv)
                self.set_guard(("geps", 1), sv)
                p.emit("vector", lambda v: v.tensor_tensor(r_ms[0:1, :], r_mu[0:1, :], r_mu[0:1, :], ALU.mult),
                       waits=[(sv, None)], incs=[(sv, 1)])
                p.emit("vector", lambda v: v.tensor_tensor(r_va[0:1, :], r_e2[0:1, :], r_ms[0:1, :], ALU.subtract),
                       waits=[(sv, None)], incs=[(sv, 1)])
                p.emit("scalar", lambda sc: sc.activation(r_va[0:1, :], r_va[0:1, :], AF.Ln,
                                                          bias=self.cc[LN_EPS][0:1, :]),
                       waits=[(sv, None)], incs=[(sa, 1)])
                p.emit("scalar", lambda sc: sc.activation(r_rs[0:1, :], r_va[0:1, :], AF.Exp, scale=-0.5),
                       waits=[(sa, None)], incs=[(sa, 1)])
                p.emit("vector", lambda v: v.tensor_tensor(r_msb[0:1, :], r_mu[0:1, :], r_rs[0:1, :], ALU.mult),
                       waits=[(sa, None), (sv, None)], incs=[(sv, 1)])
                rows_done_v = sv.n; rows_done_a = sa.n
                p.emit("tensor", lambda t: t.matmul(pw0bc, onesrow[0:1, :], r_rs[0:1, :],
                                                    start=True, stop=True),
                       waits=[(sa, rows_done_a), (sv, rows_done_v)] + self.guard(("geps", 2)),
                       incs=[(sm, 1)])
                p.emit("tensor", lambda t: t.matmul(pw1bc, onesrow[0:1, :], r_msb[0:1, :],
                                                    start=True, stop=True),
                       waits=[(sv, rows_done_v)] + self.guard(("geps", 3)), incs=[(sm, 1)])
                bc_t = sm.n
                for ct in range(8):
                    sh = 1 if ct >= 4 else 0
                    p.emit("vector", lambda v, ct=ct, tt4=tt4, sh=sh: v.tensor_tensor(
                        scr1, xf[:, ct * (T + 1) + 1 + tt4 * 512 - sh: ct * (T + 1) + 1 + (tt4 + 1) * 512 - sh],
                        pw0bc, ALU.mult),
                        waits=[(sm, bc_t), (sv, None)], incs=[(sv, 1)])
                    p.emit("vector", lambda v: v.tensor_tensor(scr2, scr1, pw1bc, ALU.subtract),
                           waits=[(sv, None)], incs=[(sv, 1)])
                    p.emit("vector", lambda v, ct=ct, tt4=tt4: v.tensor_scalar(
                        z1[:, ct * T + tt4 * 512: ct * T + (tt4 + 1) * 512], scr2,
                        lngv[:, ct:ct + 1], lnbv[:, ct:ct + 1], ALU.mult, ALU.add),
                        waits=[(sv, None), (sld, None)], incs=[(sv, 1)])
                self.set_guard(("geps", 2), sv)
                self.set_guard(("geps", 3), sv)
                z1_t[tt4] = sv.n
            quarters = [(0, 6), (6, 5), (11, 6), (17, 5)]
            q_cons = [None] * 6
            stc = [0]

            def emit_q_load(qi):
                p0, np_ = quarters[qi]
                wb = w1q[qi % 2]
                qw = np_ * 256
                ldw = [(sm, ge_m), (sv, ge_v)] if qi < 2 else [(sm, q_cons[qi - 2])]
                p.emit("gpsimd", lambda gp, l=l, p0=p0, qw=qw, wb=wb: gp.dma_start(
                    out=wb.rearrange("p (g n) -> p g n", g=8)[:, :, 0:qw],
                    in_=d["w1p"][l][:, p0 * 256: p0 * 256 + qw].rearrange(
                        "(g p) n -> p g n", p=128)),
                    waits=ldw, incs=[(sg1[qi % 2], 16)])

            def emit_gemm_tile(qi, tt4):
                p0, np_ = quarters[qi]
                wb = w1q[qi % 2]
                for pi in range(np_):
                    step = stc[0]
                    i_glob = p0 + pi
                    pa = self.ps[2 * (step % 2)]
                    pg = self.ps[2 * (step % 2) + 1]
                    for agi, pb in ((0, pa), (1, pg)):
                        def w1mm(t, pi=pi, agi=agi, tt4=tt4, pb=pb, wb=wb):
                            last = None
                            for k in range(8):
                                last = t.matmul(pb.ap()[:],
                                                wb[:, k * 1536 + (2 * pi + agi) * 128: k * 1536 + (2 * pi + agi) * 128 + 128],
                                                z1[:, k * T + tt4 * 512: k * T + (tt4 + 1) * 512],
                                                start=(k == 0), stop=(k == 7))
                            return last
                        p.emit("tensor", w1mm,
                               waits=[(sv, z1_t[tt4]), (sg1[qi % 2], sg1[qi % 2].n)] +
                                     self.guard(("gep", 2 * (step % 2) + agi)),
                               incs=[(sm, 1)])
                    w1_t = sm.n
                    p.emit("scalar", lambda sc, i_glob=i_glob, pg=pg, gb=ggb[step % 2]: sc.activation(
                        gb, pg.ap()[:], AF.Gelu, bias=b1v[:, 2 * i_glob + 1:2 * i_glob + 2]),
                        waits=[(sm, w1_t)] + self.guard(("ggb", step % 2)), incs=[(sa, 1)])
                    self.set_guard(("gep", 2 * (step % 2) + 1), sa)
                    hb = h4b[step % 2]
                    p.emit("vector", lambda v, hb=hb, i_glob=i_glob, pa=pa, gb=ggb[step % 2]: v.scalar_tensor_tensor(
                        hb, pa.ap()[:],
                        b1v[:, 2 * i_glob:2 * i_glob + 1], gb,
                        ALU.add, ALU.mult),
                        waits=[(sa, None), (sh4st[step % 2], sh4st[step % 2].n)], incs=[(sv, 1)])
                    self.set_guard(("gep", 2 * (step % 2)), sv)
                    self.set_guard(("ggb", step % 2), sv)
                    p.emit("sync", lambda sy, i_glob=i_glob, tt4=tt4, hb=hb: sy.dma_start(
                        out=d["h4_d"][i_glob * 128:(i_glob + 1) * 128, tt4 * 512:(tt4 + 1) * 512],
                        in_=hb), waits=[(sv, None)], incs=[(sh4st[step % 2], 16)])
                    stc[0] += 1

            emit_q_load(0)
            emit_q_load(1)
            emit_ln(0)
            emit_ln(1)
            emit_gemm_tile(0, 0)
            emit_ln(2)
            emit_gemm_tile(0, 1)
            emit_ln(3)
            emit_gemm_tile(0, 2)
            emit_gemm_tile(0, 3)
            q_cons[0] = sm.n
            for qi in range(1, 4):
                if qi >= 2:
                    emit_q_load(qi)
                for tt4 in range(NT4):
                    emit_gemm_tile(qi, tt4)
                q_cons[qi] = sm.n
            h4st_t = [sh4st[0].n, sh4st[1].n]
            if l == 0:
                p.emit("gpsimd", lambda gp: gp.dma_start(
                    out=self.wq_lo.rearrange("p (g n) -> p g n", g=3),
                    in_=d["wqkv"][1].rearrange("(g p) n -> p g n", p=128)[:, 5:8, :]),
                    waits=[(sm, None), (sv, None)], incs=[(sgw, 16)])
                p.emit("gpsimd", lambda gp: gp.dma_start(
                    out=self.wo2.rearrange("p (g n) -> p g n", g=4),
                    in_=d["wo"][1].rearrange("(g p) n -> p g n", p=128)),
                    waits=[], incs=[(sgw, 16)])
            self.ab.reset(m4a)
            w2h = [self.ab.alloc(11 * C) for _ in range(2)]
            h4t = [self.ab.alloc(11 * 512) for _ in range(2)]
            for half in range(2):
                wb2 = w2h[half % 2]
                p.emit("gpsimd", lambda gp, l=l, half=half, wb2=wb2: gp.dma_start(
                    out=wb2.rearrange("p (g n) -> p g n", g=11),
                    in_=d["w2p"][l][half * 1408:(half + 1) * 1408, :].rearrange(
                        "(g p) n -> p g n", p=128)),
                    waits=[(sm, None), (sv, None)],
                    incs=[(sg2[half], 16)])
                for tt4 in range(T // 512):
                    hs = h4t[tt4 % 2]
                    hld = sh4[tt4 % 2]
                    p.emit("sync", lambda sy, half=half, tt4=tt4, hs=hs: sy.dma_start(
                        out=hs.rearrange("p (g n) -> p g n", g=11),
                        in_=d["h4_d"][half * 1408:(half + 1) * 1408, tt4 * 512:(tt4 + 1) * 512].rearrange(
                            "(g p) n -> p g n", p=128)),
                        waits=[(sh4st[0], h4st_t[0]), (sh4st[1], h4st_t[1])] + self.guard(("h4t", tt4 % 2)),
                        incs=[(hld, 16)])
                    for ctO in range(8):
                        pb = self.ps[2 + ctO % 2]
                        def w2mm(t, hs=hs, ctO=ctO, pb=pb, wb2=wb2):
                            last = None
                            for k in range(11):
                                last = t.matmul(pb.ap()[:],
                                                wb2[:, k * C + ctO * 128: k * C + ctO * 128 + 128],
                                                hs[:, k * 512:(k + 1) * 512],
                                                start=(k == 0), stop=(k == 10))
                            return last
                        p.emit("tensor", w2mm, waits=[(hld, hld.n), (sg2[half], sg2[half].n)] +
                               self.guard(("gep", 2 + ctO % 2)), incs=[(sm, 1)])
                        if half == 0:
                            p.emit("vector", lambda v, ctO=ctO, tt4=tt4, pb=pb: v.tensor_tensor(
                                xf[:, ctO * (T + 1) + 1 + tt4 * 512: ctO * (T + 1) + 1 + (tt4 + 1) * 512],
                                xf[:, ctO * (T + 1) + 1 + tt4 * 512: ctO * (T + 1) + 1 + (tt4 + 1) * 512],
                                pb.ap()[:], ALU.add),
                                waits=[(sm, None)], incs=[(sv, 1)])
                        else:
                            p.emit("vector", lambda v, ctO=ctO, tt4=tt4, pb=pb: v.scalar_tensor_tensor(
                                xf[:, ctO * (T + 1) + 1 + tt4 * 512: ctO * (T + 1) + 1 + (tt4 + 1) * 512],
                                pb.ap()[:], b2v[:, ctO:ctO + 1],
                                xf[:, ctO * (T + 1) + 1 + tt4 * 512: ctO * (T + 1) + 1 + (tt4 + 1) * 512],
                                ALU.add, ALU.add),
                                waits=[(sm, None)], incs=[(sv, 1)])
                        self.set_guard(("gep", 2 + ctO % 2), sv)
                    self.set_guard(("h4t", tt4 % 2), sm)
            self.set_guard(("smallvec",), sa, sv)

        for l in range(2):
            attn_stage(l)
            geglu_stage(l)
        self.p4_done = [(sv, sv.n), (sm, sm.n), (sa, sa.n), (sl, sl.n), (sst4, sst4.n)]
        self.af.reset(mf0 + 8 * (T + 1))

    def phase5(self):
        p, T, d = self.p, self.T, self.dram
        sm = self.sem("p5_mm"); sv = self.sem("p5_dv"); sa = self.sem("p5_act")
        s_ld = [self.sem(f"p5_ld{i}") for i in range(2)]
        s_st = [self.sem(f"p5_st{i}") for i in range(2)]
        mb = self.ab.mark()
        gfb = self.ab.alloc(8 * T)
        gtt = self.ab.alloc(C)
        mf = self.af.mark()
        thb = self.af.alloc(C)
        xs_t = [self.af.alloc(C) for _ in range(2)]
        ss_t = [self.af.alloc(C) for _ in range(2)]
        tsc = thb
        thb2 = [thb, ss_t[0]]
        g_half = [0, 0]
        for hf in range(T // C):
            for ct in range(8):
                th = thb2[ct % 2]
                p.emit("scalar", lambda sc, ct=ct, hf=hf, th=th: sc.activation(
                    th, self.xf[:, ct * (T + 1) + 1 + hf * C: ct * (T + 1) + 1 + (hf + 1) * C],
                    AF.Tanh, scale=0.5),
                    waits=(list(self.p4_done) + [(sa, None), (sv, None)]) if (ct == 0 and hf == 0)
                    else ([(sv, sv.n - 1)] if sv.n else []),
                    incs=[(sa, 1)])
                p.emit("vector", lambda v, ct=ct, hf=hf, th=th: v.scalar_tensor_tensor(
                    gfb[:, ct * T + hf * C:(ct) * T + (hf + 1) * C], th, 1.0,
                    self.xf[:, ct * (T + 1) + 1 + hf * C: ct * (T + 1) + 1 + (hf + 1) * C],
                    ALU.add, ALU.mult),
                    waits=[(sa, None)], incs=[(sv, 1)])
            g_half[hf] = sv.n
        g_done = sv.n
        psb = [self.ps[2].ap().bitcast(BF16), self.ps[3].ap().bitcast(BF16)]
        gtt2 = [gtt, self.ab.alloc(C)]
        for tt in range(self.NTT):
            pb = psb[tt % 2]
            def trg(t, tt=tt, pb=pb):
                last = None
                for ct in range(8):
                    last = t.transpose(pb[:, ct * 128:(ct + 1) * 128],
                                       gfb[:, ct * T + tt * 128: ct * T + (tt + 1) * 128],
                                       self.identb)
                return last
            w_ = [(sv, g_half[tt // 8])] + self.guard(("p5ps", tt % 2))
            p.emit("tensor", trg, waits=w_, incs=[(sm, 1)])
            gt = gtt2[tt % 2]
            p.emit("vector", lambda v, pb=pb, gt=gt: v.tensor_copy(gt, pb[:, 0:1024]),
                   waits=[(sm, None)] + self.guard(("p5gt", tt % 2)), incs=[(sv, 1)])
            self.set_guard(("p5ps", tt % 2), sv)
            gt_done = sv.n
            xsl = xs_t[tt % 2]; ssl = ss_t[tt % 2]; osl = xs_t[tt % 2]
            ldw = [(s_st[tt % 2], s_st[tt % 2].n)]
            p.emit("sync", lambda sy, tt=tt, xsl=xsl: sy.dma_start(
                out=xsl, in_=d["x"][tt * 128:(tt + 1) * 128, :]),
                waits=ldw + ([(sv, g_done)] if tt < 2 else []), incs=[(s_ld[tt % 2], 16)])
            p.emit("sync", lambda sy, tt=tt, ssl=ssl: sy.dma_start(
                out=ssl, in_=d["s_d"][tt * 128:(tt + 1) * 128, :]),
                waits=[(sv, g_done)] if tt < 2 else [], incs=[(s_ld[tt % 2], 16)])
            p.emit("vector", lambda v, ssl=ssl, gt=gt: v.tensor_tensor(tsc, ssl, gt, ALU.mult),
                   waits=[(s_ld[tt % 2], s_ld[tt % 2].n), (sv, gt_done), (sv, None)], incs=[(sv, 1)])
            p.emit("vector", lambda v, xsl=xsl, osl=osl: v.scalar_tensor_tensor(
                osl, tsc, 0.5, xsl, ALU.mult, ALU.add),
                waits=[(sv, None)], incs=[(sv, 1)])
            self.set_guard(("p5gt", tt % 2), sv)
            p.emit("sync", lambda sy, tt=tt, osl=osl: sy.dma_start(
                out=d["out"][tt * 128:(tt + 1) * 128, :], in_=osl),
                waits=[(sv, None)], incs=[(s_st[tt % 2], 16)])
        self.af.reset(mf); self.ab.reset(mb)

    def finalize(self):
        nc = self.nc
        p = self.p
        waits = [(s, s.n) for s in self.sems.values() if s.n > 0]
        for i in range(0, len(waits), 8):
            p.emit("sync", lambda sy: None, waits=waits[i:i + 8])
        with nc.Block() as block:
            @block.tensor
            def _(t):
                p.replay("tensor", t)

            @block.vector
            def _(v):
                p.replay("vector", v)

            @block.scalar
            def _(sc):
                p.replay("scalar", sc)

            @block.gpsimd
            def _(gp):
                p.replay("gpsimd", gp)

            @block.sync
            def _(sy):
                p.replay("sync", sy)
        self.es.close()



def host_consts(T):
    out = {}
    out["ident"] = np.eye(128, dtype=np.float32)
    m = (np.arange(128)[:, None] < np.arange(128)[None, :]).astype(np.float32)
    out["maskA"] = np.tile(m, (1, 4))
    r = np.arange(128)
    out["mask_low"] = (r[None, :] <= r[:, None]).astype(np.float32)
    out["mask_up"] = (r[None, :] >= r[:, None]).astype(np.float32)
    d2 = np.arange(0, FDH, 2, dtype=np.float64)
    inv_freq = 1.0 / 10000.0 ** (d2 / FDH)
    t = np.arange(T, dtype=np.float64)
    freqs = t[:, None] * inv_freq[None, :]
    freqs = np.concatenate([freqs, freqs], axis=-1)
    base = (d2 + 0.4 * FDH) / (1.4 * FDH)
    power = (t - T // 2) / (WIN // 2)
    scale = base[None, :] ** power[:, None]
    scale = np.concatenate([scale, scale], axis=-1)
    cos, sin = np.cos(freqs), np.sin(freqs)
    sgn = np.concatenate([-np.ones(FDH // 2), np.ones(FDH // 2)])
    out["ct_q"] = (cos * scale).astype(np.float32)
    out["st_q"] = (sin * scale * sgn[None, :]).astype(np.float32)
    out["ct_k"] = (cos / scale).astype(np.float32)
    out["st_k"] = (sin / scale * sgn[None, :]).astype(np.float32)
    return out


def vec_fm(v):
    return np.ascontiguousarray(v.reshape(-1).reshape(8, 128).T).astype(np.float32)


def prep_core_inputs(inputs, b, T):
    g = lambda k: np.asarray(inputs[k])
    m = {}
    m["x"] = np.ascontiguousarray(g("L0_attn_x")[b, :T, :]).astype(np.float32, copy=False)
    m["tmx"] = vec_fm(g("L0_attn_time_maa_x"))
    m["tmw"] = vec_fm(g("L0_attn_time_maa_w"))
    m["tmk"] = vec_fm(g("L0_attn_time_maa_k"))
    m["tmv"] = vec_fm(g("L0_attn_time_maa_v"))
    m["tmr"] = vec_fm(g("L0_attn_time_maa_r"))
    m["tmg"] = vec_fm(g("L0_attn_time_maa_g"))
    m["tdecay"] = vec_fm(g("L0_attn_time_decay"))
    m["u_"] = vec_fm(g("L0_attn_time_faaaa"))
    m["lnw_"] = vec_fm(g("L0_attn_ln_w"))
    m["lnb_"] = vec_fm(g("L0_attn_ln_b"))
    w1 = g("L0_attn_time_maa_w1")
    m["maa_w1"] = np.ascontiguousarray(w1)
    w2 = g("L0_attn_time_maa_w2")
    m["maa_w2"] = np.ascontiguousarray(w2.transpose(1, 0, 2).reshape(32, 5 * C))
    m["td_w1"] = np.ascontiguousarray(g("L0_attn_time_decay_w1"))
    m["td_w2"] = np.ascontiguousarray(g("L0_attn_time_decay_w2"))
    m["w_r"] = np.ascontiguousarray(g("L0_attn_W_r"))
    m["w_k"] = np.ascontiguousarray(g("L0_attn_W_k"))
    m["w_v"] = np.ascontiguousarray(g("L0_attn_W_v"))
    m["w_g"] = np.ascontiguousarray(g("L0_attn_W_g"))
    m["w_o"] = np.ascontiguousarray(g("L0_attn_W_o"))
    m["lnw_rep"] = np.ascontiguousarray(np.broadcast_to(g("L0_attn_ln_w")[None, :], (128, C))).astype(np.float32)
    m["lnb_rep"] = np.ascontiguousarray(np.broadcast_to(g("L0_attn_ln_b")[None, :], (128, C))).astype(np.float32)
    m["wqkv"] = np.ascontiguousarray(g("L0_ffn_wqkv"))
    m["qscale_rep"] = np.ascontiguousarray(np.broadcast_to(g("L0_ffn_q_scale")[:, None, :], (2, 128, FDH))).astype(np.float32)
    m["kscale_rep"] = np.ascontiguousarray(np.broadcast_to(g("L0_ffn_k_scale")[:, None, :], (2, 128, FDH))).astype(np.float32)
    m["wo"] = np.ascontiguousarray(g("L0_ffn_wo"))
    m["lng"] = np.stack([vec_fm(g("L0_ffn_ln_g")[l]) for l in range(2)])
    m["lnb2"] = np.stack([vec_fm(g("L0_ffn_ln_b")[l]) for l in range(2)])
    m["b2"] = np.stack([vec_fm(g("L0_ffn_b2")[l]) for l in range(2)])
    w1f = g("L0_ffn_w1")
    b1f = g("L0_ffn_b1")
    w1p = np.zeros((2, C, 2 * HIDP), dtype=np.float32)
    b1p = np.zeros((2, 128, 44), dtype=np.float32)
    for l in range(2):
        a = w1f[l][:, :HID]; gg = w1f[l][:, HID:]
        ba = b1f[l][:HID]; bg = b1f[l][HID:]
        for i in range(22):
            lo, hi = i * 128, min((i + 1) * 128, HID)
            n = hi - lo
            if n > 0:
                w1p[l][:, (2 * i) * 128:(2 * i) * 128 + n] = a[:, lo:hi]
                w1p[l][:, (2 * i + 1) * 128:(2 * i + 1) * 128 + n] = gg[:, lo:hi]
                b1p[l][:n, 2 * i] = ba[lo:hi]
                b1p[l][:n, 2 * i + 1] = bg[lo:hi]
    m["w1p"] = w1p
    m["b1p"] = b1p
    w2f = g("L0_ffn_w2")
    w2p = np.zeros((2, HIDP, C), dtype=np.float32)
    w2p[:, :HID, :] = w2f
    m["w2p"] = w2p
    m.update(host_consts(T))
    return m


def build_nc(T=2048, dbg=(), phases=6):
    nc = bass.Bass("TRN2", target_bir_lowering=False, debug=False)
    b = Builder(nc, T, dbg)
    b.setup()
    b.load_consts()
    b.load_rwkv_weights()
    b.phase0()
    if phases > 1:
        b.phase1()
    if phases > 4:
        b.prefetch_ffn0()
    if phases > 2:
        b.phase2(parts=globals().get('_P2_PARTS', 31))
    if phases > 3:
        b.phase3()
    if phases > 4:
        b.phase4()
    if phases > 5:
        b.phase5()
    b.finalize()
    return b, nc


def kernel(**inputs):
    T = 2048
    bld, nc = build_nc(T)
    in_maps = [prep_core_inputs(inputs, b, T) for b in range(8)]
    res = run_bass_kernel_spmd(nc, in_maps, core_ids=list(range(8)))
    out = np.stack([res.results[i]["out"] for i in range(8)], axis=0)
    return out
